# version 20
# baseline (speedup 1.0000x reference)
import numpy as np
from contextlib import ExitStack

import concourse.bass as bass
import concourse.mybir as mybir
from concourse.bass_utils import run_bass_kernel_spmd

F32 = mybir.dt.float32
BF16 = mybir.dt.bfloat16
U8 = mybir.dt.uint8
AF = mybir.ActivationFunctionType
ALU = mybir.AluOpType
AX = mybir.AxisListType

D = 1024
INNER = 2048
NH = 8
NST = 64
PHD = 256
KCV = 4
Q = 128
CD = D // 128
CI = INNER // 128
CB = (NH * NST) // 128
EPS = 1e-5
NEG = -1e30

ENGS = ("sync", "tensor", "vector", "scalar", "gpsimd")
SEMS = ("LD", "LDA", "LDB", "LDC", "LS", "LSA", "SP", "TE", "VE", "SE", "GE", "CC", "TT", "VV", "SS", "GG")
SELF_SEM = {"tensor": "TT", "vector": "VV", "scalar": "SS", "gpsimd": "GG",
            "sync": "VV"}


def _tiles(n, m=512):
    k = -(-n // m)
    base, rem = n // k, n % k
    out, off = [], 0
    for i in range(k):
        w = base + (1 if i < rem else 0)
        out.append((off, w))
        off += w
    return out


class Sched:

    def __init__(self):
        self.items = {e: [] for e in ENGS}
        self.cnt = {s: 0 for s in SEMS}
        self.handles = {}
        self.auto_edge = True

    def op(self, eng, fn, inc=None, amt=1):
        if self.auto_edge and eng in ("vector", "scalar", "gpsimd"):
            m = self.mark(eng)
            if m:
                self.items[eng].append(["wait", m[0], m[1], None])
        if inc is not None:
            self.cnt[inc] += amt
        self.items[eng].append(["op", fn, inc, amt])

    def dma(self, eng, fn, inc):
        ae = self.auto_edge
        self.auto_edge = False
        self.op(eng, fn, inc, 16)
        self.auto_edge = ae

    def wait(self, eng, sem, val=None):
        v = self.cnt[sem] if val is None else val
        if v > 0:
            self.items[eng].append(["wait", sem, v, None])

    def now(self, sem):
        return self.cnt[sem]

    def mark(self, eng):
        for it in reversed(self.items[eng]):
            if it[0] == "op":
                if it[2] is None:
                    s = SELF_SEM[eng]
                    it[2], it[3] = s, 1
                    self.cnt[s] += 1
                return (it[2], self.cnt[it[2]])
        return None

    def edge(self, eng):
        m = self.mark(eng)
        if m:
            self.wait(eng, m[0], m[1])

    def qbar(self, eng, others=("tensor", "vector", "scalar", "gpsimd")):
        for o in others:
            if o == eng:
                continue
            m = self.mark(o)
            if m:
                self.wait(eng, m[0], m[1])

    def barrier(self, engs=ENGS):
        marks = [m for m in (self.mark(e) for e in engs) if m]
        for e in engs:
            for sem, v in marks:
                self.wait(e, sem, v)


class Arena:
    def __init__(self, size):
        self.size = size
        self.left = 0
        self.right = size
        self.lstack = []
        self.rstack = []
        self.peak = 0

    def alloc(self, nbytes, side="left", align=64):
        if side == "left":
            off = -(-self.left // align) * align
            self.left = off + nbytes
        else:
            off = (self.right - nbytes) // align * align
            self.right = off
        assert self.left <= self.right, (
            f"SBUF arena overflow: left={self.left} right={self.right}")
        assert self.left <= self.right, (
            f"SBUF pool overflow: left={self.left} right={self.right}")
        self.peak = max(self.peak, self.left + (self.size - self.right))
        return off

    def mark(self, side="right"):
        st = self.lstack if side == "left" else self.rstack
        st.append(self.left if side == "left" else self.right)

    def release(self, side="right"):
        if side == "left":
            self.left = self.lstack.pop()
        else:
            self.right = self.rstack.pop()


def build_program(T, n_shards, n_batches, part_bytes=200704):
    NCH = T // Q
    TH = T + 3
    CNT_GN = float(n_shards * T * PHD)
    FT = min(256, T)

    nc = bass.Bass("TRN2", target_bir_lowering=False, debug=False)

    dx = nc.dram_tensor("x", [T, D], F32, kind="ExternalInput").ap()
    dxh = nc.dram_tensor("xh3", [3, D], F32, kind="ExternalInput").ap()
    dwx = nc.dram_tensor("wxp", [CI, CD, 128, 128], F32, kind="ExternalInput").ap()
    dwz = nc.dram_tensor("wzp", [CI, CD, 128, 128], F32, kind="ExternalInput").ap()
    dwb = nc.dram_tensor("wbp", [CB, CI, 128, 128], F32, kind="ExternalInput").ap()
    dwc = nc.dram_tensor("wcp", [CB, CI, 128, 128], F32, kind="ExternalInput").ap()
    dwdt = nc.dram_tensor("wdtp", [CI, 128, 8], F32, kind="ExternalInput").ap()
    dwout = nc.dram_tensor("woutr", [CI, 128, D], F32, kind="ExternalInput").ap()
    dwr = nc.dram_tensor("wrr", [CD, 128, D], F32, kind="ExternalInput").ap()
    dwv = nc.dram_tensor("wvr", [CD, 128, D], F32, kind="ExternalInput").ap()
    dwk = nc.dram_tensor("wkp", [CD, CD, 128, 128], F32, kind="ExternalInput").ap()
    dconvw = nc.dram_tensor("convw", [128, CI * KCV], F32, kind="ExternalInput").ap()
    dconvb = nc.dram_tensor("convb", [128, CI], F32, kind="ExternalInput").ap()
    dgnw = nc.dram_tensor("gnw", [128, CI], F32, kind="ExternalInput").ap()
    dgnb = nc.dram_tensor("gnb", [128, CI], F32, kind="ExternalInput").ap()
    ddp = nc.dram_tensor("dprep", [128, CI], F32, kind="ExternalInput").ap()
    dmur = nc.dram_tensor("murc", [128, CD], F32, kind="ExternalInput").ap()
    dmuk = nc.dram_tensor("mukc", [128, CD], F32, kind="ExternalInput").ap()
    dbdt = nc.dram_tensor("bdt", [8, 1], F32, kind="ExternalInput").ap()
    dloga = nc.dram_tensor("loga", [8, 1], F32, kind="ExternalInput").ap()
    didb = nc.dram_tensor("identb", [128, 128], BF16, kind="ExternalInput").ap()
    didf = nc.dram_tensor("identf", [128, 128], F32, kind="ExternalInput").ap()
    dmaskn = nc.dram_tensor("maskneg", [128, 128], F32, kind="ExternalInput").ap()
    dselp = nc.dram_tensor("selp", [8, CB * 128], F32, kind="ExternalInput").ap()
    dselh = nc.dram_tensor("selh", [8, NH * 128], F32, kind="ExternalInput").ap()
    do8 = nc.dram_tensor("ones8x128", [8, 128], F32, kind="ExternalInput").ap()
    do128 = nc.dram_tensor("ones128", [128, 1], F32, kind="ExternalInput").ap()
    dmci = nc.dram_tensor("maskci", [8, CI], F32, kind="ExternalInput").ap()
    dsel = nc.dram_tensor("selshard", [128, 4], F32, kind="ExternalInput").ap()
    dout = nc.dram_tensor("out", [T, D], F32, kind="ExternalOutput").ap()

    sp_xnh = nc.dram_tensor("sp_xnh", [128, CD * TH], BF16).ap()
    sp_xnl = nc.dram_tensor("sp_xnl", [128, CD * TH], BF16).ap()
    sp_ym = nc.dram_tensor("sp_ym", [NCH, 128, CI * Q], F32).ap()
    sp_xph = nc.dram_tensor("sp_xph", [128, CI * T], BF16).ap()
    sp_xpl = nc.dram_tensor("sp_xpl", [128, CI * T], BF16).ap()
    sp_zsh = nc.dram_tensor("sp_zsh", [128, CI * T], BF16).ap()
    sp_zsl = nc.dram_tensor("sp_zsl", [128, CI * T], BF16).ap()
    sp_ffn = nc.dram_tensor("sp_ffn", [T, D], F32).ap()
    cc_in = nc.dram_tensor("cc_in", [CB, 128, 257], F32)
    cc_out = nc.dram_tensor("cc_out", [n_shards, CB, 128, 257], F32)
    dbg_t = nc.dram_tensor("dbg", [16, 128, 512], F32).ap()
    ar_in = nc.dram_tensor("ar_in", [1, 16], F32)
    ar_out = nc.dram_tensor("ar_out", [1, 16], F32)
    groups = [list(range(b * n_shards, (b + 1) * n_shards))
              for b in range(n_batches)]

    S = Sched()
    es = ExitStack()
    pool_h = es.enter_context(nc.sbuf_tensor("pool", [128, part_bytes], U8))
    pool = pool_h[:, :]
    psF = [es.enter_context(nc.psum_tensor(f"ps{i}", [128, 512], F32))
           for i in range(6)]
    psT = es.enter_context(nc.psum_tensor("psT", [128, 1024], BF16))
    psX = es.enter_context(nc.psum_tensor("psX", [128, 512], F32))
    for s in SEMS:
        S.handles[s] = es.enter_context(nc.semaphore(s))

    A = Arena(part_bytes)
    _ldpool = ["LD", "LDA", "LDB", "LDC"]
    _ldi = [0]
    _lspool = ["LS", "LSA"]
    _lsi = [0]

    def newld():
        s = _ldpool[_ldi[0] % 4]
        _ldi[0] += 1
        for q in ("sync", "gpsimd"):
            S.wait(q, s)
        return s

    def newls():
        s = _lspool[_lsi[0] % 2]
        _lsi[0] += 1
        S.wait("scalar", s)
        return s

    def ap(off, n, dt, p0=0, p=128):
        sz = 2 if dt == BF16 else 4
        return pool[p0:p0 + p, off:off + n * sz].bitcast(dt)

    def sb(n):
        return A.alloc(n, "left")
    o_idb = sb(256); o_idf = sb(512); o_mkn = sb(512)
    o_o8 = sb(512); o_o128 = sb(64); o_mci = sb(CI * 4); o_sel = sb(16)
    o_cvw = sb(CI * KCV * 4); o_cvb = sb(CI * 4); o_gnw = sb(CI * 4)
    o_gnb = sb(CI * 4); o_dp = sb(CI * 4)
    o_mur = sb(CD * 4); o_muk = sb(CD * 4); o_mur1 = sb(CD * 4); o_muk1 = sb(CD * 4)
    o_bdt = sb(64); o_loga = sb(64); o_aneg = sb(64)
    o_eppf = sb(max(NCH, 16) * 4)
    o_atn = sb(NCH * 32); o_dtt = sb(NCH * 32); o_dwt = sb(NCH * 32)
    o_dtc = sb(CB * NCH * 4); o_epc = sb(NH * NCH * 4); o_dsh = sb(CB * 4)
    o_stc = sb(2 * CI * 4)
    o_tiny = sb(128 * 4)
    o_one8 = sb(64); o_eps8 = sb(64)

    c_idb = ap(o_idb, 128, BF16); c_idf = ap(o_idf, 128, F32)
    c_mkn = ap(o_mkn, 128, F32)
    c_o8 = ap(o_o8, 128, F32, p=8); c_o128 = ap(o_o128, 1, F32)
    c_mci = ap(o_mci, CI, F32, p=8); c_sel = ap(o_sel, 4, F32)
    c_cvb = ap(o_cvb, CI, F32)
    c_gnw = ap(o_gnw, CI, F32); c_gnb = ap(o_gnb, CI, F32)
    c_mur = ap(o_mur, CD, F32); c_muk = ap(o_muk, CD, F32)
    c_mur1 = ap(o_mur1, CD, F32); c_muk1 = ap(o_muk1, CD, F32)
    c_bdt = ap(o_bdt, 1, F32, p=8); c_loga = ap(o_loga, 1, F32, p=8)
    c_aneg = ap(o_aneg, 1, F32, p=8)
    t_eppf = ap(o_eppf, NCH, F32, p=8)
    t_atn = ap(o_atn, NCH * 8, F32)
    t_dtt = ap(o_dtt, NCH * 8, F32)
    t_dwt = ap(o_dwt, NCH * 8, F32)
    t_dtc = ap(o_dtc, CB * NCH, F32)
    t_epc = ap(o_epc, NH * NCH, F32)
    t_dsh = ap(o_dsh, CB, F32)
    t_stc = ap(o_stc, 2 * CI, F32)
    t_tiny = ap(o_tiny, 128, F32)
    c_one8 = ap(o_one8, 1, F32, p=8); c_eps8 = ap(o_eps8, 1, F32, p=8)

    def mm3(eng_psum, lh, ll, rh, rl, kslices, out_sl, extra_first=False):
        seq = []
        for wop, aop in ((lh, rh), (lh, rl), (ll, rh)):
            for ks in kslices:
                seq.append((wop, aop, ks))
        return seq

    _c0par = newld()
    for dst, src in [
        (c_idb, didb), (c_idf, didf), (c_mkn, dmaskn),
        (c_o8, do8), (c_o128, do128), (c_mci, dmci),
        (c_sel, dsel), (ap(o_cvw, CI * KCV, F32), dconvw), (c_cvb, dconvb),
        (c_gnw, dgnw), (c_gnb, dgnb), (ap(o_dp, CI, F32), ddp),
        (c_mur, dmur), (c_muk, dmuk), (c_bdt, dbdt), (c_loga, dloga),
    ]:
        S.dma("sync", (lambda e, d=dst, s=src: e.dma_start(out=d, in_=s[:, :])), _c0par)
    S.wait("vector", _c0par)
    S.op("vector", lambda e: e.tensor_scalar(c_mur1, c_mur, -1.0, 1.0, ALU.mult, ALU.add))
    S.op("vector", lambda e: e.tensor_scalar(c_muk1, c_muk, -1.0, 1.0, ALU.mult, ALU.add))
    S.op("vector", lambda e: e.memset(c_one8, 1.0))
    S.op("vector", lambda e: e.memset(c_eps8, EPS), inc="VE")

    A.mark("right")
    o_xnh = A.alloc(CD * TH * 2, "right")
    o_xnl = A.alloc(CD * TH * 2, "right")
    t_xnh = ap(o_xnh, CD * TH, BF16)
    t_xnl = ap(o_xnl, CD * TH, BF16)
    A.mark("right")
    o_xt = A.alloc(2 * D * 4, "right")
    o_xsq = A.alloc(D * 4, "right")
    o_lnc = A.alloc(8 * 4, "right")
    o_xnth = A.alloc(D * 2, "right")
    o_xntl = A.alloc(D * 2, "right")

    nrows = TH
    rchunks = [(i * 128, min(128, nrows - i * 128)) for i in range(-(-nrows // 128))]
    ve_rows = []
    for r, (row0, rn) in enumerate(rchunks):
        xt = ap(o_xt + (r % 2) * D * 4, D, F32, p=rn)
        xsq = ap(o_xsq, D, F32, p=rn)
        xnth = ap(o_xnth, D, BF16, p=rn)
        xntl = ap(o_xntl, D, BF16, p=rn)

        def lc(i, p=rn):
            return ap(o_lnc + i * 4, 1, F32, p=p)
        par = newld()
        if r >= 2:
            S.wait("sync", "VE", ve_rows[r - 2])
        if r == 0:
            S.dma("sync", lambda e: e.dma_start(
                out=pool[0:3, o_xt:o_xt + D * 4].bitcast(F32), in_=dxh[:, :]), par)
            S.dma("sync", lambda e, b=min(125, T): e.dma_start(
                out=pool[3:3 + b, o_xt:o_xt + D * 4].bitcast(F32), in_=dx[0:b, :]), par)
        else:
            S.dma("sync", lambda e, d=xt, a=row0 - 3, b=rn: e.dma_start(
                out=d, in_=dx[a:a + b, :]), par)
        S.wait("vector", par)
        S.wait("scalar", par)
        S.op("vector", lambda e, xt=xt, c=lc(0): e.reduce_sum(c, xt, axis=AX.X))
        S.op("vector", lambda e, xt=xt, xs=xsq: e.tensor_mul(xs, xt, xt))
        S.edge("vector")
        S.op("vector", lambda e, xs=xsq, c=lc(1): e.reduce_sum(c, xs, axis=AX.X))
        S.edge("vector")
        S.op("vector", lambda e, c=lc(0), o=lc(2): e.tensor_scalar(o, c, -1.0 / D, None, ALU.mult))
        S.edge("vector")
        S.op("vector", lambda e, c=lc(2), o=lc(3): e.tensor_mul(o, c, c))
        S.edge("vector")
        S.op("vector", lambda e, c=lc(3), o=lc(4): e.tensor_scalar(o, c, -1.0, EPS, ALU.mult, ALU.add),
             inc="VE")
        S.wait("scalar", "VE")
        S.op("scalar", lambda e, c=lc(1), b=lc(4), o=lc(6): e.activation(
            out=o, in_=c, func=AF.Sqrt, bias=b, scale=1.0 / D), inc="SE")
        S.wait("vector", "SE")
        S.op("vector", lambda e, s=lc(6), o=lc(5): e.reciprocal(o, s))
        S.edge("vector")
        S.op("vector", lambda e, xt=xt, nm=lc(2), rs=lc(5): e.tensor_scalar(
            xt, xt, nm, rs, ALU.add, ALU.mult))
        S.edge("vector")
        S.op("vector", lambda e, xt=xt, h=xnth: e.tensor_copy(out=h, in_=xt))
        S.edge("vector")
        S.op("vector", lambda e, xt=xt, h=xnth, l=xntl: e.tensor_sub(l, xt, h), inc="VE")
        ve_rows.append(S.now("VE"))
        S.wait("tensor", "VE")
        for src, dstt in ((xnth, t_xnh), (xntl, t_xnl)):
            for cc in range(CD):
                S.op("tensor", lambda e, s=src, cc=cc, rn=rn: e.transpose(
                    psT[:, cc * 128:cc * 128 + rn], s[:, cc * 128:(cc + 1) * 128],
                    c_idb[0:rn, 0:rn]),
                     inc="TE" if cc == CD - 1 else None)
            S.wait("scalar", "TE")
            for cc in range(CD):
                S.op("scalar", lambda e, cc=cc, d=dstt, row0=row0, rn=rn: e.copy(
                    out=d[:, cc * TH + row0:cc * TH + row0 + rn],
                    in_=psT[:, cc * 128:cc * 128 + rn]),
                     inc="SE" if cc == CD - 1 else None)
            S.wait("tensor", "SE")
    S.wait("gpsimd", "SE")
    S.dma("gpsimd", lambda e: e.dma_start(out=sp_xnh, in_=t_xnh), "SP")
    S.dma("gpsimd", lambda e: e.dma_start(out=sp_xnl, in_=t_xnl), "SP")
    S.barrier()
    A.release("right")

    A.mark("left")
    A.mark("right")
    o_xst = A.alloc(2 * 2 * T * 2, "right")
    o_ws = A.alloc(2 * CD * 512, "right")
    o_wh = A.alloc(2 * CD * 256, "right")
    o_wl = A.alloc(2 * CD * 256, "right")
    o_xpr = A.alloc(2 * TH * 4, "right")
    o_cv = A.alloc(2 * T * 4, "right")
    o_sil = A.alloc(2 * T * 4, "right")
    o_cvt = A.alloc(4 * T * 4, "right")
    wtiles = _tiles(TH, 512)
    te_oc = []
    sp_oc = []
    ge_oc = []
    se_oc = []
    ve_sub = []
    ve_nt = {}
    g1i = [0]
    for oc in range(CI):
        ws = ap(o_ws + (oc % 2) * CD * 512, CD * 128, F32)
        wh = ap(o_wh + (oc % 2) * CD * 256, CD * 128, BF16)
        wl = ap(o_wl + (oc % 2) * CD * 256, CD * 128, BF16)
        xpr = ap(o_xpr + (oc % 2) * TH * 4, TH, F32)
        cv = ap(o_cv + (oc % 2) * T * 4, T, F32)
        sil = ap(o_sil + (oc % 2) * T * 4, T, F32)
        par = newld()
        if oc >= 2:
            S.wait("sync", "TE", te_oc[oc - 2])
        for cc in range(CD):
            S.dma("sync", lambda e, oc=oc, cc=cc, d=ws: e.dma_start(
                out=d[:, cc * 128:(cc + 1) * 128], in_=dwx[oc, cc]), par)
        S.wait("vector", par)
        if oc >= 2:
            S.wait("vector", "TE", te_oc[oc - 2])
        S.op("vector", lambda e, s=ws, h=wh: e.tensor_copy(out=h, in_=s))
        S.edge("vector")
        S.op("vector", lambda e, s=ws, h=wh, l=wl: e.tensor_sub(l, s, h), inc="VE")
        S.wait("tensor", "VE")
        for nt, (off, w) in enumerate(wtiles):
            g1 = g1i[0]; g1i[0] += 1
            pb = psF[g1 % 4]
            if g1 >= 4:
                S.wait("tensor", "VE", ve_nt[g1 - 4])
            seq = [(a, b, cc) for a, b in ((wh, t_xnh), (wh, t_xnl), (wl, t_xnh))
                   for cc in range(CD)]
            for i, (wop, aop, cc) in enumerate(seq):
                S.op("tensor", lambda e, p=pb, wop=wop, aop=aop, cc=cc, off=off, w=w,
                     fs=(i == 0), ls=(i == len(seq) - 1): e.matmul(
                         p[:, 0:w], wop[:, cc * 128:(cc + 1) * 128],
                         aop[:, cc * TH + off:cc * TH + off + w], start=fs, stop=ls),
                     inc="TE" if i == len(seq) - 1 else None)
            S.wait("vector", "TE")
            if nt == 0 and oc >= 2:
                S.wait("vector", "GE", ge_oc[oc - 2])
                S.wait("vector", "SE", se_oc[oc - 2])
            S.op("vector", lambda e, p=pb, off=off, w=w, x=xpr: e.tensor_copy(
                out=x[:, off:off + w], in_=p[:, 0:w]), inc="VE")
            ve_nt[g1] = S.now("VE")
        te_oc.append(S.now("TE"))
        tmp1 = ap(o_cvt + (oc % 2) * 2 * T * 4, T, F32)
        tmp2 = ap(o_cvt + (oc % 2) * 2 * T * 4 + T * 4, T, F32)
        wc_ = [ap(o_cvw + (oc * KCV + k) * 4, 1, F32) for k in range(KCV)]
        S.wait("scalar", "VE", ve_nt[g1i[0] - 1])
        if oc >= 2:
            S.wait("scalar", "GE", ge_oc[oc - 2])
            S.wait("scalar", "VE", ve_sub[oc - 2])
        S.op("scalar", lambda e, x=xpr, o=tmp1, wc=wc_[0]: e.activation(
            out=o, in_=x[:, 0:T], func=AF.Identity, scale=wc))
        S.edge("scalar")
        S.op("scalar", lambda e, x=xpr, o=tmp2, wc=wc_[1]: e.activation(
            out=o, in_=x[:, 1:1 + T], func=AF.Identity, scale=wc), inc="SE")
        S.wait("gpsimd", "SE")
        S.op("gpsimd", lambda e, c=cv, a=tmp1, b=tmp2: e.tensor_add(c, a, b), inc="GE")
        S.wait("scalar", "GE")
        S.op("scalar", lambda e, x=xpr, o=tmp1, wc=wc_[2]: e.activation(
            out=o, in_=x[:, 2:2 + T], func=AF.Identity, scale=wc))
        S.edge("scalar")
        S.op("scalar", lambda e, x=xpr, o=tmp2, wc=wc_[3]: e.activation(
            out=o, in_=x[:, 3:3 + T], func=AF.Identity, scale=wc), inc="SE")
        S.wait("gpsimd", "SE")
        S.op("gpsimd", lambda e, c=cv, a=tmp1: e.tensor_add(c, c, a))
        S.edge("gpsimd")
        S.op("gpsimd", lambda e, c=cv, b=tmp2: e.tensor_add(c, c, b), inc="GE")
        S.wait("scalar", "GE")
        ge_oc.append(S.now("GE"))
        S.op("scalar", lambda e, c=cv, s=sil, b=ap(o_cvb + oc * 4, 1, F32): e.activation(
            out=s, in_=c, func=AF.Silu, bias=b, scale=1.0))
        S.edge("scalar")
        xsh = ap(o_xst + (oc % 2) * 2 * T * 2, T, BF16)
        xsl = ap(o_xst + (oc % 2) * 2 * T * 2 + T * 2, T, BF16)
        if oc >= 2:
            S.wait("scalar", "SP", sp_oc[oc - 2])
        S.op("scalar", lambda e, s=sil, d=xsh: e.copy(out=d, in_=s), inc="SE")
        S.wait("vector", "SE")
        S.op("vector", lambda e, s=sil, h=xsh, l=xsl: e.tensor_sub(l, s, h), inc="VE")
        ve_sub.append(S.now("VE"))
        se_oc.append(S.now("SE"))
        S.wait("gpsimd", "VE")
        S.dma("gpsimd", lambda e, oc=oc, h=xsh: e.dma_start(
            out=sp_xph[:, oc * T:(oc + 1) * T], in_=h), "SP")
        S.dma("gpsimd", lambda e, oc=oc, l=xsl: e.dma_start(
            out=sp_xpl[:, oc * T:(oc + 1) * T], in_=l), "SP")
        sp_oc.append(S.now("SP"))
    S.barrier()
    A.release("right")
    S.barrier()
    A.release("right")

    o_selp = sb(CB * 512); o_selh = sb(NH * 512)
    o_st = sb(CB * 1024)
    o_sth = sb(CB * 512); o_stl = sb(CB * 512)
    o_parts = sb(2 * NCH * CI * 4)
    o_expdA = sb(T * 4)
    o_A = sb((T + 4) * 4)
    c_selp = ap(o_selp, CB * 128, F32, p=8)
    c_selh = ap(o_selh, NH * 128, F32, p=8)
    t_st = ap(o_st, CB * 256, F32)
    t_sth = ap(o_sth, CB * 256, BF16)
    t_stl = ap(o_stl, CB * 256, BF16)
    t_parts = ap(o_parts, 2 * NCH * CI, F32)
    t_expdA = ap(o_expdA, T, F32, p=8)
    _selpar = newld()
    S.dma("sync", lambda e: e.dma_start(out=c_selp, in_=dselp[:, :]), _selpar)
    S.dma("sync", lambda e: e.dma_start(out=c_selh, in_=dselh[:, :]), _selpar)
    S.wait("vector", _selpar)
    S.op("vector", lambda e: e.memset(t_st, 0.0))
    S.op("vector", lambda e: e.memset(t_parts, 0.0), inc="VE")
    S.wait("tensor", _selpar)
    o_bh = sb(CB * T * 2); o_bl = sb(CB * T * 2)
    o_ch = sb(CB * T * 2); o_cl = sb(CB * T * 2)
    t_bh = ap(o_bh, CB * T, BF16); t_bl = ap(o_bl, CB * T, BF16)
    t_ch = ap(o_ch, CB * T, BF16); t_cl = ap(o_cl, CB * T, BF16)
    t_A = ap(o_A, T + 1, F32, p=8)
    A.mark("right")
    o_wdth = A.alloc(CI * 16, "right"); o_wdtl = A.alloc(CI * 16, "right")
    o_wbh = A.alloc(CB * CI * 256, "right"); o_wbl = A.alloc(CB * CI * 256, "right")
    o_wch = A.alloc(CB * CI * 256, "right"); o_wcl = A.alloc(CB * CI * 256, "right")
    o_ws2 = A.alloc(CI * 512, "right")
    XT2 = min(256, T)
    o_xpt = A.alloc(2 * CI * XT2 * 2, "right")
    o_dtraw = o_expdA
    wdth = ap(o_wdth, CI * 8, BF16); wdtl = ap(o_wdtl, CI * 8, BF16)
    ws2 = ap(o_ws2, CI * 128, F32)
    t_dtraw = ap(o_dtraw, T, F32, p=8)

    par = newld()
    for cc in range(CI):
        S.dma("sync", lambda e, cc=cc: e.dma_start(
            out=pool[:, o_ws2 + cc * 32:o_ws2 + cc * 32 + 32].bitcast(F32),
            in_=dwdt[cc]), par)
    S.wait("vector", par)
    wdt_s = ap(o_ws2, CI * 8, F32)
    S.op("vector", lambda e: e.tensor_copy(out=wdth, in_=wdt_s))
    S.op("vector", lambda e: e.tensor_sub(wdtl, wdt_s, wdth), inc="VE")
    for dwt_, oh_, ol_ in ((dwb, o_wbh, o_wbl), (dwc, o_wch, o_wcl)):
        for ob in range(CB):
            S.qbar("sync", ("vector",))
            par = newld()
            for cc in range(CI):
                S.dma("sync", lambda e, dw=dwt_, ob=ob, cc=cc: e.dma_start(
                    out=ws2[:, cc * 128:(cc + 1) * 128], in_=dw[ob, cc]), par)
            S.wait("vector", par)
            wh_ = pool[:, oh_ + ob * CI * 256:oh_ + (ob + 1) * CI * 256].bitcast(BF16)
            wl_ = pool[:, ol_ + ob * CI * 256:ol_ + (ob + 1) * CI * 256].bitcast(BF16)
            S.op("vector", lambda e, h=wh_: e.tensor_copy(out=h, in_=ws2))
            S.op("vector", lambda e, h=wh_, l=wl_: e.tensor_sub(l, ws2, h), inc="VE")
    t_wbh = ap(o_wbh, CB * CI * 128, BF16); t_wbl = ap(o_wbl, CB * CI * 128, BF16)
    t_wch = ap(o_wch, CB * CI * 128, BF16); t_wcl = ap(o_wcl, CB * CI * 128, BF16)
    xpth = ap(o_xpt, CI * XT2, BF16)
    xptl = ap(o_xpt + CI * XT2 * 2, CI * XT2, BF16)
    ztiles = _tiles(T, XT2)
    ve_pnt = []
    se_dt = []
    for nt, (off, w) in enumerate(ztiles):
        S.qbar("sync", ("tensor",))
        par = newld()
        for cc in range(CI):
            S.dma("sync", lambda e, cc=cc, off=off, w=w: e.dma_start(
                out=xpth[:, cc * XT2:cc * XT2 + w],
                in_=sp_xph[:, cc * T + off:cc * T + off + w]), par)
            S.dma("sync", lambda e, cc=cc, off=off, w=w: e.dma_start(
                out=xptl[:, cc * XT2:cc * XT2 + w],
                in_=sp_xpl[:, cc * T + off:cc * T + off + w]), par)
        S.wait("tensor", par)
        if nt >= 1:
            S.wait("tensor", "SE", se_dt[-1])
        pb = psF[2]
        seq = [(a, b, cc) for a, b in ((wdth, xpth), (wdth, xptl), (wdtl, xpth))
               for cc in range(CI)]
        for i, (wop, aop, cc) in enumerate(seq):
            S.op("tensor", lambda e, p=pb, wop=wop, aop=aop, cc=cc, w=w,
                 fs=(i == 0), ls=(i == len(seq) - 1): e.matmul(
                     p[0:8, 0:w], wop[:, cc * 8:(cc + 1) * 8],
                     aop[:, cc * XT2:cc * XT2 + w], start=fs, stop=ls),
                 inc="TE" if i == len(seq) - 1 else None)
        S.wait("scalar", "TE")
        S.op("scalar", lambda e, p=pb, off=off, w=w: e.activation(
            out=t_dtraw[:, off:off + w], in_=p[0:8, 0:w], func=AF.Identity,
            bias=c_bdt, scale=1.0), inc="SE")
        se_dt.append(S.now("SE"))
        for wsel, (whh, wll, th_, tl_) in enumerate(
                ((t_wbh, t_wbl, t_bh, t_bl), (t_wch, t_wcl, t_ch, t_cl))):
            for ob in range(CB):
                gi = nt * 2 * CB + wsel * CB + ob
                pb = (psF[0], psF[1], psF[3], psF[4])[gi % 4]
                if gi >= 4:
                    S.wait("tensor", "VE", ve_pnt[gi - 4])
                seq = [(a, b, cc) for a, b in ((whh, xpth), (whh, xptl), (wll, xpth))
                       for cc in range(CI)]
                for i, (wop, aop, cc) in enumerate(seq):
                    S.op("tensor", lambda e, p=pb, wop=wop, aop=aop, ob=ob, cc=cc, w=w,
                         fs=(i == 0), ls=(i == len(seq) - 1): e.matmul(
                             p[:, 0:w], wop[:, (ob * CI + cc) * 128:(ob * CI + cc) * 128 + 128],
                             aop[:, cc * XT2:cc * XT2 + w], start=fs, stop=ls),
                         inc="TE" if i == len(seq) - 1 else None)
                S.wait("scalar", "TE")
                S.op("scalar", lambda e, p=pb, d=th_, ob=ob, off=off, w=w: e.copy(
                    out=d[:, ob * T + off:ob * T + off + w], in_=p[:, 0:w]), inc="SE")
                S.wait("vector", "SE")
                S.op("vector", lambda e, p=pb, dh=th_, dl=tl_, ob=ob, off=off, w=w:
                     e.tensor_sub(dl[:, ob * T + off:ob * T + off + w], p[:, 0:w],
                                  dh[:, ob * T + off:ob * T + off + w]), inc="VE")
                ve_pnt.append(S.now("VE"))
    S.barrier()
    A.release("right")

    A.mark("right")
    o_dt = A.alloc(T * 4, "right")
    o_8a = A.alloc(T * 4, "right")
    o_8b = A.alloc(T * 4, "right")
    t_dt = ap(o_dt, T, F32, p=8)
    t_8a = ap(o_8a, T, F32, p=8)
    t_8b = ap(o_8b, T, F32, p=8)
    S.op("scalar", lambda e: e.activation(out=t_8b, in_=t_dtraw, func=AF.Abs))
    S.edge("scalar")
    S.op("scalar", lambda e: e.activation(out=t_8b, in_=t_8b, func=AF.Exp, scale=-1.0))
    S.edge("scalar")
    S.op("scalar", lambda e: e.activation(out=t_8b, in_=t_8b, func=AF.Ln, bias=c_one8), inc="SE")
    S.wait("vector", "SE")
    S.op("vector", lambda e: e.tensor_relu(t_dt, t_dtraw))
    S.edge("vector")
    S.op("vector", lambda e: e.tensor_add(t_dt, t_dt, t_8b), inc="VE")
    S.wait("scalar", "VE")
    S.op("scalar", lambda e: e.activation(out=t_tiny[0:8, 8:9], in_=c_loga, func=AF.Exp),
         inc="SE")
    S.wait("vector", "SE")
    S.op("vector", lambda e: e.tensor_scalar(c_aneg, t_tiny[0:8, 8:9], -1.0, None, ALU.mult))
    S.op("vector", lambda e: e.memset(t_8b, 1.0))
    S.op("vector", lambda e: e.memset(t_A[:, 0:1], 0.0))
    S.edge("vector")
    S.op("vector", lambda e: e.tensor_scalar(t_8a, t_dt, c_aneg, None, ALU.mult))
    S.edge("vector")
    S.op("vector", lambda e: e.tensor_tensor_scan(t_A[:, 1:T + 1], t_8b, t_8a, 0.0,
                                                  ALU.mult, ALU.add))
    S.edge("vector")
    aprev = t_A[:, 0:(NCH - 1) * Q + 1:Q].unsqueeze(2).broadcast_to([8, NCH, Q])
    acur3 = t_A[:, 1:T + 1].rearrange("p (c q) -> p c q", q=Q)
    out3a = t_8a.rearrange("p (c q) -> p c q", q=Q)
    S.op("vector", lambda e: e.tensor_tensor(out3a, acur3, aprev, ALU.subtract), inc="VE")
    S.wait("scalar", "VE")
    S.op("scalar", lambda e: e.activation(out=t_expdA, in_=t_8a, func=AF.Exp), inc="SE")
    aend = t_A[:, Q:T + 1:Q].unsqueeze(2).broadcast_to([8, NCH, Q])
    S.wait("vector", "SE")
    S.op("vector", lambda e: e.tensor_tensor(out3a, aend, acur3, ALU.subtract), inc="VE")
    S.wait("scalar", "VE")
    S.op("scalar", lambda e: e.activation(out=t_8a, in_=t_8a, func=AF.Exp), inc="SE")
    S.wait("vector", "SE")
    S.op("vector", lambda e: e.tensor_mul(t_8a, t_8a, t_dt), inc="VE")
    S.wait("scalar", "VE")
    S.op("scalar", lambda e: e.activation(out=t_eppf, in_=t_A[:, 0:(NCH - 1) * Q + 1:Q],
                                          func=AF.Exp), inc="SE")
    S.wait("vector", "SE")
    S.op("vector", lambda e: e.tensor_sub(t_tiny[0:8, 16:16 + NCH], t_A[:, Q:T + 1:Q],
                                          t_A[:, 0:(NCH - 1) * Q + 1:Q]), inc="VE")
    S.wait("scalar", "VE")
    S.op("scalar", lambda e: e.activation(out=t_tiny[0:8, 48:48 + NCH],
                                          in_=t_tiny[0:8, 16:16 + NCH], func=AF.Exp))
    S.edge("scalar")
    S.op("scalar", lambda e: e.activation(out=t_tiny[0:8, 78:79], in_=t_A[:, T:T + 1],
                                          func=AF.Exp), inc="SE")
    S.wait("tensor", "SE")
    for k in range(NCH):
        for srcf, dsto, negf in ((t_A[:, 1 + k * Q:1 + (k + 1) * Q], t_atn, -1.0),
                                 (t_dt[:, k * Q:(k + 1) * Q], t_dtt, 1.0),
                                 (t_8a[:, k * Q:(k + 1) * Q], t_dwt, 1.0)):
            S.op("tensor", lambda e, s=srcf: e.transpose(psX[:, 0:8], s, c_idf[0:8, 0:8]), inc="TE")
            S.wait("vector", "TE")
            S.op("vector", lambda e, d=dsto, k=k, ng=negf: e.tensor_scalar(
                d[:, k * 8:(k + 1) * 8], psX[:, 0:8], ng, None, ALU.mult), inc="VE")
            S.wait("tensor", "VE")
    for cbi in range(CB):
        S.op("tensor", lambda e, cbi=cbi: e.matmul(
            psX[:, 0:NCH], c_selp[:, cbi * 128:(cbi + 1) * 128],
            t_tiny[0:8, 48:48 + NCH], start=True, stop=True), inc="TE")
        S.wait("vector", "TE")
        S.op("vector", lambda e, cbi=cbi: e.tensor_copy(
            out=t_dtc[:, cbi * NCH:(cbi + 1) * NCH], in_=psX[:, 0:NCH]), inc="VE")
        S.wait("tensor", "VE")
        S.op("tensor", lambda e, cbi=cbi: e.matmul(
            psX[:, 0:1], c_selp[:, cbi * 128:(cbi + 1) * 128],
            t_tiny[0:8, 78:79], start=True, stop=True), inc="TE")
        S.wait("vector", "TE")
        S.op("vector", lambda e, cbi=cbi: e.tensor_copy(
            out=t_dsh[:, cbi:cbi + 1], in_=psX[:, 0:1]), inc="VE")
        S.wait("tensor", "VE")
    for h in range(NH):
        S.op("tensor", lambda e, h=h: e.matmul(
            psX[:, 0:NCH], c_selh[:, h * 128:(h + 1) * 128], t_eppf,
            start=True, stop=True), inc="TE")
        S.wait("vector", "TE")
        S.op("vector", lambda e, h=h: e.tensor_copy(
            out=t_epc[:, h * NCH:(h + 1) * NCH], in_=psX[:, 0:NCH]), inc="VE")
        S.wait("tensor", "VE")
    S.barrier()
    A.release("right")

    A.mark("right")
    o_xpc = A.alloc(2 * 2 * CI * Q * 2, "right")
    o_xtk = A.alloc(4 * 1024, "right")
    o_bwt = A.alloc(2 * 2 * 256, "right")
    o_ctk = A.alloc(2 * 2 * 256, "right")
    o_sp = A.alloc(4 * 512, "right")
    o_dp2 = A.alloc(4 * 512, "right")
    o_u2 = A.alloc(4 * 512, "right")
    o_scr = A.alloc(8 * 512, "right")
    o_xrc = A.alloc(CI * Q * 4, "right")
    o_crc = A.alloc(CB * Q * 4, "right")
    o_ymst = A.alloc(2 * CI * Q * 4, "right")

    def xpcb(par, half):
        return (ap(o_xpc + par * 2 * CI * Q * 2, CI * Q, BF16),
                ap(o_xpc + par * 2 * CI * Q * 2 + CI * Q * 2, CI * Q, BF16))

    def ymstb(par):
        return ap(o_ymst + par * CI * Q * 4, CI * Q, F32)

    def xtk_h(j):
        return ap(o_xtk + j * 1024, 256, BF16)

    def xtk_l(j):
        return ap(o_xtk + j * 1024 + 512, 256, BF16)

    def bwt(hl, i):
        return ap(o_bwt + hl * 512 + i * 256, 128, BF16)

    def bwt_all(hl):
        return ap(o_bwt + hl * 512, 256, BF16)

    def ctk(hl, i):
        return ap(o_ctk + hl * 512 + i * 256, 128, BF16)

    def ctk_all(hl):
        return ap(o_ctk + hl * 512, 256, BF16)

    def sp_t(hl, j):
        return ap(o_sp + hl * 1024 + j * 256, 128, BF16)

    def sp_all(hl):
        return ap(o_sp + hl * 1024, 512, BF16)

    def u_t(j):
        return ap(o_u2 + j * 512, 128, F32)

    def dp_t(j):
        return ap(o_dp2 + j * 512, 128, F32)

    scan_marks = {}
    gp_gate = [None]

    import os as _osd
    _HVY = _osd.environ.get("KDBG_SCANBAR")
    _GSP = _osd.environ.get("KDBG_GRPSPLIT")
    _SLVL = int(_osd.environ.get("KDBG_SLVL", 9))
    _S1M = int(_osd.environ.get("KDBG_S1MASK", 15))

    def scan_chunk(k, corr):
        if _HVY:
            S.barrier()
        par = k % 2
        ymst = ymstb(par)
        if corr:
            if k >= 2:
                S.wait("sync", "SP", scan_marks[("wb", k - 2)])
                S.wait("sync", "SE", scan_marks[("sq", k - 2)])
            ld = newld()
            S.dma("sync", lambda e, k=k, d=ymst: e.dma_start(out=d, in_=sp_ym[k]), ld)
            S.wait("vector", ld)
            if k >= 1:
                S.wait("gpsimd", "VE", gp_gate[0][1])
            for hp in range(CB):
                S.op("gpsimd", lambda e, hp=hp, k=k: e.tensor_add(
                    ap(o_crc + hp * Q * 4, Q, F32),
                    t_ch[:, hp * T + k * Q:hp * T + (k + 1) * Q],
                    t_cl[:, hp * T + k * Q:hp * T + (k + 1) * Q]), inc="GE")
        else:
            xph_, xpl_ = xpcb(par, 0)
            if k >= 2:
                m = scan_marks[("xte", k - 2)]
                S.wait("sync", m[0], m[1])
                m = scan_marks[("xve", k - 2)]
                S.wait("sync", m[0], m[1])
                S.wait("vector", "SP", scan_marks[("wb", k - 2)])
            ld = newld()
            for ci in range(CI):
                S.dma("sync", lambda e, ci=ci, k=k, d=xph_: e.dma_start(
                    out=d[:, ci * Q:(ci + 1) * Q],
                    in_=sp_xph[:, ci * T + k * Q:ci * T + (k + 1) * Q]), ld)
                S.dma("sync", lambda e, ci=ci, k=k, d=xpl_: e.dma_start(
                    out=d[:, ci * Q:(ci + 1) * Q],
                    in_=sp_xpl[:, ci * T + k * Q:ci * T + (k + 1) * Q]), ld)
            S.op("vector", lambda e: e.tensor_copy(out=t_sth, in_=t_st))
            S.op("vector", lambda e: e.tensor_sub(t_stl, t_st, t_sth), inc="VE")
            S.wait("gpsimd", ld)
            if k >= 1:
                S.wait("gpsimd", "VE", gp_gate[0][1])
            for ci in range(CI):
                S.op("gpsimd", lambda e, ci=ci: e.tensor_add(
                    ap(o_xrc + ci * Q * 4, Q, F32),
                    xph_[:, ci * Q:(ci + 1) * Q], xpl_[:, ci * Q:(ci + 1) * Q]))
            for hp in range(CB):
                S.op("gpsimd", lambda e, hp=hp, k=k: e.tensor_add(
                    ap(o_crc + hp * Q * 4, Q, F32),
                    t_ch[:, hp * T + k * Q:hp * T + (k + 1) * Q],
                    t_cl[:, hp * T + k * Q:hp * T + (k + 1) * Q]), inc="GE")
        for half in range(2):
            if _HVY:
                S.barrier()
            hps = (2 * half, 2 * half + 1)
            heads = [4 * half + j for j in range(4)]
            if not corr:
                S.wait("tensor", ld)
            S.wait("tensor", "VE")
            S.wait("tensor", "SE")
            for i, hp in (enumerate(hps) if (_S1M & 1) else []):
                S.op("tensor", lambda e, hp=hp, k=k, i=i: e.matmul(
                    psF[0][:, i * 128:(i + 1) * 128],
                    c_selp[:, hp * 128:(hp + 1) * 128],
                    t_expdA[:, k * Q:(k + 1) * Q],
                    start=True if _GSP else (i == 0),
                    stop=True if _GSP else (i == 1)))
            if not corr:
                for i, hp in enumerate(hps):
                    S.op("tensor", lambda e, hp=hp, k=k, i=i: e.transpose(
                        psT[:, i * 256:i * 256 + 128],
                        t_bh[:, hp * T + k * Q:hp * T + (k + 1) * Q], c_idb))
                    S.op("tensor", lambda e, hp=hp, k=k, i=i: e.transpose(
                        psT[:, i * 256 + 128:i * 256 + 256],
                        t_bl[:, hp * T + k * Q:hp * T + (k + 1) * Q], c_idb))
                g_seq = [(t_bh, t_ch), (t_bh, t_cl), (t_bl, t_ch)]
                for half_j, jset in ((0, (0, 2)), (1, (1, 3))):
                    nmm = 0
                    for j in jset:
                        h = heads[j]
                        hp, r0 = h // 2, (h % 2) * 64
                        for gg, (ph, pr) in enumerate(g_seq):
                            S.op("tensor", lambda e, ph=ph, pr=pr, hp=hp, r0=r0, k=k, j=j,
                                 fs=(nmm == 0), ls=(nmm == 5): e.matmul(
                                     psF[2][:, j * 128:(j + 1) * 128],
                                     ph[r0:r0 + 64, hp * T + k * Q:hp * T + (k + 1) * Q],
                                     pr[r0:r0 + 64, hp * T + k * Q:hp * T + (k + 1) * Q],
                                     start=fs, stop=ls))
                            nmm += 1
                    if half_j == 0:
                        for j, h in enumerate(heads):
                            S.op("tensor", lambda e, h=h, k=k, j=j: e.matmul(
                                psF[5][:, j * 128:(j + 1) * 128],
                                c_selh[:, h * 128:(h + 1) * 128],
                                t_A[0:8, 1 + k * Q:1 + (k + 1) * Q],
                                start=(j == 0), stop=(j == 3)))
                S.op("tensor", lambda e: e.matmul(
                    psX[:, 0:NCH], c_selp[:, 0:128], t_eppf, start=True, stop=True),
                     inc="TE")
                m_s1 = S.now("TE")
                if _SLVL < 2:
                    continue
                S.wait("vector", "TE", m_s1)
                bwf = ap(o_scr + 1024, 256, F32)
                bwtmp = ap(o_scr + 2048, 256, F32)
                psT3h = psT[:, 0:512].rearrange("p (i t q) -> p i t q", i=2, t=2)
                S.op("vector", lambda e: e.tensor_copy(
                    out=bwtmp, in_=psT3h[:, :, 0, :]))
                S.op("vector", lambda e: e.tensor_add(
                    bwf, bwtmp, psT3h[:, :, 1, :]))
                dwt_bc = t_dwt[:, k * 8 + 4 * half:k * 8 + 4 * half + 4].unsqueeze(2).broadcast_to([128, 4, 64])
                S.op("vector", lambda e: e.tensor_tensor(
                    bwf.rearrange("p (c u) -> p c u", u=64), bwf.rearrange(
                        "p (c u) -> p c u", u=64), dwt_bc, ALU.mult))
                S.op("vector", lambda e: e.tensor_copy(out=bwt_all(0), in_=bwf))
                S.op("vector", lambda e: e.tensor_sub(bwt_all(1), bwf, bwt_all(0)),
                     inc="VE")
                m_bw = S.now("VE")
                S.wait("tensor", "VE", m_bw)
                for j in (0, 1):
                    h = heads[j]
                    for hl, src in ((0, xph_), (1, xpl_)):
                        for pc in range(2):
                            ci = h * 2 + pc
                            S.op("tensor", lambda e, s=src, ci=ci, j=j, hl=hl, pc=pc:
                                 e.transpose(psT[:, j * 512 + hl * 256 + pc * 128:
                                                 j * 512 + hl * 256 + pc * 128 + 128],
                                             s[:, ci * Q:(ci + 1) * Q], c_idb),
                                 inc="TE" if (j == 1 and hl == 1 and pc == 1) else None)
                m_w1 = S.now("TE")
                if _SLVL < 3:
                    continue
                S.wait("vector", "GE")
                crec = ap(o_scr, 256, F32)
                S.op("vector", lambda e, hp0=hps[0]: e.tensor_mul(
                    crec, ap(o_crc + hp0 * Q * 4, 2 * Q, F32), psF[0][:, 0:256]))
                S.op("vector", lambda e: e.tensor_copy(out=ctk_all(0), in_=crec))
                S.op("vector", lambda e: e.tensor_sub(ctk_all(1), crec, ctk_all(0)),
                     inc="VE")
                for j, h in enumerate(heads):
                    S.op("vector", lambda e, j=j, k=k, h=h: e.tensor_scalar(
                        u_t(j), psF[5][:, j * 128:(j + 1) * 128],
                        t_atn[:, k * 8 + h:k * 8 + h + 1], 0.0, ALU.add, ALU.min))
                    S.op("vector", lambda e, j=j: e.tensor_add(u_t(j), u_t(j), c_mkn),
                         inc="VE" if j == 3 else None)
                m_u = S.now("VE")
                S.wait("scalar", "VE", m_u)
                for j in range(4):
                    S.op("scalar", lambda e, j=j: e.activation(
                        out=dp_t(j), in_=u_t(j), func=AF.Exp), inc="SE" if j == 3 else None)
                m_dp = S.now("SE")
                S.wait("vector", "SE", m_dp)
                for j, h in enumerate(heads):
                    P1j = u_t(j)
                    S.op("vector", lambda e, P1=P1j, j=j: e.tensor_mul(
                        P1, psF[2][:, j * 128:(j + 1) * 128], dp_t(j)))
                    S.op("vector", lambda e, P1=P1j, j=j, k=k, h=h: e.tensor_scalar(
                        sp_t(0, j), P1, t_dtt[:, k * 8 + h:k * 8 + h + 1], None, ALU.mult))
                    S.op("vector", lambda e, P1=P1j, j=j, k=k, h=h: e.scalar_tensor_tensor(
                        sp_t(1, j), P1, t_dtt[:, k * 8 + h:k * 8 + h + 1], sp_t(0, j),
                        ALU.mult, ALU.subtract), inc="VE" if j == 3 else None)
                m_sp = S.now("VE")
                if _SLVL < 4:
                    continue
                S.wait("scalar", "TE", m_w1)
                for j in (0, 1):
                    S.op("scalar", lambda e, j=j: e.copy(
                        out=pool[:, o_xtk + j * 1024:o_xtk + j * 1024 + 1024].bitcast(BF16),
                        in_=psT[:, j * 512:j * 512 + 512]), inc="SE" if j == 1 else None)
                m_xe1 = S.now("SE")
                S.wait("tensor", "SE", m_xe1)
                for j in (2, 3):
                    h = heads[j]
                    for hl, src in ((0, xph_), (1, xpl_)):
                        for pc in range(2):
                            ci = h * 2 + pc
                            S.op("tensor", lambda e, s=src, ci=ci, j=j, hl=hl, pc=pc:
                                 e.transpose(psT[:, (j - 2) * 512 + hl * 256 + pc * 128:
                                                 (j - 2) * 512 + hl * 256 + pc * 128 + 128],
                                             s[:, ci * Q:(ci + 1) * Q], c_idb),
                                 inc="TE" if (j == 3 and hl == 1 and pc == 1) else None)
                m_w2 = S.now("TE")
                S.wait("scalar", "TE", m_w2)
                for j in (2, 3):
                    S.op("scalar", lambda e, j=j: e.copy(
                        out=pool[:, o_xtk + j * 1024:o_xtk + j * 1024 + 1024].bitcast(BF16),
                        in_=psT[:, (j - 2) * 512:(j - 2) * 512 + 512]),
                         inc="SE" if j == 3 else None)
                m_xe2 = S.now("SE")
                if _SLVL < 5:
                    continue
                S.wait("tensor", "VE", m_sp)
                S.wait("tensor", "SE", m_xe2)
                if k > 0 or half > 0 or True:
                    S.wait("tensor", "VE")
                for j, h in enumerate(heads):
                    hp, r0 = h // 2, (h % 2) * 64
                    bank = psF[3] if j < 2 else psF[1]
                    c0 = (j % 2) * 256
                    nmm = 0
                    for pc in range(2):
                        for lh, rh in ((xtk_h(j), sp_t(0, j)), (xtk_h(j), sp_t(1, j)),
                                       (xtk_l(j), sp_t(0, j))):
                            S.op("tensor", lambda e, lh=lh, rh=rh, pc=pc, c0=c0, bank=bank,
                                 fs=(nmm == 0), ls=False: e.matmul(
                                     bank[:, c0 + pc * 128:c0 + pc * 128 + 128],
                                     lh[:, pc * 128:pc * 128 + 128], rh,
                                     start=fs, stop=False))
                            nmm += 1
                    i = hp - 2 * half
                    for pc in range(2):
                        for lh, rh in ((t_sth, ctk(0, i)), (t_sth, ctk(1, i)),
                                       (t_stl, ctk(0, i))):
                            nmm += 1
                            S.op("tensor", lambda e, lh=lh, rh=rh, pc=pc, hp=hp, r0=r0,
                                 c0=c0, bank=bank, i=i, ls=(nmm == 12): e.matmul(
                                     bank[:, c0 + pc * 128:c0 + pc * 128 + 128],
                                     lh[r0:r0 + 64, hp * 256 + pc * 128:hp * 256 + pc * 128 + 128],
                                     rh[r0:r0 + 64, :], start=False, stop=ls),
                                 inc="TE" if (j == 3 and nmm == 12) else None)
                for i, hp in enumerate(hps):
                    for hh in range(2):
                        j = 2 * i + hh
                        r0 = hh * 64
                        cs_seq = [(bwt(0, i), xtk_h(j)), (bwt(0, i), xtk_l(j)),
                                  (bwt(1, i), xtk_h(j))]
                        for jj, (lh, rh) in enumerate(cs_seq):
                            S.op("tensor", lambda e, lh=lh, rh=rh, i=i, r0=r0, hh=hh,
                                 fs=(jj == 0), ls=(jj == 2): e.matmul(
                                     psF[4][r0:r0 + 64, i * 256:i * 256 + 256],
                                     lh[:, hh * 64:hh * 64 + 64], rh,
                                     start=fs, stop=ls),
                                 inc="TE" if (i == 1 and hh == 1 and jj == 2) else None)
                m_s3 = S.now("TE")
                if _SLVL < 6:
                    continue
                S.wait("vector", "TE", m_s3)
                S.wait("vector", "GE")
                for j, h in enumerate(heads):
                    bank = psF[3] if j < 2 else psF[1]
                    c0 = (j % 2) * 256
                    S.op("vector", lambda e, h=h, c0=c0, bank=bank, ymst=ymst:
                         e.scalar_tensor_tensor(
                             ymst[:, h * 256:(h + 1) * 256],
                             ap(o_xrc + h * 2 * Q * 4, 256, F32),
                             ap(o_dp + h * 2 * 4, 1, F32),
                             bank[:, c0:c0 + 256], ALU.mult, ALU.add))
                for i, hp in enumerate(hps):
                    S.op("vector", lambda e, hp=hp, k=k, i=i: e.scalar_tensor_tensor(
                        t_st[:, hp * 256:(hp + 1) * 256], t_st[:, hp * 256:(hp + 1) * 256],
                        t_dtc[:, hp * NCH + k:hp * NCH + k + 1],
                        psF[4][:, i * 256:i * 256 + 256], ALU.mult, ALU.add),
                         inc="VE" if i == 1 else None)
            else:
                m_s1 = S.mark("tensor")
                if m_s1:
                    S.wait("vector", m_s1[0], m_s1[1])
                for i, hp in enumerate(hps):
                    crec = ap(o_scr, 128, F32)
                    S.op("vector", lambda e, hp=hp, k=k, o=crec: e.tensor_add(
                        o, t_ch[:, hp * T + k * Q:hp * T + (k + 1) * Q],
                        t_cl[:, hp * T + k * Q:hp * T + (k + 1) * Q]))
                    S.op("vector", lambda e, o=crec, i=i: e.tensor_mul(
                        o, o, psF[0][:, i * 128:(i + 1) * 128]))
                    S.op("vector", lambda e, o=crec, i=i: e.tensor_copy(out=ctk(0, i), in_=o))
                    S.op("vector", lambda e, o=crec, i=i: e.tensor_sub(ctk(1, i), o, ctk(0, i)),
                         inc="VE" if i == 1 else None)
                m_ct = S.now("VE")
                S.wait("tensor", "VE", m_ct)
                for half_j, jset in ((0, (0, 2)), (1, (1, 3))):
                    for j in jset:
                        h = heads[j]
                        hp, r0 = h // 2, (h % 2) * 64
                        i = hp - 2 * half
                        bank = psF[3] if j < 2 else psF[1]
                        c0 = (j % 2) * 256
                        nmm = 0
                        for pc in range(2):
                            for lh, rh in ((t_sth, ctk(0, i)), (t_sth, ctk(1, i)),
                                           (t_stl, ctk(0, i))):
                                nmm += 1
                                S.op("tensor", lambda e, lh=lh, rh=rh, pc=pc, hp=hp, r0=r0,
                                     c0=c0, bank=bank, fs=(nmm == 1), ls=(nmm == 6):
                                     e.matmul(
                                         bank[:, c0 + pc * 128:c0 + pc * 128 + 128],
                                         lh[r0:r0 + 64, hp * 256 + pc * 128:
                                            hp * 256 + pc * 128 + 128],
                                         rh[r0:r0 + 64, :], start=fs, stop=ls))
                    S.op("tensor", lambda e: e.matmul(
                        psX[:, 0:NCH], c_selp[:, 0:128], t_eppf, start=True, stop=True),
                         inc="TE" if half_j == 1 else None)
                m_s3 = S.now("TE")
                S.wait("vector", "TE", m_s3)
                for j, h in enumerate(heads):
                    bank = psF[3] if j < 2 else psF[1]
                    c0 = (j % 2) * 256
                    for pc in range(2):
                        ci = h * 2 + pc
                        S.op("vector", lambda e, ci=ci, pc=pc, k=k, h=h, c0=c0, bank=bank,
                             ymst=ymst: e.scalar_tensor_tensor(
                                 ymst[:, ci * Q:(ci + 1) * Q],
                                 bank[:, c0 + pc * 128:c0 + pc * 128 + 128],
                                 t_epc[:, h * NCH + k:h * NCH + k + 1],
                                 ymst[:, ci * Q:(ci + 1) * Q], ALU.mult, ALU.add,
                                 accum_out=t_parts[:, k * CI + ci:k * CI + ci + 1]),
                             inc="VE" if (j == 3 and pc == 1) else None)
                m_ev = S.now("VE")
                S.wait("scalar", "VE", m_ev)
                for j, h in enumerate(heads):
                    for pc in range(2):
                        ci = h * 2 + pc
                        S.op("scalar", lambda e, ci=ci, k=k, ymst=ymst: e.activation(
                            out=ap(o_scr + 512, 128, F32), in_=ymst[:, ci * Q:(ci + 1) * Q],
                            func=AF.Square,
                            accum_out=t_parts[:, NCH * CI + k * CI + ci:
                                              NCH * CI + k * CI + ci + 1]),
                             inc="SE" if (j == 3 and pc == 1) else None)
        gp_gate[0] = ("VE", S.now("VE"))
        S.mark("vector")
        S.wait("gpsimd", "VV")
        S.wait("gpsimd", "VE")
        if corr:
            S.mark("scalar")
            S.wait("gpsimd", "SS")
            S.wait("gpsimd", "SE")
            scan_marks[("sq", k)] = S.now("SE")
        S.dma("gpsimd", lambda e, k=k, d=ymst: e.dma_start(out=sp_ym[k], in_=d), "SP")
        scan_marks[("wb", k)] = S.now("SP")
        if not corr:
            m = S.mark("tensor")
            scan_marks[("xte", k)] = m
            scan_marks[("xve", k)] = ("VE", S.now("VE"))


    _NLIM = int(_osd.environ.get("KDBG_NCHLIM", NCH))
    for k in range(min(NCH, _NLIM)):
        scan_chunk(k, corr=False)

    S.wait("gpsimd", "VE")
    for hp in range(CB):
        S.dma("gpsimd", lambda e, hp=hp: e.dma_start(
            out=cc_in.ap()[hp, :, 0:256], in_=t_st[:, hp * 256:(hp + 1) * 256]), "SP")
        S.dma("gpsimd", lambda e, hp=hp: e.dma_start(
            out=cc_in.ap()[hp, :, 256:257], in_=t_dsh[:, hp:hp + 1],
            allow_slow_non_contiguous=True), "SP")
    S.wait("gpsimd", "SP")
    if n_shards > 1:
        S.op("gpsimd", lambda e: e.collective_compute(
            "AllGather", ALU.bypass, replica_groups=groups,
            ins=[cc_in.ap().opt()], outs=[cc_out.ap().opt()]), inc="CC")
        S.wait("gpsimd", "CC")
    else:
        S.dma("gpsimd", lambda e: e.dma_start(out=cc_out.ap()[0], in_=cc_in.ap()[:, :, :]),
              "SP")
        S.wait("gpsimd", "SP")
    A.mark("right")
    o_gth = A.alloc(n_shards * CB * 257 * 4, "right")
    o_pfx = A.alloc(2 * CB * 1024, "right")
    t_gth = ap(o_gth, n_shards * CB * 257, F32)
    for m_ in range(n_shards):
        for hp_ in range(CB):
            S.dma("gpsimd", lambda e, m_=m_, hp_=hp_: e.dma_start(
                out=t_gth[:, (m_ * CB + hp_) * 257:(m_ * CB + hp_ + 1) * 257],
                in_=cc_out.ap()[m_, hp_]), "SP")
    S.wait("vector", "SP")

    def gs(m, hp):
        return ap(o_gth + (m * CB + hp) * 257 * 4, 256, F32)

    def gd(m, hp):
        return ap(o_gth + ((m * CB + hp) * 257 + 256) * 4, 1, F32)

    def pb_(par, hp):
        return ap(o_pfx + (par * CB + hp) * 1024, 256, F32)

    for hp in range(CB):
        S.op("vector", lambda e, hp=hp: e.tensor_copy(out=pb_(0, hp), in_=gs(0, hp)))
        S.edge("vector")
        S.op("vector", lambda e, hp=hp: e.tensor_scalar(
            t_st[:, hp * 256:(hp + 1) * 256], pb_(0, hp), c_sel[:, 0:1], None, ALU.mult))
        S.edge("vector")
        for m in range(1, n_shards):
            S.op("vector", lambda e, hp=hp, m=m: e.scalar_tensor_tensor(
                pb_(m % 2, hp), pb_((m - 1) % 2, hp), gd(m, hp), gs(m, hp),
                ALU.mult, ALU.add))
            S.edge("vector")
            S.op("vector", lambda e, hp=hp, m=m: e.scalar_tensor_tensor(
                t_st[:, hp * 256:(hp + 1) * 256], pb_(m % 2, hp), c_sel[:, m:m + 1],
                t_st[:, hp * 256:(hp + 1) * 256], ALU.mult, ALU.add))
            S.edge("vector")
    S.op("vector", lambda e: e.tensor_copy(out=t_sth, in_=t_st))
    S.edge("vector")
    S.op("vector", lambda e: e.tensor_sub(t_stl, t_st, t_sth), inc="VE")
    A.release("right")

    S.barrier()
    import os as _os
    if not _os.environ.get("KDBG_NO_CORR"):
        for k in range(min(NCH, _NLIM)):
            scan_chunk(k, corr=True)
    S.barrier()
    A.release("right")

    A.mark("right")
    pview = t_parts.rearrange("p (g k c) -> p g k c", g=2, c=CI)
    for g in range(2):
        for h in range(NH):
            S.op("vector", lambda e, g=g, h=h: e.reduce_sum(
                t_tiny[:, 32 + g * 8 + h:32 + g * 8 + h + 1],
                pview[:, g, :, 2 * h:2 * h + 2], axis=AX.XY),
                 inc="VE" if (g == 1 and h == NH - 1) else None)
    S.wait("tensor", "VE")
    S.op("tensor", lambda e: e.matmul(psX[0:1, 0:16], c_o128, t_tiny[:, 32:48],
                                      start=True, stop=True), inc="TE")
    S.wait("vector", "TE")
    S.op("vector", lambda e: e.tensor_copy(out=t_tiny[0:1, 0:16], in_=psX[0:1, 0:16]),
         inc="VE")
    S.wait("gpsimd", "VE")
    S.dma("gpsimd", lambda e: e.dma_start(out=ar_in.ap()[:, :], in_=t_tiny[0:1, 0:16]), "SP")
    S.wait("gpsimd", "SP")
    if n_shards > 1:
        S.op("gpsimd", lambda e: e.collective_compute(
            "AllReduce", ALU.add, replica_groups=groups,
            ins=[ar_in.ap().opt()], outs=[ar_out.ap().opt()]), inc="CC")
        S.wait("gpsimd", "CC")
    else:
        S.dma("gpsimd", lambda e: e.dma_start(out=ar_out.ap()[:, :], in_=ar_in.ap()[:, :]),
              "SP")
        S.wait("gpsimd", "SP")
    S.dma("gpsimd", lambda e: e.dma_start(out=t_tiny[0:1, 16:32], in_=ar_out.ap()[:, :]),
          "SP")
    S.wait("tensor", "SP")
    S.op("tensor", lambda e: e.transpose(psX[0:8, 0:1], t_tiny[0:1, 16:24], c_idf[0:1, 0:1]))
    S.op("tensor", lambda e: e.transpose(psX[0:8, 1:2], t_tiny[0:1, 24:32], c_idf[0:1, 0:1]),
         inc="TE")
    S.wait("vector", "TE")
    S.op("vector", lambda e: e.tensor_scalar(t_tiny[0:8, 48:49], psX[0:8, 0:1],
                                             1.0 / CNT_GN, None, ALU.mult))
    S.op("vector", lambda e: e.tensor_scalar(t_tiny[0:8, 47:48], psX[0:8, 1:2],
                                             1.0 / CNT_GN, None, ALU.mult))
    S.op("vector", lambda e: e.tensor_mul(t_tiny[0:8, 49:50], t_tiny[0:8, 48:49],
                                          t_tiny[0:8, 48:49]))
    S.op("vector", lambda e: e.tensor_sub(t_tiny[0:8, 50:51], t_tiny[0:8, 47:48],
                                          t_tiny[0:8, 49:50]), inc="VE")
    S.wait("scalar", "VE")
    S.op("scalar", lambda e: e.activation(out=t_tiny[0:8, 54:55], in_=t_tiny[0:8, 50:51],
                                          func=AF.Sqrt, bias=c_eps8), inc="SE")
    S.wait("vector", "SE")
    S.op("vector", lambda e: e.reciprocal(t_tiny[0:8, 51:52], t_tiny[0:8, 54:55]))
    S.edge("vector")
    S.op("vector", lambda e: e.tensor_mul(t_tiny[0:8, 52:53], t_tiny[0:8, 48:49],
                                          t_tiny[0:8, 51:52]))
    S.edge("vector")
    S.op("vector", lambda e: e.tensor_scalar(t_tiny[0:8, 53:54], t_tiny[0:8, 52:53],
                                             -1.0, None, ALU.mult))
    S.edge("vector")
    S.op("vector", lambda e: e.tensor_scalar(t_tiny[0:8, 56:56 + CI], c_mci,
                                             t_tiny[0:8, 51:52], None, ALU.mult))
    S.edge("vector")
    S.op("vector", lambda e: e.tensor_scalar(t_tiny[0:8, 56 + CI:56 + 2 * CI], c_mci,
                                             t_tiny[0:8, 53:54], None, ALU.mult), inc="VE")
    S.wait("tensor", "VE")
    S.op("tensor", lambda e: e.matmul(psX[:, 0:CI], c_o8, t_tiny[0:8, 56:56 + CI],
                                      start=True, stop=True))
    S.op("tensor", lambda e: e.matmul(psX[:, CI:2 * CI], c_o8,
                                      t_tiny[0:8, 56 + CI:56 + 2 * CI],
                                      start=True, stop=True), inc="TE")
    S.wait("vector", "TE")
    S.op("vector", lambda e: e.tensor_mul(t_stc[:, 0:CI], psX[:, 0:CI], c_gnw))
    S.edge("vector")
    S.op("vector", lambda e: e.tensor_mul(t_tiny[:, 60:60 + CI], psX[:, CI:2 * CI], c_gnw))
    S.edge("vector")
    S.op("vector", lambda e: e.tensor_add(t_stc[:, CI:2 * CI], t_tiny[:, 60:60 + CI],
                                          c_gnb), inc="VE")
    S.barrier()
    A.release("right")
    A.release("left")

    A.mark("left")
    o_wrh = sb(CD * D * 2); o_wrl = sb(CD * D * 2)
    o_wvh = sb(CD * D * 2); o_wvl = sb(CD * D * 2)
    o_wkh = sb(CD * CD * 256); o_wkl = sb(CD * CD * 256)
    t_wrh = ap(o_wrh, CD * D, BF16); t_wrl = ap(o_wrl, CD * D, BF16)
    t_wvh = ap(o_wvh, CD * D, BF16); t_wvl = ap(o_wvl, CD * D, BF16)
    t_wkh = ap(o_wkh, CD * CD * 128, BF16); t_wkl = ap(o_wkl, CD * CD * 128, BF16)
    A.mark("right")
    o_wst = A.alloc(D * 4, "right")
    o_xnt = A.alloc(2 * CD * 257 * 2, "right")
    o_xnr = A.alloc(CD * 257 * 4, "right")
    o_mx2 = A.alloc(2 * FT * 4, "right")
    o_xrh = A.alloc(CD * FT * 2, "right"); o_xrl = A.alloc(CD * FT * 2, "right")
    o_xkh = A.alloc(CD * FT * 2, "right"); o_xkl = A.alloc(CD * FT * 2, "right")
    o_kq = A.alloc(FT * 4, "right")
    o_kqh = A.alloc(CD * FT * 2, "right"); o_kql = A.alloc(CD * FT * 2, "right")
    o_rt = A.alloc((FT // 128) * D * 4, "right")
    o_ffo = A.alloc(D * 4, "right")
    o_xr2 = A.alloc((FT // 128) * D * 4, "right")
    wst = ap(o_wst, D, F32)

    for (dsrc, th_, tl_, isp) in ((dwr, t_wrh, t_wrl, False), (dwv, t_wvh, t_wvl, False),
                                  (dwk, t_wkh, t_wkl, True)):
        for oc in range(CD):
            S.wait("sync", "VE")
            par = newld()
            if isp:
                for cc in range(CD):
                    S.dma("sync", lambda e, oc=oc, cc=cc, s=dsrc: e.dma_start(
                        out=wst[:, cc * 128:(cc + 1) * 128], in_=s[oc, cc]), par)
            else:
                S.dma("sync", lambda e, oc=oc, s=dsrc: e.dma_start(
                    out=wst, in_=s[oc]), par)
            S.wait("vector", par)
            S.op("vector", lambda e, oc=oc, d=th_: e.tensor_copy(
                out=d[:, oc * D:(oc + 1) * D], in_=wst))
            S.op("vector", lambda e, oc=oc, dh=th_, dl=tl_: e.tensor_sub(
                dl[:, oc * D:(oc + 1) * D], wst, dh[:, oc * D:(oc + 1) * D]), inc="VE")

    nft = T // FT
    ve_gf = []
    gfi = [0]
    for tt in range(nft):
        c0 = 2 + tt * FT
        xnth2 = ap(o_xnt, CD * 257, BF16)
        xntl2 = ap(o_xnt + CD * 257 * 2, CD * 257, BF16)
        xnr = ap(o_xnr, CD * 257, F32)
        xrh = ap(o_xrh, CD * FT, BF16); xrl = ap(o_xrl, CD * FT, BF16)
        xkh = ap(o_xkh, CD * FT, BF16); xkl = ap(o_xkl, CD * FT, BF16)
        kq = ap(o_kq, FT, F32)
        kqh = ap(o_kqh, CD * FT, BF16); kql = ap(o_kql, CD * FT, BF16)
        rt = ap(o_rt, (FT // 128) * D, F32)
        ffo = ap(o_ffo, D, F32)
        S.qbar("sync", ("tensor", "vector"))
        par = newld()
        for cc in range(CD):
            S.dma("sync", lambda e, cc=cc, c0=c0, d=xnth2: e.dma_start(
                out=d[:, cc * 257:(cc + 1) * 257],
                in_=sp_xnh[:, cc * TH + c0:cc * TH + c0 + FT + 1]), par)
            S.dma("sync", lambda e, cc=cc, c0=c0, d=xntl2: e.dma_start(
                out=d[:, cc * 257:(cc + 1) * 257],
                in_=sp_xnl[:, cc * TH + c0:cc * TH + c0 + FT + 1]), par)
        for tb in range(FT // 128):
            S.dma("gpsimd", lambda e, tt=tt, tb=tb: e.dma_start(
                out=ap(o_xr2 + tb * D * 4, D, F32),
                in_=dx[tt * FT + tb * 128:tt * FT + tb * 128 + 128, :]), "SP")
        S.wait("vector", par)
        S.wait("vector", "SP")
        S.op("vector", lambda e: e.tensor_add(xnr, xnth2, xntl2))
        for cc in range(CD):
            for si, (ho, lo_, mu, mu1) in enumerate(((xrh, xrl, o_mur, o_mur1),
                                                     (xkh, xkl, o_muk, o_muk1))):
                mxf = ap(o_mx2 + si * FT * 4, FT, F32)
                S.op("vector", lambda e, cc=cc, d=mxf, mu1=mu1: e.tensor_scalar(
                    d, xnr[:, cc * 257:cc * 257 + FT],
                    ap(mu1 + cc * 4, 1, F32), None, ALU.mult))
                S.op("vector", lambda e, cc=cc, d=mxf, mu=mu: e.scalar_tensor_tensor(
                    d, xnr[:, cc * 257 + 1:cc * 257 + FT + 1],
                    ap(mu + cc * 4, 1, F32), d, ALU.mult, ALU.add))
                S.op("vector", lambda e, cc=cc, d=mxf, h=ho: e.tensor_copy(
                    out=h[:, cc * FT:(cc + 1) * FT], in_=d))
                S.op("vector", lambda e, cc=cc, d=mxf, h=ho, l=lo_: e.tensor_sub(
                    l[:, cc * FT:(cc + 1) * FT], d, h[:, cc * FT:(cc + 1) * FT]),
                     inc="VE" if (cc == CD - 1 and si == 1) else None)
        S.wait("tensor", "VE")
        for oc in range(CD):
            g = gfi[0]; gfi[0] += 1
            pb = psF[g % 4]
            if g >= 4:
                S.wait("tensor", ve_gf[g - 4][0], ve_gf[g - 4][1])
            seq = [(a, b, cc) for a, b in ((t_wkh, xkh), (t_wkh, xkl), (t_wkl, xkh))
                   for cc in range(CD)]
            for i, (wop, aop, cc) in enumerate(seq):
                S.op("tensor", lambda e, p=pb, wop=wop, aop=aop, oc=oc, cc=cc,
                     fs=(i == 0), ls=(i == len(seq) - 1): e.matmul(
                         p[:, 0:FT], wop[:, (oc * CD + cc) * 128:(oc * CD + cc) * 128 + 128],
                         aop[:, cc * FT:(cc + 1) * FT], start=fs, stop=ls),
                     inc="TE" if i == len(seq) - 1 else None)
            S.wait("vector", "TE")
            S.op("vector", lambda e, p=pb: e.tensor_scalar_max(kq, p[:, 0:FT], 0.0),
                 inc="VE")
            ve_gf.append(("VE", S.now("VE")))
            S.wait("scalar", "VE")
            S.op("scalar", lambda e: e.activation(
                out=pool[:, o_kq:o_kq + FT * 4].bitcast(F32), in_=kq, func=AF.Square))
            S.op("scalar", lambda e, oc=oc: e.copy(
                out=kqh[:, oc * FT:(oc + 1) * FT], in_=kq), inc="SE")
            S.wait("vector", "SE")
            S.op("vector", lambda e, oc=oc: e.tensor_sub(
                kql[:, oc * FT:(oc + 1) * FT], kq, kqh[:, oc * FT:(oc + 1) * FT]),
                 inc="VE")
        for tb in range(FT // 128):
            for half in range(2):
                g = gfi[0]; gfi[0] += 1
                pb = psF[g % 4]
                if g >= 4:
                    S.wait("tensor", ve_gf[g - 4][0], ve_gf[g - 4][1])
                seq = [(a, b, cc) for a, b in ((xrh, t_wrh), (xrh, t_wrl), (xrl, t_wrh))
                       for cc in range(CD)]
                for i, (aop, wop, cc) in enumerate(seq):
                    S.op("tensor", lambda e, p=pb, aop=aop, wop=wop, cc=cc, tb=tb,
                         half=half, fs=(i == 0), ls=(i == len(seq) - 1): e.matmul(
                             p[:, 0:512],
                             aop[:, cc * FT + tb * 128:cc * FT + tb * 128 + 128],
                             wop[:, cc * D + half * 512:cc * D + half * 512 + 512],
                             start=fs, stop=ls),
                         inc="TE" if i == len(seq) - 1 else None)
                S.wait("scalar", "TE")
                S.op("scalar", lambda e, p=pb, tb=tb, half=half: e.activation(
                    out=rt[:, tb * D + half * 512:tb * D + half * 512 + 512],
                    in_=p[:, 0:512], func=AF.Sigmoid), inc="SE")
                ve_gf.append(("SE", S.now("SE")))
        S.wait("tensor", "SE")
        S.wait("tensor", "VE")
        for tb in range(FT // 128):
            for half in range(2):
                g = gfi[0]; gfi[0] += 1
                pb = psF[g % 4]
                if g >= 4:
                    S.wait("tensor", ve_gf[g - 4][0], ve_gf[g - 4][1])
                seq = [(a, b, cc) for a, b in ((kqh, t_wvh), (kqh, t_wvl), (kql, t_wvh))
                       for cc in range(CD)]
                for i, (aop, wop, cc) in enumerate(seq):
                    S.op("tensor", lambda e, p=pb, aop=aop, wop=wop, cc=cc, tb=tb,
                         half=half, fs=(i == 0), ls=(i == len(seq) - 1): e.matmul(
                             p[:, 0:512],
                             aop[:, cc * FT + tb * 128:cc * FT + tb * 128 + 128],
                             wop[:, cc * D + half * 512:cc * D + half * 512 + 512],
                             start=fs, stop=ls),
                         inc="TE" if i == len(seq) - 1 else None)
                S.wait("vector", "TE")
                S.op("vector", lambda e, p=pb, tb=tb, half=half: e.tensor_mul(
                    ffo[:, half * 512:half * 512 + 512], p[:, 0:512],
                    rt[:, tb * D + half * 512:tb * D + half * 512 + 512]))
                S.op("vector", lambda e, tb=tb, half=half: e.tensor_add(
                    ffo[:, half * 512:half * 512 + 512],
                    ffo[:, half * 512:half * 512 + 512],
                    ap(o_xr2 + tb * D * 4, D, F32)[:, half * 512:half * 512 + 512]),
                     inc="VE")
                ve_gf.append(("VE", S.now("VE")))
            S.wait("gpsimd", "VE")
            S.dma("gpsimd", lambda e, tt=tt, tb=tb: e.dma_start(
                out=sp_ffn[tt * FT + tb * 128:tt * FT + tb * 128 + 128, :], in_=ffo),
                "SP")
            S.wait("vector", "SP")
    S.barrier()
    A.release("right")
    A.release("left")

    A.mark("right")
    o_wz = A.alloc(CI * CD * 256, "right")
    o_wzl = A.alloc(CI * CD * 256, "right")
    o_wo = A.alloc(CI * D * 2, "right")
    o_wol = A.alloc(CI * D * 2, "right")
    o_wst3 = A.alloc(CD * 512, "right")
    o_wst4 = A.alloc(D * 4, "right")
    o_xnt3 = A.alloc(2 * 2 * CD * 256, "right")
    o_ymt3 = A.alloc(2 * CI * Q * 4, "right")
    o_zt = A.alloc(2 * Q * 4, "right")
    o_yt = A.alloc(2 * Q * 4, "right")
    o_mtg = A.alloc(2 * 2 * CI * 256, "right")
    o_ffr = A.alloc(2 * D * 4, "right")
    t_wzh = ap(o_wz, CI * CD * 128, BF16)
    t_wzl = ap(o_wzl, CI * CD * 128, BF16)
    t_woh = ap(o_wo, CI * D, BF16)
    t_wol = ap(o_wol, CI * D, BF16)
    wst3 = ap(o_wst3, CD * 128, F32)
    wst4 = ap(o_wst4, D, F32)

    wz_ve = {}
    wo_ve = {}
    for i in range(CI):
        S.wait("scalar", "VE")
        parz = newls()
        for cc in range(CD):
            S.dma("scalar", lambda e, oc=i, cc=cc: e.dma_start(
                out=wst3[:, cc * 128:(cc + 1) * 128], in_=dwz[oc, cc]), parz)
        S.wait("vector", parz)
        S.op("vector", lambda e, oc=i: e.tensor_copy(
            out=t_wzh[:, oc * CD * 128:(oc + 1) * CD * 128], in_=wst3))
        S.op("vector", lambda e, oc=i: e.tensor_sub(
            t_wzl[:, oc * CD * 128:(oc + 1) * CD * 128], wst3,
            t_wzh[:, oc * CD * 128:(oc + 1) * CD * 128]), inc="VE")
        wz_ve[i] = S.now("VE")
        S.wait("gpsimd", "VE")
        S.dma("gpsimd", lambda e, cc=i: e.dma_start(out=wst4, in_=dwout[cc]), "SP")
        S.wait("vector", "SP")
        S.op("vector", lambda e, cc=i: e.tensor_copy(
            out=t_woh[:, cc * D:(cc + 1) * D], in_=wst4))
        S.op("vector", lambda e, cc=i: e.tensor_sub(
            t_wol[:, cc * D:(cc + 1) * D], wst4,
            t_woh[:, cc * D:(cc + 1) * D]), inc="VE")
        wo_ve[i] = S.now("VE")

    ntl3 = NCH
    te_wz = {}
    te_out = {}
    ve_mix = {}
    ve_out = {}
    sp_out = {}
    pf_par = {}
    se_z = {}
    ve_oc = {}
    gi3 = [0]

    def sl3(t):
        sl = t % 2
        return (ap(o_xnt3 + sl * 2 * CD * 256, CD * 128, BF16),
                ap(o_xnt3 + sl * 2 * CD * 256 + CD * 256, CD * 128, BF16),
                ap(o_ymt3 + sl * CI * Q * 4, CI * Q, F32),
                ap(o_ffr + sl * D * 4, D, F32),
                ap(o_mtg + sl * 2 * CI * 256, CI * 128, BF16),
                ap(o_mtg + sl * 2 * CI * 256 + CI * 256, CI * 128, BF16))

    def pf3(t):
        if t >= ntl3:
            return
        xnh3, xnl3, ymt, ffr, _, _ = sl3(t)
        if t >= 2:
            S.wait("sync", "TE", te_wz[t - 2])
            S.wait("sync", "VE", ve_mix[t - 2])
            S.wait("scalar", "SP", sp_out[t - 2])
        par = newld()
        for cc in range(CD):
            S.dma("sync", lambda e, cc=cc, t=t, d=xnh3: e.dma_start(
                out=d[:, cc * 128:(cc + 1) * 128],
                in_=sp_xnh[:, cc * TH + 3 + t * Q:cc * TH + 3 + (t + 1) * Q]), par)
            S.dma("sync", lambda e, cc=cc, t=t, d=xnl3: e.dma_start(
                out=d[:, cc * 128:(cc + 1) * 128],
                in_=sp_xnl[:, cc * TH + 3 + t * Q:cc * TH + 3 + (t + 1) * Q]), par)
        S.dma("sync", lambda e, t=t, d=ymt: e.dma_start(out=d, in_=sp_ym[t]), par)
        pars = newls()
        S.dma("scalar", lambda e, t=t, d=ffr: e.dma_start(
            out=d, in_=sp_ffn[t * Q:(t + 1) * Q, :]), pars)
        pf_par[t] = (par, pars)

    pf3(0)
    for t in range(ntl3 + 1):
        if t >= 1:
            tp = t - 1
            _, _, _, ffrp, mghp, mglp = sl3(tp)
            S.wait("tensor", "VE", ve_mix[tp])
            if tp == 0:
                S.wait("tensor", "VE", wo_ve[CI - 1])
            for half in range(2):
                pb = psF[4 + half]
                if tp >= 1:
                    S.wait("tensor", "VE", ve_out[tp - 1])
                seq = [(a, b, cc) for a, b in ((mghp, t_woh), (mghp, t_wol),
                                               (mglp, t_woh)) for cc in range(CI)]
                for i, (aop, wop, cc) in enumerate(seq):
                    S.op("tensor", lambda e, p=pb, aop=aop, wop=wop, cc=cc, half=half,
                         fs=(i == 0), ls=(i == len(seq) - 1): e.matmul(
                             p[:, 0:512], aop[:, cc * 128:(cc + 1) * 128],
                             wop[:, cc * D + half * 512:cc * D + half * 512 + 512],
                             start=fs, stop=ls),
                         inc="TE" if i == len(seq) - 1 else None)
            te_out[tp] = S.now("TE")
            S.wait("vector", "TE", te_out[tp])
            S.wait("vector", pf_par[tp][0])
            S.wait("vector", pf_par[tp][1])
            for half in range(2):
                S.op("vector", lambda e, p=psF[4 + half], d=ffrp, half=half:
                     e.tensor_add(d[:, half * 512:half * 512 + 512], p[:, 0:512],
                                  d[:, half * 512:half * 512 + 512]),
                     inc="VE" if half == 1 else None)
            ve_out[tp] = S.now("VE")
            S.wait("gpsimd", "VE", ve_out[tp])
            S.dma("gpsimd", lambda e, tp=tp, d=ffrp: e.dma_start(
                out=dout[tp * Q:(tp + 1) * Q, :], in_=d), "SP")
            sp_out[tp] = S.now("SP")
        if t >= ntl3:
            continue
        xnh3, xnl3, ymt, ffr, mgh, mgl = sl3(t)
        pf3(t + 1)
        S.wait("tensor", pf_par[t][0])
        first_mix = True
        for oc in range(CI):
            g = gi3[0]
            gi3[0] += 1
            pb = psF[g % 4]
            if t == 0:
                S.wait("tensor", "VE", wz_ve[oc])
            if g >= 4:
                S.wait("tensor", "SE", se_z[g - 4])
            seq = [(a, b, cc) for a, b in ((t_wzh, xnh3), (t_wzh, xnl3), (t_wzl, xnh3))
                   for cc in range(CD)]
            for i, (wop, aop, cc) in enumerate(seq):
                S.op("tensor", lambda e, p=pb, wop=wop, aop=aop, oc=oc, cc=cc,
                     fs=(i == 0), ls=(i == len(seq) - 1): e.matmul(
                         p[:, 0:Q],
                         wop[:, (oc * CD + cc) * 128:(oc * CD + cc) * 128 + 128],
                         aop[:, cc * 128:(cc + 1) * 128], start=fs, stop=ls),
                     inc="TE" if i == len(seq) - 1 else None)
            if oc == CI - 1:
                te_wz[t] = S.now("TE")
            zt = ap(o_zt + (g % 2) * Q * 4, Q, F32)
            yt = ap(o_yt + (g % 2) * Q * 4, Q, F32)
            S.wait("scalar", "TE")
            if g >= 2:
                S.wait("scalar", "VE", ve_oc[g - 2])
            S.op("scalar", lambda e, p=pb, d=zt: e.activation(
                out=d, in_=p[:, 0:Q], func=AF.Silu), inc="SE")
            se_z[g] = S.now("SE")
            S.wait("vector", "SE", se_z[g])
            if first_mix:
                S.wait("vector", pf_par[t][0])
                if t >= 2:
                    S.wait("vector", "TE", te_out[t - 2])
                first_mix = False
            S.op("vector", lambda e, oc=oc, d=yt, ymt=ymt: e.tensor_scalar(
                d, ymt[:, oc * Q:(oc + 1) * Q], ap(o_stc + oc * 4, 1, F32),
                ap(o_stc + (CI + oc) * 4, 1, F32), ALU.mult, ALU.add))
            S.op("vector", lambda e, d=yt, z=zt: e.tensor_mul(d, d, z))
            S.op("vector", lambda e, oc=oc, d=yt, m=mgh: e.tensor_copy(
                out=m[:, oc * 128:(oc + 1) * 128], in_=d))
            S.op("vector", lambda e, oc=oc, d=yt, mh=mgh, ml=mgl: e.tensor_sub(
                ml[:, oc * 128:(oc + 1) * 128], d, mh[:, oc * 128:(oc + 1) * 128]),
                 inc="VE")
            ve_oc[g] = S.now("VE")
        ve_mix[t] = ve_oc[gi3[0] - 1]
    S.wait("gpsimd", "SP")
    A.release("right")

    nc._dbg = dict(o_bh=o_bh, o_bl=o_bl,
                   o_ch=o_ch, o_cl=o_cl, o_st=o_st, o_parts=o_parts,
                   o_expdA=o_expdA, o_stc=o_stc, o_tiny=o_tiny,
                   
                   o_atn=o_atn, o_dtt=o_dtt, o_dwt=o_dwt, o_dtc=o_dtc,
                   o_epc=o_epc, o_dsh=o_dsh, o_eppf=o_eppf)

    with nc.Block() as block:
        def mk(eng_name):
            def go(e):
                for it in S.items[eng_name]:
                    if it[0] == "op":
                        ins = it[1](e)
                        if it[2] is not None:
                            ins.then_inc(S.handles[it[2]], it[3])
                    else:
                        e.wait_ge(S.handles[it[1]], it[2])
            return go
        block.sync(mk("sync"))
        block.tensor(mk("tensor"))
        block.vector(mk("vector"))
        block.scalar(mk("scalar"))
        block.gpsimd(mk("gpsimd"))
    es.close()
    return nc



def _pack(inputs, T, n_shards, n_batches):
    f32 = np.float32
    wx = np.ascontiguousarray(inputs["L0_attn_Wx"], f32)
    wz = np.ascontiguousarray(inputs["L0_attn_Wz"], f32)
    wb = np.ascontiguousarray(inputs["L0_attn_Wb"], f32)
    wc = np.ascontiguousarray(inputs["L0_attn_Wc"], f32)
    wdt = np.ascontiguousarray(inputs["L0_attn_Wdt"], f32)
    wout = np.ascontiguousarray(inputs["L0_attn_Wout"], f32)
    wr = np.ascontiguousarray(inputs["L0_ffn_W_r"], f32)
    wk = np.ascontiguousarray(inputs["L0_ffn_W_k"], f32)
    wv = np.ascontiguousarray(inputs["L0_ffn_W_v"], f32)
    import ml_dtypes
    shared = {
        "wxp": np.ascontiguousarray(wx.reshape(CI, 128, CD, 128).transpose(0, 2, 3, 1)),
        "wzp": np.ascontiguousarray(wz.reshape(CI, 128, CD, 128).transpose(0, 2, 3, 1)),
        "wbp": np.ascontiguousarray(wb.reshape(CB, 128, CI, 128).transpose(0, 2, 3, 1)),
        "wcp": np.ascontiguousarray(wc.reshape(CB, 128, CI, 128).transpose(0, 2, 3, 1)),
        "wdtp": np.ascontiguousarray(wdt.reshape(8, CI, 128).transpose(1, 2, 0)),
        "woutr": np.ascontiguousarray(wout.reshape(D, CI, 128).transpose(1, 2, 0)),
        "wrr": np.ascontiguousarray(wr.reshape(D, CD, 128).transpose(1, 2, 0)),
        "wvr": np.ascontiguousarray(wv.reshape(D, CD, 128).transpose(1, 2, 0)),
        "wkp": np.ascontiguousarray(wk.reshape(CD, 128, CD, 128).transpose(0, 2, 3, 1)),
        "convw": np.ascontiguousarray(
            np.asarray(inputs["L0_attn_conv_w"], f32)[:, 0, :]
            .reshape(CI, 128, KCV).transpose(1, 0, 2).reshape(128, CI * KCV)),
        "convb": np.ascontiguousarray(
            np.asarray(inputs["L0_attn_conv_b"], f32).reshape(CI, 128).T),
        "gnw": np.ascontiguousarray(
            np.asarray(inputs["L0_attn_gn_w"], f32).reshape(CI, 128).T),
        "gnb": np.ascontiguousarray(
            np.asarray(inputs["L0_attn_gn_b"], f32).reshape(CI, 128).T),
        "dprep": np.ascontiguousarray(
            np.repeat(np.asarray(inputs["L0_attn_d_param"], f32), PHD)
            .reshape(CI, 128).T),
        "murc": np.ascontiguousarray(
            np.asarray(inputs["L0_ffn_mu_r"], f32).reshape(CD, 128).T),
        "mukc": np.ascontiguousarray(
            np.asarray(inputs["L0_ffn_mu_k"], f32).reshape(CD, 128).T),
        "bdt": np.asarray(inputs["L0_attn_b_dt"], f32).reshape(8, 1),
        "loga": np.asarray(inputs["L0_attn_log_a"], f32).reshape(8, 1),
        "identb": np.eye(128, dtype=ml_dtypes.bfloat16),
        "identf": np.eye(128, dtype=f32),
        "maskneg": np.where(np.arange(128)[None, :] >= np.arange(128)[:, None],
                            np.float32(0.0), np.float32(NEG)),
        "ones8x128": np.ones((8, 128), f32),
        "ones128": np.ones((128, 1), f32),
        "maskci": (np.arange(CI)[None, :] // 2 == np.arange(8)[:, None]).astype(f32),
    }
    selp = np.zeros((8, CB * 128), f32)
    for pc in range(CB):
        for m in range(128):
            selp[2 * pc + (m >= 64), pc * 128 + m] = 1.0
    shared["selp"] = selp
    selh = np.zeros((8, NH * 128), f32)
    for h in range(NH):
        selh[h, h * 128:(h + 1) * 128] = 1.0
    shared["selh"] = selh

    x = np.ascontiguousarray(np.asarray(inputs["L0_attn_input_tensor"], f32))
    per_core = []
    for core in range(n_shards * n_batches):
        b, s = core // n_shards, core % n_shards
        m = dict(shared)
        m["x"] = np.ascontiguousarray(x[b, s * T:(s + 1) * T])
        if s == 0:
            m["xh3"] = np.zeros((3, D), f32)
        else:
            m["xh3"] = np.ascontiguousarray(x[b, s * T - 3:s * T])
        sel = np.zeros((128, 4), f32)
        if s > 0:
            sel[:, s - 1] = 1.0
        m["selshard"] = sel
        per_core.append(m)
    return per_core


def kernel(**inputs):
    B, L, _ = inputs["L0_attn_input_tensor"].shape
    n_batches = B
    n_shards = 8 // n_batches
    T = L // n_shards
    in_maps = _pack(inputs, T, n_shards, n_batches)
    nc = build_program(T, n_shards, n_batches)
    res = run_bass_kernel_spmd(nc, in_maps, core_ids=list(range(8)))
    out = np.empty((B, L, D), np.float32)
    for core in range(8):
        b, s = core // n_shards, core % n_shards
        out[b, s * T:(s + 1) * T] = res.results[core]["out"]
    return out


# version 21
# speedup vs baseline: 1.0234x; 1.0234x over previous
import numpy as np
from contextlib import ExitStack

import concourse.bass as bass
import concourse.mybir as mybir
from concourse.bass_utils import run_bass_kernel_spmd

F32 = mybir.dt.float32
BF16 = mybir.dt.bfloat16
U8 = mybir.dt.uint8
AF = mybir.ActivationFunctionType
ALU = mybir.AluOpType
AX = mybir.AxisListType

D = 1024
INNER = 2048
NH = 8
NST = 64
PHD = 256
KCV = 4
Q = 128
CD = D // 128
CI = INNER // 128
CB = (NH * NST) // 128
EPS = 1e-5
NEG = -1e30

ENGS = ("sync", "tensor", "vector", "scalar", "gpsimd")
SEMS = ("LD", "LDA", "LDB", "LDC", "LS", "LSA", "SP", "TE", "VE", "SE", "GE", "CC", "TT", "VV", "SS", "GG")
SELF_SEM = {"tensor": "TT", "vector": "VV", "scalar": "SS", "gpsimd": "GG",
            "sync": "VV"}


def _tiles(n, m=512):
    k = -(-n // m)
    base, rem = n // k, n % k
    out, off = [], 0
    for i in range(k):
        w = base + (1 if i < rem else 0)
        out.append((off, w))
        off += w
    return out


class Sched:

    def __init__(self):
        self.items = {e: [] for e in ENGS}
        self.cnt = {s: 0 for s in SEMS}
        self.handles = {}
        self.auto_edge = True

    def op(self, eng, fn, inc=None, amt=1):
        if self.auto_edge and eng in ("vector", "scalar", "gpsimd"):
            m = self.mark(eng)
            if m:
                self.items[eng].append(["wait", m[0], m[1], None])
        if inc is not None:
            self.cnt[inc] += amt
        self.items[eng].append(["op", fn, inc, amt])

    def dma(self, eng, fn, inc):
        ae = self.auto_edge
        self.auto_edge = False
        self.op(eng, fn, inc, 16)
        self.auto_edge = ae

    def wait(self, eng, sem, val=None):
        v = self.cnt[sem] if val is None else val
        if v > 0:
            self.items[eng].append(["wait", sem, v, None])

    def now(self, sem):
        return self.cnt[sem]

    def mark(self, eng):
        for it in reversed(self.items[eng]):
            if it[0] == "op":
                if it[2] is None:
                    s = SELF_SEM[eng]
                    it[2], it[3] = s, 1
                    self.cnt[s] += 1
                return (it[2], self.cnt[it[2]])
        return None

    def edge(self, eng):
        m = self.mark(eng)
        if m:
            self.wait(eng, m[0], m[1])

    def qbar(self, eng, others=("tensor", "vector", "scalar", "gpsimd")):
        for o in others:
            if o == eng:
                continue
            m = self.mark(o)
            if m:
                self.wait(eng, m[0], m[1])

    def barrier(self, engs=ENGS):
        marks = [m for m in (self.mark(e) for e in engs) if m]
        for e in engs:
            for sem, v in marks:
                self.wait(e, sem, v)


class Arena:
    def __init__(self, size):
        self.size = size
        self.left = 0
        self.right = size
        self.lstack = []
        self.rstack = []
        self.peak = 0

    def alloc(self, nbytes, side="left", align=64):
        if side == "left":
            off = -(-self.left // align) * align
            self.left = off + nbytes
        else:
            off = (self.right - nbytes) // align * align
            self.right = off
        assert self.left <= self.right, (
            f"SBUF arena overflow: left={self.left} right={self.right}")
        assert self.left <= self.right, (
            f"SBUF pool overflow: left={self.left} right={self.right}")
        self.peak = max(self.peak, self.left + (self.size - self.right))
        return off

    def mark(self, side="right"):
        st = self.lstack if side == "left" else self.rstack
        st.append(self.left if side == "left" else self.right)

    def release(self, side="right"):
        if side == "left":
            self.left = self.lstack.pop()
        else:
            self.right = self.rstack.pop()


def build_program(T, n_shards, n_batches, part_bytes=200704):
    NCH = T // Q
    TH = T + 3
    CNT_GN = float(n_shards * T * PHD)
    FT = min(256, T)

    nc = bass.Bass("TRN2", target_bir_lowering=False, debug=False)

    dx = nc.dram_tensor("x", [T, D], F32, kind="ExternalInput").ap()
    dxh = nc.dram_tensor("xh3", [3, D], F32, kind="ExternalInput").ap()
    dwx = nc.dram_tensor("wxp", [CI, CD, 128, 128], F32, kind="ExternalInput").ap()
    dwz = nc.dram_tensor("wzp", [CI, CD, 128, 128], F32, kind="ExternalInput").ap()
    dwb = nc.dram_tensor("wbp", [CB, CI, 128, 128], F32, kind="ExternalInput").ap()
    dwc = nc.dram_tensor("wcp", [CB, CI, 128, 128], F32, kind="ExternalInput").ap()
    dwdt = nc.dram_tensor("wdtp", [CI, 128, 8], F32, kind="ExternalInput").ap()
    dwout = nc.dram_tensor("woutr", [CI, 128, D], F32, kind="ExternalInput").ap()
    dwr = nc.dram_tensor("wrr", [CD, 128, D], F32, kind="ExternalInput").ap()
    dwv = nc.dram_tensor("wvr", [CD, 128, D], F32, kind="ExternalInput").ap()
    dwk = nc.dram_tensor("wkp", [CD, CD, 128, 128], F32, kind="ExternalInput").ap()
    dconvw = nc.dram_tensor("convw", [128, CI * KCV], F32, kind="ExternalInput").ap()
    dconvb = nc.dram_tensor("convb", [128, CI], F32, kind="ExternalInput").ap()
    dgnw = nc.dram_tensor("gnw", [128, CI], F32, kind="ExternalInput").ap()
    dgnb = nc.dram_tensor("gnb", [128, CI], F32, kind="ExternalInput").ap()
    ddp = nc.dram_tensor("dprep", [128, CI], F32, kind="ExternalInput").ap()
    dmur = nc.dram_tensor("murc", [128, CD], F32, kind="ExternalInput").ap()
    dmuk = nc.dram_tensor("mukc", [128, CD], F32, kind="ExternalInput").ap()
    dbdt = nc.dram_tensor("bdt", [8, 1], F32, kind="ExternalInput").ap()
    dloga = nc.dram_tensor("loga", [8, 1], F32, kind="ExternalInput").ap()
    didb = nc.dram_tensor("identb", [128, 128], BF16, kind="ExternalInput").ap()
    didf = nc.dram_tensor("identf", [128, 128], F32, kind="ExternalInput").ap()
    dmaskn = nc.dram_tensor("maskneg", [128, 128], F32, kind="ExternalInput").ap()
    dselp = nc.dram_tensor("selp", [8, CB * 128], F32, kind="ExternalInput").ap()
    dselh = nc.dram_tensor("selh", [8, NH * 128], F32, kind="ExternalInput").ap()
    do8 = nc.dram_tensor("ones8x128", [8, 128], F32, kind="ExternalInput").ap()
    do128 = nc.dram_tensor("ones128", [128, 1], F32, kind="ExternalInput").ap()
    dmci = nc.dram_tensor("maskci", [8, CI], F32, kind="ExternalInput").ap()
    dsel = nc.dram_tensor("selshard", [128, 4], F32, kind="ExternalInput").ap()
    dout = nc.dram_tensor("out", [T, D], F32, kind="ExternalOutput").ap()

    sp_xnh = nc.dram_tensor("sp_xnh", [128, CD * TH], BF16).ap()
    sp_xnl = nc.dram_tensor("sp_xnl", [128, CD * TH], BF16).ap()
    sp_ym = nc.dram_tensor("sp_ym", [NCH, 128, CI * Q], F32).ap()
    sp_xph = nc.dram_tensor("sp_xph", [128, CI * T], BF16).ap()
    sp_xpl = nc.dram_tensor("sp_xpl", [128, CI * T], BF16).ap()
    sp_zsh = nc.dram_tensor("sp_zsh", [128, CI * T], BF16).ap()
    sp_zsl = nc.dram_tensor("sp_zsl", [128, CI * T], BF16).ap()
    sp_ffn = nc.dram_tensor("sp_ffn", [T, D], F32).ap()
    cc_in = nc.dram_tensor("cc_in", [CB, 128, 257], F32)
    cc_out = nc.dram_tensor("cc_out", [n_shards, CB, 128, 257], F32)
    dbg_t = nc.dram_tensor("dbg", [16, 128, 512], F32).ap()
    ar_in = nc.dram_tensor("ar_in", [1, 16], F32)
    ar_out = nc.dram_tensor("ar_out", [1, 16], F32)
    groups = [list(range(b * n_shards, (b + 1) * n_shards))
              for b in range(n_batches)]

    S = Sched()
    es = ExitStack()
    pool_h = es.enter_context(nc.sbuf_tensor("pool", [128, part_bytes], U8))
    pool = pool_h[:, :]
    psF = [es.enter_context(nc.psum_tensor(f"ps{i}", [128, 512], F32))
           for i in range(6)]
    psT = es.enter_context(nc.psum_tensor("psT", [128, 1024], BF16))
    psX = es.enter_context(nc.psum_tensor("psX", [128, 512], F32))
    for s in SEMS:
        S.handles[s] = es.enter_context(nc.semaphore(s))

    A = Arena(part_bytes)
    _ldpool = ["LD", "LDA", "LDB", "LDC"]
    _ldi = [0]
    _lspool = ["LS", "LSA"]
    _lsi = [0]

    def newld():
        s = _ldpool[_ldi[0] % 4]
        _ldi[0] += 1
        for q in ("sync", "gpsimd"):
            S.wait(q, s)
        return s

    def newls():
        s = _lspool[_lsi[0] % 2]
        _lsi[0] += 1
        S.wait("scalar", s)
        return s

    def ap(off, n, dt, p0=0, p=128):
        sz = 2 if dt == BF16 else 4
        return pool[p0:p0 + p, off:off + n * sz].bitcast(dt)

    def sb(n):
        return A.alloc(n, "left")
    o_idb = sb(256); o_idf = sb(512); o_mkn = sb(512)
    o_o8 = sb(512); o_o128 = sb(64); o_mci = sb(CI * 4); o_sel = sb(16)
    o_cvw = sb(CI * KCV * 4); o_cvb = sb(CI * 4); o_gnw = sb(CI * 4)
    o_gnb = sb(CI * 4); o_dp = sb(CI * 4)
    o_mur = sb(CD * 4); o_muk = sb(CD * 4); o_mur1 = sb(CD * 4); o_muk1 = sb(CD * 4)
    o_bdt = sb(64); o_loga = sb(64); o_aneg = sb(64)
    o_eppf = sb(max(NCH, 16) * 4)
    o_atn = sb(NCH * 32); o_dtt = sb(NCH * 32); o_dwt = sb(NCH * 32)
    o_dtc = sb(CB * NCH * 4); o_epc = sb(NH * NCH * 4); o_dsh = sb(CB * 4)
    o_stc = sb(2 * CI * 4)
    o_tiny = sb(128 * 4)
    o_one8 = sb(64); o_eps8 = sb(64)

    c_idb = ap(o_idb, 128, BF16); c_idf = ap(o_idf, 128, F32)
    c_mkn = ap(o_mkn, 128, F32)
    c_o8 = ap(o_o8, 128, F32, p=8); c_o128 = ap(o_o128, 1, F32)
    c_mci = ap(o_mci, CI, F32, p=8); c_sel = ap(o_sel, 4, F32)
    c_cvb = ap(o_cvb, CI, F32)
    c_gnw = ap(o_gnw, CI, F32); c_gnb = ap(o_gnb, CI, F32)
    c_mur = ap(o_mur, CD, F32); c_muk = ap(o_muk, CD, F32)
    c_mur1 = ap(o_mur1, CD, F32); c_muk1 = ap(o_muk1, CD, F32)
    c_bdt = ap(o_bdt, 1, F32, p=8); c_loga = ap(o_loga, 1, F32, p=8)
    c_aneg = ap(o_aneg, 1, F32, p=8)
    t_eppf = ap(o_eppf, NCH, F32, p=8)
    t_atn = ap(o_atn, NCH * 8, F32)
    t_dtt = ap(o_dtt, NCH * 8, F32)
    t_dwt = ap(o_dwt, NCH * 8, F32)
    t_dtc = ap(o_dtc, CB * NCH, F32)
    t_epc = ap(o_epc, NH * NCH, F32)
    t_dsh = ap(o_dsh, CB, F32)
    t_stc = ap(o_stc, 2 * CI, F32)
    t_tiny = ap(o_tiny, 128, F32)
    c_one8 = ap(o_one8, 1, F32, p=8); c_eps8 = ap(o_eps8, 1, F32, p=8)

    def mm3(eng_psum, lh, ll, rh, rl, kslices, out_sl, extra_first=False):
        seq = []
        for wop, aop in ((lh, rh), (lh, rl), (ll, rh)):
            for ks in kslices:
                seq.append((wop, aop, ks))
        return seq

    _c0par = newld()
    for dst, src in [
        (c_idb, didb), (c_idf, didf), (c_mkn, dmaskn),
        (c_o8, do8), (c_o128, do128), (c_mci, dmci),
        (c_sel, dsel), (ap(o_cvw, CI * KCV, F32), dconvw), (c_cvb, dconvb),
        (c_gnw, dgnw), (c_gnb, dgnb), (ap(o_dp, CI, F32), ddp),
        (c_mur, dmur), (c_muk, dmuk), (c_bdt, dbdt), (c_loga, dloga),
    ]:
        S.dma("sync", (lambda e, d=dst, s=src: e.dma_start(out=d, in_=s[:, :])), _c0par)
    S.wait("vector", _c0par)
    S.op("vector", lambda e: e.tensor_scalar(c_mur1, c_mur, -1.0, 1.0, ALU.mult, ALU.add))
    S.op("vector", lambda e: e.tensor_scalar(c_muk1, c_muk, -1.0, 1.0, ALU.mult, ALU.add))
    S.op("vector", lambda e: e.memset(c_one8, 1.0))
    S.op("vector", lambda e: e.memset(c_eps8, EPS), inc="VE")

    A.mark("right")
    o_xnh = A.alloc(CD * TH * 2, "right")
    o_xnl = A.alloc(CD * TH * 2, "right")
    t_xnh = ap(o_xnh, CD * TH, BF16)
    t_xnl = ap(o_xnl, CD * TH, BF16)
    A.mark("right")
    o_xt = A.alloc(2 * D * 4, "right")
    o_xsq = A.alloc(D * 4, "right")
    o_lnc = A.alloc(8 * 4, "right")
    o_xnth = A.alloc(D * 2, "right")
    o_xntl = A.alloc(D * 2, "right")

    nrows = TH
    rchunks = [(i * 128, min(128, nrows - i * 128)) for i in range(-(-nrows // 128))]
    ve_rows = []
    for r, (row0, rn) in enumerate(rchunks):
        xt = ap(o_xt + (r % 2) * D * 4, D, F32, p=rn)
        xsq = ap(o_xsq, D, F32, p=rn)
        xnth = ap(o_xnth, D, BF16, p=rn)
        xntl = ap(o_xntl, D, BF16, p=rn)

        def lc(i, p=rn):
            return ap(o_lnc + i * 4, 1, F32, p=p)
        par = newld()
        if r >= 2:
            S.wait("sync", "VE", ve_rows[r - 2])
        if r == 0:
            S.dma("sync", lambda e: e.dma_start(
                out=pool[0:3, o_xt:o_xt + D * 4].bitcast(F32), in_=dxh[:, :]), par)
            S.dma("sync", lambda e, b=min(125, T): e.dma_start(
                out=pool[3:3 + b, o_xt:o_xt + D * 4].bitcast(F32), in_=dx[0:b, :]), par)
        else:
            S.dma("sync", lambda e, d=xt, a=row0 - 3, b=rn: e.dma_start(
                out=d, in_=dx[a:a + b, :]), par)
        S.wait("vector", par)
        S.wait("scalar", par)
        S.op("vector", lambda e, xt=xt, c=lc(0): e.reduce_sum(c, xt, axis=AX.X))
        S.op("vector", lambda e, xt=xt, xs=xsq: e.tensor_mul(xs, xt, xt))
        S.edge("vector")
        S.op("vector", lambda e, xs=xsq, c=lc(1): e.reduce_sum(c, xs, axis=AX.X))
        S.edge("vector")
        S.op("vector", lambda e, c=lc(0), o=lc(2): e.tensor_scalar(o, c, -1.0 / D, None, ALU.mult))
        S.edge("vector")
        S.op("vector", lambda e, c=lc(2), o=lc(3): e.tensor_mul(o, c, c))
        S.edge("vector")
        S.op("vector", lambda e, c=lc(3), o=lc(4): e.tensor_scalar(o, c, -1.0, EPS, ALU.mult, ALU.add),
             inc="VE")
        S.wait("scalar", "VE")
        S.op("scalar", lambda e, c=lc(1), b=lc(4), o=lc(6): e.activation(
            out=o, in_=c, func=AF.Sqrt, bias=b, scale=1.0 / D), inc="SE")
        S.wait("vector", "SE")
        S.op("vector", lambda e, s=lc(6), o=lc(5): e.reciprocal(o, s))
        S.edge("vector")
        S.op("vector", lambda e, xt=xt, nm=lc(2), rs=lc(5): e.tensor_scalar(
            xt, xt, nm, rs, ALU.add, ALU.mult))
        S.edge("vector")
        S.op("vector", lambda e, xt=xt, h=xnth: e.tensor_copy(out=h, in_=xt))
        S.edge("vector")
        S.op("vector", lambda e, xt=xt, h=xnth, l=xntl: e.tensor_sub(l, xt, h), inc="VE")
        ve_rows.append(S.now("VE"))
        S.wait("tensor", "VE")
        for src, dstt in ((xnth, t_xnh), (xntl, t_xnl)):
            for cc in range(CD):
                S.op("tensor", lambda e, s=src, cc=cc, rn=rn: e.transpose(
                    psT[:, cc * 128:cc * 128 + rn], s[:, cc * 128:(cc + 1) * 128],
                    c_idb[0:rn, 0:rn]),
                     inc="TE" if cc == CD - 1 else None)
            S.wait("scalar", "TE")
            for cc in range(CD):
                S.op("scalar", lambda e, cc=cc, d=dstt, row0=row0, rn=rn: e.copy(
                    out=d[:, cc * TH + row0:cc * TH + row0 + rn],
                    in_=psT[:, cc * 128:cc * 128 + rn]),
                     inc="SE" if cc == CD - 1 else None)
            S.wait("tensor", "SE")
    S.wait("gpsimd", "SE")
    S.dma("gpsimd", lambda e: e.dma_start(out=sp_xnh, in_=t_xnh), "SP")
    S.dma("gpsimd", lambda e: e.dma_start(out=sp_xnl, in_=t_xnl), "SP")
    S.barrier()
    A.release("right")

    A.mark("left")
    A.mark("right")
    o_xst = A.alloc(2 * 2 * T * 2, "right")
    o_ws = A.alloc(2 * CD * 512, "right")
    o_wh = A.alloc(2 * CD * 256, "right")
    o_wl = A.alloc(2 * CD * 256, "right")
    o_xpr = A.alloc(2 * TH * 4, "right")
    o_cv = A.alloc(2 * T * 4, "right")
    o_sil = A.alloc(2 * T * 4, "right")
    o_cvt = A.alloc(4 * T * 4, "right")
    wtiles = _tiles(TH, 512)
    te_oc = []
    sp_oc = []
    ge_oc = []
    se_oc = []
    ve_sub = []
    ve_nt = {}
    g1i = [0]
    for oc in range(CI):
        ws = ap(o_ws + (oc % 2) * CD * 512, CD * 128, F32)
        wh = ap(o_wh + (oc % 2) * CD * 256, CD * 128, BF16)
        wl = ap(o_wl + (oc % 2) * CD * 256, CD * 128, BF16)
        xpr = ap(o_xpr + (oc % 2) * TH * 4, TH, F32)
        cv = ap(o_cv + (oc % 2) * T * 4, T, F32)
        sil = ap(o_sil + (oc % 2) * T * 4, T, F32)
        par = newld()
        if oc >= 2:
            S.wait("sync", "TE", te_oc[oc - 2])
        for cc in range(CD):
            S.dma("sync", lambda e, oc=oc, cc=cc, d=ws: e.dma_start(
                out=d[:, cc * 128:(cc + 1) * 128], in_=dwx[oc, cc]), par)
        S.wait("vector", par)
        if oc >= 2:
            S.wait("vector", "TE", te_oc[oc - 2])
        S.op("vector", lambda e, s=ws, h=wh: e.tensor_copy(out=h, in_=s))
        S.edge("vector")
        S.op("vector", lambda e, s=ws, h=wh, l=wl: e.tensor_sub(l, s, h), inc="VE")
        S.wait("tensor", "VE")
        for nt, (off, w) in enumerate(wtiles):
            g1 = g1i[0]; g1i[0] += 1
            pb = psF[g1 % 4]
            if g1 >= 4:
                S.wait("tensor", "VE", ve_nt[g1 - 4])
            seq = [(a, b, cc) for a, b in ((wh, t_xnh), (wh, t_xnl), (wl, t_xnh))
                   for cc in range(CD)]
            for i, (wop, aop, cc) in enumerate(seq):
                S.op("tensor", lambda e, p=pb, wop=wop, aop=aop, cc=cc, off=off, w=w,
                     fs=(i == 0), ls=(i == len(seq) - 1): e.matmul(
                         p[:, 0:w], wop[:, cc * 128:(cc + 1) * 128],
                         aop[:, cc * TH + off:cc * TH + off + w], start=fs, stop=ls),
                     inc="TE" if i == len(seq) - 1 else None)
            S.wait("vector", "TE")
            if nt == 0 and oc >= 2:
                S.wait("vector", "GE", ge_oc[oc - 2])
                S.wait("vector", "SE", se_oc[oc - 2])
            S.op("vector", lambda e, p=pb, off=off, w=w, x=xpr: e.tensor_copy(
                out=x[:, off:off + w], in_=p[:, 0:w]), inc="VE")
            ve_nt[g1] = S.now("VE")
        te_oc.append(S.now("TE"))
        tmp1 = ap(o_cvt + (oc % 2) * 2 * T * 4, T, F32)
        tmp2 = ap(o_cvt + (oc % 2) * 2 * T * 4 + T * 4, T, F32)
        wc_ = [ap(o_cvw + (oc * KCV + k) * 4, 1, F32) for k in range(KCV)]
        S.wait("scalar", "VE", ve_nt[g1i[0] - 1])
        if oc >= 2:
            S.wait("scalar", "GE", ge_oc[oc - 2])
            S.wait("scalar", "VE", ve_sub[oc - 2])
        S.op("scalar", lambda e, x=xpr, o=tmp1, wc=wc_[0]: e.activation(
            out=o, in_=x[:, 0:T], func=AF.Identity, scale=wc))
        S.edge("scalar")
        S.op("scalar", lambda e, x=xpr, o=tmp2, wc=wc_[1]: e.activation(
            out=o, in_=x[:, 1:1 + T], func=AF.Identity, scale=wc), inc="SE")
        S.wait("gpsimd", "SE")
        S.op("gpsimd", lambda e, c=cv, a=tmp1, b=tmp2: e.tensor_add(c, a, b), inc="GE")
        S.wait("scalar", "GE")
        S.op("scalar", lambda e, x=xpr, o=tmp1, wc=wc_[2]: e.activation(
            out=o, in_=x[:, 2:2 + T], func=AF.Identity, scale=wc))
        S.edge("scalar")
        S.op("scalar", lambda e, x=xpr, o=tmp2, wc=wc_[3]: e.activation(
            out=o, in_=x[:, 3:3 + T], func=AF.Identity, scale=wc), inc="SE")
        S.wait("gpsimd", "SE")
        S.op("gpsimd", lambda e, c=cv, a=tmp1: e.tensor_add(c, c, a))
        S.edge("gpsimd")
        S.op("gpsimd", lambda e, c=cv, b=tmp2: e.tensor_add(c, c, b), inc="GE")
        S.wait("scalar", "GE")
        ge_oc.append(S.now("GE"))
        S.op("scalar", lambda e, c=cv, s=sil, b=ap(o_cvb + oc * 4, 1, F32): e.activation(
            out=s, in_=c, func=AF.Silu, bias=b, scale=1.0))
        S.edge("scalar")
        xsh = ap(o_xst + (oc % 2) * 2 * T * 2, T, BF16)
        xsl = ap(o_xst + (oc % 2) * 2 * T * 2 + T * 2, T, BF16)
        if oc >= 2:
            S.wait("scalar", "SP", sp_oc[oc - 2])
        S.op("scalar", lambda e, s=sil, d=xsh: e.copy(out=d, in_=s), inc="SE")
        S.wait("vector", "SE")
        S.op("vector", lambda e, s=sil, h=xsh, l=xsl: e.tensor_sub(l, s, h), inc="VE")
        ve_sub.append(S.now("VE"))
        se_oc.append(S.now("SE"))
        S.wait("gpsimd", "VE")
        S.dma("gpsimd", lambda e, oc=oc, h=xsh: e.dma_start(
            out=sp_xph[:, oc * T:(oc + 1) * T], in_=h), "SP")
        S.dma("gpsimd", lambda e, oc=oc, l=xsl: e.dma_start(
            out=sp_xpl[:, oc * T:(oc + 1) * T], in_=l), "SP")
        sp_oc.append(S.now("SP"))
    S.barrier()
    A.release("right")
    S.barrier()
    A.release("right")

    o_selp = sb(CB * 512); o_selh = sb(NH * 512)
    o_st = sb(CB * 1024)
    o_sth = sb(CB * 512); o_stl = sb(CB * 512)
    o_parts = sb(2 * NCH * CI * 4)
    o_expdA = sb(T * 4)
    o_A = sb((T + 4) * 4)
    c_selp = ap(o_selp, CB * 128, F32, p=8)
    c_selh = ap(o_selh, NH * 128, F32, p=8)
    t_st = ap(o_st, CB * 256, F32)
    t_sth = ap(o_sth, CB * 256, BF16)
    t_stl = ap(o_stl, CB * 256, BF16)
    t_parts = ap(o_parts, 2 * NCH * CI, F32)
    t_expdA = ap(o_expdA, T, F32, p=8)
    _selpar = newld()
    S.dma("sync", lambda e: e.dma_start(out=c_selp, in_=dselp[:, :]), _selpar)
    S.dma("sync", lambda e: e.dma_start(out=c_selh, in_=dselh[:, :]), _selpar)
    S.wait("vector", _selpar)
    S.op("vector", lambda e: e.memset(t_st, 0.0))
    S.op("vector", lambda e: e.memset(t_parts, 0.0), inc="VE")
    S.wait("tensor", _selpar)
    o_bh = sb(CB * T * 2); o_bl = sb(CB * T * 2)
    o_ch = sb(CB * T * 2); o_cl = sb(CB * T * 2)
    t_bh = ap(o_bh, CB * T, BF16); t_bl = ap(o_bl, CB * T, BF16)
    t_ch = ap(o_ch, CB * T, BF16); t_cl = ap(o_cl, CB * T, BF16)
    t_A = ap(o_A, T + 1, F32, p=8)
    A.mark("right")
    o_wdth = A.alloc(CI * 16, "right"); o_wdtl = A.alloc(CI * 16, "right")
    o_wbh = A.alloc(CB * CI * 256, "right"); o_wbl = A.alloc(CB * CI * 256, "right")
    o_wch = A.alloc(CB * CI * 256, "right"); o_wcl = A.alloc(CB * CI * 256, "right")
    o_ws2 = A.alloc(CI * 512, "right")
    XT2 = min(128, T)
    o_xpt = A.alloc(2 * 2 * CI * XT2 * 2, "right")
    o_dtraw = o_expdA
    wdth = ap(o_wdth, CI * 8, BF16); wdtl = ap(o_wdtl, CI * 8, BF16)
    ws2 = ap(o_ws2, CI * 128, F32)
    t_dtraw = ap(o_dtraw, T, F32, p=8)

    par = newld()
    for cc in range(CI):
        S.dma("sync", lambda e, cc=cc: e.dma_start(
            out=pool[:, o_ws2 + cc * 32:o_ws2 + cc * 32 + 32].bitcast(F32),
            in_=dwdt[cc]), par)
    S.wait("vector", par)
    wdt_s = ap(o_ws2, CI * 8, F32)
    S.op("vector", lambda e: e.tensor_copy(out=wdth, in_=wdt_s))
    S.op("vector", lambda e: e.tensor_sub(wdtl, wdt_s, wdth), inc="VE")
    for dwt_, oh_, ol_ in ((dwb, o_wbh, o_wbl), (dwc, o_wch, o_wcl)):
        for ob in range(CB):
            S.qbar("sync", ("vector",))
            par = newld()
            for cc in range(CI):
                S.dma("sync", lambda e, dw=dwt_, ob=ob, cc=cc: e.dma_start(
                    out=ws2[:, cc * 128:(cc + 1) * 128], in_=dw[ob, cc]), par)
            S.wait("vector", par)
            wh_ = pool[:, oh_ + ob * CI * 256:oh_ + (ob + 1) * CI * 256].bitcast(BF16)
            wl_ = pool[:, ol_ + ob * CI * 256:ol_ + (ob + 1) * CI * 256].bitcast(BF16)
            S.op("vector", lambda e, h=wh_: e.tensor_copy(out=h, in_=ws2))
            S.op("vector", lambda e, h=wh_, l=wl_: e.tensor_sub(l, ws2, h), inc="VE")
    t_wbh = ap(o_wbh, CB * CI * 128, BF16); t_wbl = ap(o_wbl, CB * CI * 128, BF16)
    t_wch = ap(o_wch, CB * CI * 128, BF16); t_wcl = ap(o_wcl, CB * CI * 128, BF16)
    ztiles = _tiles(T, XT2)
    ve_pnt = []
    se_dt = []
    te_tile = []
    for nt, (off, w) in enumerate(ztiles):
        xpth = ap(o_xpt + (nt % 2) * 2 * CI * XT2 * 2, CI * XT2, BF16)
        xptl = ap(o_xpt + (nt % 2) * 2 * CI * XT2 * 2 + CI * XT2 * 2, CI * XT2, BF16)
        if nt >= 2:
            S.wait("sync", "TE", te_tile[nt - 2])
        par = newld()
        for cc in range(CI):
            S.dma("sync", lambda e, cc=cc, off=off, w=w, d=xpth: e.dma_start(
                out=d[:, cc * XT2:cc * XT2 + w],
                in_=sp_xph[:, cc * T + off:cc * T + off + w]), par)
            S.dma("sync", lambda e, cc=cc, off=off, w=w, d=xptl: e.dma_start(
                out=d[:, cc * XT2:cc * XT2 + w],
                in_=sp_xpl[:, cc * T + off:cc * T + off + w]), par)
        S.wait("tensor", par)
        if nt >= 1:
            S.wait("tensor", "SE", se_dt[-1])
        pb = psF[2]
        seq = [(a, b, cc) for a, b in ((wdth, xpth), (wdth, xptl), (wdtl, xpth))
               for cc in range(CI)]
        for i, (wop, aop, cc) in enumerate(seq):
            S.op("tensor", lambda e, p=pb, wop=wop, aop=aop, cc=cc, w=w,
                 fs=(i == 0), ls=(i == len(seq) - 1): e.matmul(
                     p[0:8, 0:w], wop[:, cc * 8:(cc + 1) * 8],
                     aop[:, cc * XT2:cc * XT2 + w], start=fs, stop=ls),
                 inc="TE" if i == len(seq) - 1 else None)
        S.wait("scalar", "TE")
        S.op("scalar", lambda e, p=pb, off=off, w=w: e.activation(
            out=t_dtraw[:, off:off + w], in_=p[0:8, 0:w], func=AF.Identity,
            bias=c_bdt, scale=1.0), inc="SE")
        se_dt.append(S.now("SE"))
        for wsel, (whh, wll, th_, tl_) in enumerate(
                ((t_wbh, t_wbl, t_bh, t_bl), (t_wch, t_wcl, t_ch, t_cl))):
            for ob in range(CB):
                gi = nt * 2 * CB + wsel * CB + ob
                pb = (psF[0], psF[1], psF[3], psF[4])[gi % 4]
                if gi >= 4:
                    S.wait("tensor", "VE", ve_pnt[gi - 4])
                seq = [(a, b, cc) for a, b in ((whh, xpth), (whh, xptl), (wll, xpth))
                       for cc in range(CI)]
                for i, (wop, aop, cc) in enumerate(seq):
                    S.op("tensor", lambda e, p=pb, wop=wop, aop=aop, ob=ob, cc=cc, w=w,
                         fs=(i == 0), ls=(i == len(seq) - 1): e.matmul(
                             p[:, 0:w], wop[:, (ob * CI + cc) * 128:(ob * CI + cc) * 128 + 128],
                             aop[:, cc * XT2:cc * XT2 + w], start=fs, stop=ls),
                         inc="TE" if i == len(seq) - 1 else None)
                S.wait("scalar", "TE")
                S.op("scalar", lambda e, p=pb, d=th_, ob=ob, off=off, w=w: e.copy(
                    out=d[:, ob * T + off:ob * T + off + w], in_=p[:, 0:w]), inc="SE")
                S.wait("vector", "SE")
                S.op("vector", lambda e, p=pb, dh=th_, dl=tl_, ob=ob, off=off, w=w:
                     e.tensor_sub(dl[:, ob * T + off:ob * T + off + w], p[:, 0:w],
                                  dh[:, ob * T + off:ob * T + off + w]), inc="VE")
                ve_pnt.append(S.now("VE"))
        te_tile.append(S.now("TE"))
    S.barrier()
    A.release("right")

    A.mark("right")
    o_dt = A.alloc(T * 4, "right")
    o_8a = A.alloc(T * 4, "right")
    o_8b = A.alloc(T * 4, "right")
    t_dt = ap(o_dt, T, F32, p=8)
    t_8a = ap(o_8a, T, F32, p=8)
    t_8b = ap(o_8b, T, F32, p=8)
    S.op("scalar", lambda e: e.activation(out=t_8b, in_=t_dtraw, func=AF.Abs))
    S.edge("scalar")
    S.op("scalar", lambda e: e.activation(out=t_8b, in_=t_8b, func=AF.Exp, scale=-1.0))
    S.edge("scalar")
    S.op("scalar", lambda e: e.activation(out=t_8b, in_=t_8b, func=AF.Ln, bias=c_one8), inc="SE")
    S.wait("vector", "SE")
    S.op("vector", lambda e: e.tensor_relu(t_dt, t_dtraw))
    S.edge("vector")
    S.op("vector", lambda e: e.tensor_add(t_dt, t_dt, t_8b), inc="VE")
    S.wait("scalar", "VE")
    S.op("scalar", lambda e: e.activation(out=t_tiny[0:8, 8:9], in_=c_loga, func=AF.Exp),
         inc="SE")
    S.wait("vector", "SE")
    S.op("vector", lambda e: e.tensor_scalar(c_aneg, t_tiny[0:8, 8:9], -1.0, None, ALU.mult))
    S.op("vector", lambda e: e.memset(t_8b, 1.0))
    S.op("vector", lambda e: e.memset(t_A[:, 0:1], 0.0))
    S.edge("vector")
    S.op("vector", lambda e: e.tensor_scalar(t_8a, t_dt, c_aneg, None, ALU.mult))
    S.edge("vector")
    S.op("vector", lambda e: e.tensor_tensor_scan(t_A[:, 1:T + 1], t_8b, t_8a, 0.0,
                                                  ALU.mult, ALU.add))
    S.edge("vector")
    aprev = t_A[:, 0:(NCH - 1) * Q + 1:Q].unsqueeze(2).broadcast_to([8, NCH, Q])
    acur3 = t_A[:, 1:T + 1].rearrange("p (c q) -> p c q", q=Q)
    out3a = t_8a.rearrange("p (c q) -> p c q", q=Q)
    S.op("vector", lambda e: e.tensor_tensor(out3a, acur3, aprev, ALU.subtract), inc="VE")
    S.wait("scalar", "VE")
    S.op("scalar", lambda e: e.activation(out=t_expdA, in_=t_8a, func=AF.Exp), inc="SE")
    aend = t_A[:, Q:T + 1:Q].unsqueeze(2).broadcast_to([8, NCH, Q])
    S.wait("vector", "SE")
    S.op("vector", lambda e: e.tensor_tensor(out3a, aend, acur3, ALU.subtract), inc="VE")
    S.wait("scalar", "VE")
    S.op("scalar", lambda e: e.activation(out=t_8a, in_=t_8a, func=AF.Exp), inc="SE")
    S.wait("vector", "SE")
    S.op("vector", lambda e: e.tensor_mul(t_8a, t_8a, t_dt), inc="VE")
    S.wait("scalar", "VE")
    S.op("scalar", lambda e: e.activation(out=t_eppf, in_=t_A[:, 0:(NCH - 1) * Q + 1:Q],
                                          func=AF.Exp), inc="SE")
    S.wait("vector", "SE")
    S.op("vector", lambda e: e.tensor_sub(t_tiny[0:8, 16:16 + NCH], t_A[:, Q:T + 1:Q],
                                          t_A[:, 0:(NCH - 1) * Q + 1:Q]), inc="VE")
    S.wait("scalar", "VE")
    S.op("scalar", lambda e: e.activation(out=t_tiny[0:8, 48:48 + NCH],
                                          in_=t_tiny[0:8, 16:16 + NCH], func=AF.Exp))
    S.edge("scalar")
    S.op("scalar", lambda e: e.activation(out=t_tiny[0:8, 78:79], in_=t_A[:, T:T + 1],
                                          func=AF.Exp), inc="SE")
    S.wait("tensor", "SE")
    for k in range(NCH):
        for srcf, dsto, negf in ((t_A[:, 1 + k * Q:1 + (k + 1) * Q], t_atn, -1.0),
                                 (t_dt[:, k * Q:(k + 1) * Q], t_dtt, 1.0),
                                 (t_8a[:, k * Q:(k + 1) * Q], t_dwt, 1.0)):
            S.op("tensor", lambda e, s=srcf: e.transpose(psX[:, 0:8], s, c_idf[0:8, 0:8]), inc="TE")
            S.wait("vector", "TE")
            S.op("vector", lambda e, d=dsto, k=k, ng=negf: e.tensor_scalar(
                d[:, k * 8:(k + 1) * 8], psX[:, 0:8], ng, None, ALU.mult), inc="VE")
            S.wait("tensor", "VE")
    for cbi in range(CB):
        S.op("tensor", lambda e, cbi=cbi: e.matmul(
            psX[:, 0:NCH], c_selp[:, cbi * 128:(cbi + 1) * 128],
            t_tiny[0:8, 48:48 + NCH], start=True, stop=True), inc="TE")
        S.wait("vector", "TE")
        S.op("vector", lambda e, cbi=cbi: e.tensor_copy(
            out=t_dtc[:, cbi * NCH:(cbi + 1) * NCH], in_=psX[:, 0:NCH]), inc="VE")
        S.wait("tensor", "VE")
        S.op("tensor", lambda e, cbi=cbi: e.matmul(
            psX[:, 0:1], c_selp[:, cbi * 128:(cbi + 1) * 128],
            t_tiny[0:8, 78:79], start=True, stop=True), inc="TE")
        S.wait("vector", "TE")
        S.op("vector", lambda e, cbi=cbi: e.tensor_copy(
            out=t_dsh[:, cbi:cbi + 1], in_=psX[:, 0:1]), inc="VE")
        S.wait("tensor", "VE")
    for h in range(NH):
        S.op("tensor", lambda e, h=h: e.matmul(
            psX[:, 0:NCH], c_selh[:, h * 128:(h + 1) * 128], t_eppf,
            start=True, stop=True), inc="TE")
        S.wait("vector", "TE")
        S.op("vector", lambda e, h=h: e.tensor_copy(
            out=t_epc[:, h * NCH:(h + 1) * NCH], in_=psX[:, 0:NCH]), inc="VE")
        S.wait("tensor", "VE")
    S.barrier()
    A.release("right")

    A.mark("right")
    o_xpc = A.alloc(2 * 2 * CI * Q * 2, "right")
    o_xtk = A.alloc(4 * 1024, "right")
    o_bwt = A.alloc(2 * 2 * 256, "right")
    o_ctk = A.alloc(2 * 2 * 256, "right")
    o_sp = A.alloc(4 * 512, "right")
    o_dp2 = A.alloc(4 * 512, "right")
    o_u2 = A.alloc(4 * 512, "right")
    o_scr = A.alloc(8 * 512, "right")
    o_xrc = A.alloc(CI * Q * 4, "right")
    o_crc = A.alloc(CB * Q * 4, "right")
    o_ymst = A.alloc(2 * CI * Q * 4, "right")

    def xpcb(par, half):
        return (ap(o_xpc + par * 2 * CI * Q * 2, CI * Q, BF16),
                ap(o_xpc + par * 2 * CI * Q * 2 + CI * Q * 2, CI * Q, BF16))

    def ymstb(par):
        return ap(o_ymst + par * CI * Q * 4, CI * Q, F32)

    def xtk_h(j):
        return ap(o_xtk + j * 1024, 256, BF16)

    def xtk_l(j):
        return ap(o_xtk + j * 1024 + 512, 256, BF16)

    def bwt(hl, i):
        return ap(o_bwt + hl * 512 + i * 256, 128, BF16)

    def bwt_all(hl):
        return ap(o_bwt + hl * 512, 256, BF16)

    def ctk(hl, i):
        return ap(o_ctk + hl * 512 + i * 256, 128, BF16)

    def ctk_all(hl):
        return ap(o_ctk + hl * 512, 256, BF16)

    def sp_t(hl, j):
        return ap(o_sp + hl * 1024 + j * 256, 128, BF16)

    def sp_all(hl):
        return ap(o_sp + hl * 1024, 512, BF16)

    def u_t(j):
        return ap(o_u2 + j * 512, 128, F32)

    def dp_t(j):
        return ap(o_dp2 + j * 512, 128, F32)

    scan_marks = {}
    gp_gate = [None]

    import os as _osd
    _HVY = _osd.environ.get("KDBG_SCANBAR")
    _GSP = _osd.environ.get("KDBG_GRPSPLIT")
    _SLVL = int(_osd.environ.get("KDBG_SLVL", 9))
    _S1M = int(_osd.environ.get("KDBG_S1MASK", 15))

    def scan_chunk(k, corr):
        if _HVY:
            S.barrier()
        par = k % 2
        ymst = ymstb(par)
        if corr:
            if k >= 2:
                S.wait("sync", "SP", scan_marks[("wb", k - 2)])
                S.wait("sync", "SE", scan_marks[("sq", k - 2)])
            ld = newld()
            S.dma("sync", lambda e, k=k, d=ymst: e.dma_start(out=d, in_=sp_ym[k]), ld)
            S.wait("vector", ld)
            if k >= 1:
                S.wait("gpsimd", "VE", gp_gate[0][1])
            for hp in range(CB):
                S.op("gpsimd", lambda e, hp=hp, k=k: e.tensor_add(
                    ap(o_crc + hp * Q * 4, Q, F32),
                    t_ch[:, hp * T + k * Q:hp * T + (k + 1) * Q],
                    t_cl[:, hp * T + k * Q:hp * T + (k + 1) * Q]), inc="GE")
        else:
            xph_, xpl_ = xpcb(par, 0)
            if k >= 2:
                m = scan_marks[("xte", k - 2)]
                S.wait("sync", m[0], m[1])
                m = scan_marks[("xve", k - 2)]
                S.wait("sync", m[0], m[1])
                S.wait("vector", "SP", scan_marks[("wb", k - 2)])
            ld = newld()
            for ci in range(CI):
                S.dma("sync", lambda e, ci=ci, k=k, d=xph_: e.dma_start(
                    out=d[:, ci * Q:(ci + 1) * Q],
                    in_=sp_xph[:, ci * T + k * Q:ci * T + (k + 1) * Q]), ld)
                S.dma("sync", lambda e, ci=ci, k=k, d=xpl_: e.dma_start(
                    out=d[:, ci * Q:(ci + 1) * Q],
                    in_=sp_xpl[:, ci * T + k * Q:ci * T + (k + 1) * Q]), ld)
            S.op("vector", lambda e: e.tensor_copy(out=t_sth, in_=t_st))
            S.op("vector", lambda e: e.tensor_sub(t_stl, t_st, t_sth), inc="VE")
            S.wait("gpsimd", ld)
            if k >= 1:
                S.wait("gpsimd", "VE", gp_gate[0][1])
            for ci in range(CI):
                S.op("gpsimd", lambda e, ci=ci: e.tensor_add(
                    ap(o_xrc + ci * Q * 4, Q, F32),
                    xph_[:, ci * Q:(ci + 1) * Q], xpl_[:, ci * Q:(ci + 1) * Q]))
            for hp in range(CB):
                S.op("gpsimd", lambda e, hp=hp, k=k: e.tensor_add(
                    ap(o_crc + hp * Q * 4, Q, F32),
                    t_ch[:, hp * T + k * Q:hp * T + (k + 1) * Q],
                    t_cl[:, hp * T + k * Q:hp * T + (k + 1) * Q]), inc="GE")
        for half in range(2):
            if _HVY:
                S.barrier()
            hps = (2 * half, 2 * half + 1)
            heads = [4 * half + j for j in range(4)]
            if not corr:
                S.wait("tensor", ld)
            S.wait("tensor", "VE")
            S.wait("tensor", "SE")
            for i, hp in (enumerate(hps) if (_S1M & 1) else []):
                S.op("tensor", lambda e, hp=hp, k=k, i=i: e.matmul(
                    psF[0][:, i * 128:(i + 1) * 128],
                    c_selp[:, hp * 128:(hp + 1) * 128],
                    t_expdA[:, k * Q:(k + 1) * Q],
                    start=True if _GSP else (i == 0),
                    stop=True if _GSP else (i == 1)))
            if not corr:
                for i, hp in enumerate(hps):
                    S.op("tensor", lambda e, hp=hp, k=k, i=i: e.transpose(
                        psT[:, i * 256:i * 256 + 128],
                        t_bh[:, hp * T + k * Q:hp * T + (k + 1) * Q], c_idb))
                    S.op("tensor", lambda e, hp=hp, k=k, i=i: e.transpose(
                        psT[:, i * 256 + 128:i * 256 + 256],
                        t_bl[:, hp * T + k * Q:hp * T + (k + 1) * Q], c_idb))
                g_seq = [(t_bh, t_ch), (t_bh, t_cl), (t_bl, t_ch)]
                for half_j, jset in ((0, (0, 2)), (1, (1, 3))):
                    nmm = 0
                    for j in jset:
                        h = heads[j]
                        hp, r0 = h // 2, (h % 2) * 64
                        for gg, (ph, pr) in enumerate(g_seq):
                            S.op("tensor", lambda e, ph=ph, pr=pr, hp=hp, r0=r0, k=k, j=j,
                                 fs=(nmm == 0), ls=(nmm == 5): e.matmul(
                                     psF[2][:, j * 128:(j + 1) * 128],
                                     ph[r0:r0 + 64, hp * T + k * Q:hp * T + (k + 1) * Q],
                                     pr[r0:r0 + 64, hp * T + k * Q:hp * T + (k + 1) * Q],
                                     start=fs, stop=ls))
                            nmm += 1
                    if half_j == 0:
                        for j, h in enumerate(heads):
                            S.op("tensor", lambda e, h=h, k=k, j=j: e.matmul(
                                psF[5][:, j * 128:(j + 1) * 128],
                                c_selh[:, h * 128:(h + 1) * 128],
                                t_A[0:8, 1 + k * Q:1 + (k + 1) * Q],
                                start=(j == 0), stop=(j == 3)))
                S.op("tensor", lambda e: e.matmul(
                    psX[:, 0:NCH], c_selp[:, 0:128], t_eppf, start=True, stop=True),
                     inc="TE")
                m_s1 = S.now("TE")
                if _SLVL < 2:
                    continue
                S.wait("vector", "TE", m_s1)
                bwf = ap(o_scr + 1024, 256, F32)
                bwtmp = ap(o_scr + 2048, 256, F32)
                psT3h = psT[:, 0:512].rearrange("p (i t q) -> p i t q", i=2, t=2)
                S.op("vector", lambda e: e.tensor_copy(
                    out=bwtmp, in_=psT3h[:, :, 0, :]))
                S.op("vector", lambda e: e.tensor_add(
                    bwf, bwtmp, psT3h[:, :, 1, :]))
                dwt_bc = t_dwt[:, k * 8 + 4 * half:k * 8 + 4 * half + 4].unsqueeze(2).broadcast_to([128, 4, 64])
                S.op("vector", lambda e: e.tensor_tensor(
                    bwf.rearrange("p (c u) -> p c u", u=64), bwf.rearrange(
                        "p (c u) -> p c u", u=64), dwt_bc, ALU.mult))
                S.op("vector", lambda e: e.tensor_copy(out=bwt_all(0), in_=bwf))
                S.op("vector", lambda e: e.tensor_sub(bwt_all(1), bwf, bwt_all(0)),
                     inc="VE")
                m_bw = S.now("VE")
                S.wait("tensor", "VE", m_bw)
                for j in (0, 1):
                    h = heads[j]
                    for hl, src in ((0, xph_), (1, xpl_)):
                        for pc in range(2):
                            ci = h * 2 + pc
                            S.op("tensor", lambda e, s=src, ci=ci, j=j, hl=hl, pc=pc:
                                 e.transpose(psT[:, j * 512 + hl * 256 + pc * 128:
                                                 j * 512 + hl * 256 + pc * 128 + 128],
                                             s[:, ci * Q:(ci + 1) * Q], c_idb),
                                 inc="TE" if (j == 1 and hl == 1 and pc == 1) else None)
                m_w1 = S.now("TE")
                if _SLVL < 3:
                    continue
                S.wait("vector", "GE")
                crec = ap(o_scr, 256, F32)
                S.op("vector", lambda e, hp0=hps[0]: e.tensor_mul(
                    crec, ap(o_crc + hp0 * Q * 4, 2 * Q, F32), psF[0][:, 0:256]))
                S.op("vector", lambda e: e.tensor_copy(out=ctk_all(0), in_=crec))
                S.op("vector", lambda e: e.tensor_sub(ctk_all(1), crec, ctk_all(0)),
                     inc="VE")
                for j, h in enumerate(heads):
                    S.op("vector", lambda e, j=j, k=k, h=h: e.tensor_scalar(
                        u_t(j), psF[5][:, j * 128:(j + 1) * 128],
                        t_atn[:, k * 8 + h:k * 8 + h + 1], 0.0, ALU.add, ALU.min))
                    S.op("vector", lambda e, j=j: e.tensor_add(u_t(j), u_t(j), c_mkn),
                         inc="VE" if j == 3 else None)
                m_u = S.now("VE")
                S.wait("scalar", "VE", m_u)
                for j in range(4):
                    S.op("scalar", lambda e, j=j: e.activation(
                        out=dp_t(j), in_=u_t(j), func=AF.Exp), inc="SE" if j == 3 else None)
                m_dp = S.now("SE")
                S.wait("vector", "SE", m_dp)
                for j, h in enumerate(heads):
                    P1j = u_t(j)
                    S.op("vector", lambda e, P1=P1j, j=j: e.tensor_mul(
                        P1, psF[2][:, j * 128:(j + 1) * 128], dp_t(j)))
                    S.op("vector", lambda e, P1=P1j, j=j, k=k, h=h: e.tensor_scalar(
                        sp_t(0, j), P1, t_dtt[:, k * 8 + h:k * 8 + h + 1], None, ALU.mult))
                    S.op("vector", lambda e, P1=P1j, j=j, k=k, h=h: e.scalar_tensor_tensor(
                        sp_t(1, j), P1, t_dtt[:, k * 8 + h:k * 8 + h + 1], sp_t(0, j),
                        ALU.mult, ALU.subtract), inc="VE" if j == 3 else None)
                m_sp = S.now("VE")
                if _SLVL < 4:
                    continue
                S.wait("scalar", "TE", m_w1)
                for j in (0, 1):
                    S.op("scalar", lambda e, j=j: e.copy(
                        out=pool[:, o_xtk + j * 1024:o_xtk + j * 1024 + 1024].bitcast(BF16),
                        in_=psT[:, j * 512:j * 512 + 512]), inc="SE" if j == 1 else None)
                m_xe1 = S.now("SE")
                S.wait("tensor", "SE", m_xe1)
                for j in (2, 3):
                    h = heads[j]
                    for hl, src in ((0, xph_), (1, xpl_)):
                        for pc in range(2):
                            ci = h * 2 + pc
                            S.op("tensor", lambda e, s=src, ci=ci, j=j, hl=hl, pc=pc:
                                 e.transpose(psT[:, (j - 2) * 512 + hl * 256 + pc * 128:
                                                 (j - 2) * 512 + hl * 256 + pc * 128 + 128],
                                             s[:, ci * Q:(ci + 1) * Q], c_idb),
                                 inc="TE" if (j == 3 and hl == 1 and pc == 1) else None)
                m_w2 = S.now("TE")
                S.wait("scalar", "TE", m_w2)
                for j in (2, 3):
                    S.op("scalar", lambda e, j=j: e.copy(
                        out=pool[:, o_xtk + j * 1024:o_xtk + j * 1024 + 1024].bitcast(BF16),
                        in_=psT[:, (j - 2) * 512:(j - 2) * 512 + 512]),
                         inc="SE" if j == 3 else None)
                m_xe2 = S.now("SE")
                if _SLVL < 5:
                    continue
                S.wait("tensor", "VE", m_sp)
                S.wait("tensor", "SE", m_xe2)
                if k > 0 or half > 0 or True:
                    S.wait("tensor", "VE")
                for j, h in enumerate(heads):
                    hp, r0 = h // 2, (h % 2) * 64
                    bank = psF[3] if j < 2 else psF[1]
                    c0 = (j % 2) * 256
                    nmm = 0
                    for pc in range(2):
                        for lh, rh in ((xtk_h(j), sp_t(0, j)), (xtk_h(j), sp_t(1, j)),
                                       (xtk_l(j), sp_t(0, j))):
                            S.op("tensor", lambda e, lh=lh, rh=rh, pc=pc, c0=c0, bank=bank,
                                 fs=(nmm == 0), ls=False: e.matmul(
                                     bank[:, c0 + pc * 128:c0 + pc * 128 + 128],
                                     lh[:, pc * 128:pc * 128 + 128], rh,
                                     start=fs, stop=False))
                            nmm += 1
                    i = hp - 2 * half
                    for pc in range(2):
                        for lh, rh in ((t_sth, ctk(0, i)), (t_sth, ctk(1, i)),
                                       (t_stl, ctk(0, i))):
                            nmm += 1
                            S.op("tensor", lambda e, lh=lh, rh=rh, pc=pc, hp=hp, r0=r0,
                                 c0=c0, bank=bank, i=i, ls=(nmm == 12): e.matmul(
                                     bank[:, c0 + pc * 128:c0 + pc * 128 + 128],
                                     lh[r0:r0 + 64, hp * 256 + pc * 128:hp * 256 + pc * 128 + 128],
                                     rh[r0:r0 + 64, :], start=False, stop=ls),
                                 inc="TE" if (j == 3 and nmm == 12) else None)
                for i, hp in enumerate(hps):
                    for hh in range(2):
                        j = 2 * i + hh
                        r0 = hh * 64
                        cs_seq = [(bwt(0, i), xtk_h(j)), (bwt(0, i), xtk_l(j)),
                                  (bwt(1, i), xtk_h(j))]
                        for jj, (lh, rh) in enumerate(cs_seq):
                            S.op("tensor", lambda e, lh=lh, rh=rh, i=i, r0=r0, hh=hh,
                                 fs=(jj == 0), ls=(jj == 2): e.matmul(
                                     psF[4][r0:r0 + 64, i * 256:i * 256 + 256],
                                     lh[:, hh * 64:hh * 64 + 64], rh,
                                     start=fs, stop=ls),
                                 inc="TE" if (i == 1 and hh == 1 and jj == 2) else None)
                m_s3 = S.now("TE")
                if _SLVL < 6:
                    continue
                S.wait("vector", "TE", m_s3)
                S.wait("vector", "GE")
                for j, h in enumerate(heads):
                    bank = psF[3] if j < 2 else psF[1]
                    c0 = (j % 2) * 256
                    S.op("vector", lambda e, h=h, c0=c0, bank=bank, ymst=ymst:
                         e.scalar_tensor_tensor(
                             ymst[:, h * 256:(h + 1) * 256],
                             ap(o_xrc + h * 2 * Q * 4, 256, F32),
                             ap(o_dp + h * 2 * 4, 1, F32),
                             bank[:, c0:c0 + 256], ALU.mult, ALU.add))
                for i, hp in enumerate(hps):
                    S.op("vector", lambda e, hp=hp, k=k, i=i: e.scalar_tensor_tensor(
                        t_st[:, hp * 256:(hp + 1) * 256], t_st[:, hp * 256:(hp + 1) * 256],
                        t_dtc[:, hp * NCH + k:hp * NCH + k + 1],
                        psF[4][:, i * 256:i * 256 + 256], ALU.mult, ALU.add),
                         inc="VE" if i == 1 else None)
            else:
                m_s1 = S.mark("tensor")
                if m_s1:
                    S.wait("vector", m_s1[0], m_s1[1])
                for i, hp in enumerate(hps):
                    crec = ap(o_scr, 128, F32)
                    S.op("vector", lambda e, hp=hp, k=k, o=crec: e.tensor_add(
                        o, t_ch[:, hp * T + k * Q:hp * T + (k + 1) * Q],
                        t_cl[:, hp * T + k * Q:hp * T + (k + 1) * Q]))
                    S.op("vector", lambda e, o=crec, i=i: e.tensor_mul(
                        o, o, psF[0][:, i * 128:(i + 1) * 128]))
                    S.op("vector", lambda e, o=crec, i=i: e.tensor_copy(out=ctk(0, i), in_=o))
                    S.op("vector", lambda e, o=crec, i=i: e.tensor_sub(ctk(1, i), o, ctk(0, i)),
                         inc="VE" if i == 1 else None)
                m_ct = S.now("VE")
                S.wait("tensor", "VE", m_ct)
                for half_j, jset in ((0, (0, 2)), (1, (1, 3))):
                    for j in jset:
                        h = heads[j]
                        hp, r0 = h // 2, (h % 2) * 64
                        i = hp - 2 * half
                        bank = psF[3] if j < 2 else psF[1]
                        c0 = (j % 2) * 256
                        nmm = 0
                        for pc in range(2):
                            for lh, rh in ((t_sth, ctk(0, i)), (t_sth, ctk(1, i)),
                                           (t_stl, ctk(0, i))):
                                nmm += 1
                                S.op("tensor", lambda e, lh=lh, rh=rh, pc=pc, hp=hp, r0=r0,
                                     c0=c0, bank=bank, fs=(nmm == 1), ls=(nmm == 6):
                                     e.matmul(
                                         bank[:, c0 + pc * 128:c0 + pc * 128 + 128],
                                         lh[r0:r0 + 64, hp * 256 + pc * 128:
                                            hp * 256 + pc * 128 + 128],
                                         rh[r0:r0 + 64, :], start=fs, stop=ls))
                    S.op("tensor", lambda e: e.matmul(
                        psX[:, 0:NCH], c_selp[:, 0:128], t_eppf, start=True, stop=True),
                         inc="TE" if half_j == 1 else None)
                m_s3 = S.now("TE")
                S.wait("vector", "TE", m_s3)
                for j, h in enumerate(heads):
                    bank = psF[3] if j < 2 else psF[1]
                    c0 = (j % 2) * 256
                    for pc in range(2):
                        ci = h * 2 + pc
                        S.op("vector", lambda e, ci=ci, pc=pc, k=k, h=h, c0=c0, bank=bank,
                             ymst=ymst: e.scalar_tensor_tensor(
                                 ymst[:, ci * Q:(ci + 1) * Q],
                                 bank[:, c0 + pc * 128:c0 + pc * 128 + 128],
                                 t_epc[:, h * NCH + k:h * NCH + k + 1],
                                 ymst[:, ci * Q:(ci + 1) * Q], ALU.mult, ALU.add,
                                 accum_out=t_parts[:, k * CI + ci:k * CI + ci + 1]),
                             inc="VE" if (j == 3 and pc == 1) else None)
                m_ev = S.now("VE")
                S.wait("scalar", "VE", m_ev)
                for j, h in enumerate(heads):
                    for pc in range(2):
                        ci = h * 2 + pc
                        S.op("scalar", lambda e, ci=ci, k=k, ymst=ymst: e.activation(
                            out=ap(o_scr + 512, 128, F32), in_=ymst[:, ci * Q:(ci + 1) * Q],
                            func=AF.Square,
                            accum_out=t_parts[:, NCH * CI + k * CI + ci:
                                              NCH * CI + k * CI + ci + 1]),
                             inc="SE" if (j == 3 and pc == 1) else None)
        gp_gate[0] = ("VE", S.now("VE"))
        S.mark("vector")
        S.wait("gpsimd", "VV")
        S.wait("gpsimd", "VE")
        if corr:
            S.mark("scalar")
            S.wait("gpsimd", "SS")
            S.wait("gpsimd", "SE")
            scan_marks[("sq", k)] = S.now("SE")
        S.dma("gpsimd", lambda e, k=k, d=ymst: e.dma_start(out=sp_ym[k], in_=d), "SP")
        scan_marks[("wb", k)] = S.now("SP")
        if not corr:
            m = S.mark("tensor")
            scan_marks[("xte", k)] = m
            scan_marks[("xve", k)] = ("VE", S.now("VE"))


    _NLIM = int(_osd.environ.get("KDBG_NCHLIM", NCH))
    for k in range(min(NCH, _NLIM)):
        scan_chunk(k, corr=False)

    S.wait("gpsimd", "VE")
    for hp in range(CB):
        S.dma("gpsimd", lambda e, hp=hp: e.dma_start(
            out=cc_in.ap()[hp, :, 0:256], in_=t_st[:, hp * 256:(hp + 1) * 256]), "SP")
        S.dma("gpsimd", lambda e, hp=hp: e.dma_start(
            out=cc_in.ap()[hp, :, 256:257], in_=t_dsh[:, hp:hp + 1],
            allow_slow_non_contiguous=True), "SP")
    S.wait("gpsimd", "SP")
    if n_shards > 1:
        S.op("gpsimd", lambda e: e.collective_compute(
            "AllGather", ALU.bypass, replica_groups=groups,
            ins=[cc_in.ap().opt()], outs=[cc_out.ap().opt()]), inc="CC")
        S.wait("gpsimd", "CC")
    else:
        S.dma("gpsimd", lambda e: e.dma_start(out=cc_out.ap()[0], in_=cc_in.ap()[:, :, :]),
              "SP")
        S.wait("gpsimd", "SP")
    A.mark("right")
    o_gth = A.alloc(n_shards * CB * 257 * 4, "right")
    o_pfx = A.alloc(2 * CB * 1024, "right")
    t_gth = ap(o_gth, n_shards * CB * 257, F32)
    for m_ in range(n_shards):
        for hp_ in range(CB):
            S.dma("gpsimd", lambda e, m_=m_, hp_=hp_: e.dma_start(
                out=t_gth[:, (m_ * CB + hp_) * 257:(m_ * CB + hp_ + 1) * 257],
                in_=cc_out.ap()[m_, hp_]), "SP")
    S.wait("vector", "SP")

    def gs(m, hp):
        return ap(o_gth + (m * CB + hp) * 257 * 4, 256, F32)

    def gd(m, hp):
        return ap(o_gth + ((m * CB + hp) * 257 + 256) * 4, 1, F32)

    def pb_(par, hp):
        return ap(o_pfx + (par * CB + hp) * 1024, 256, F32)

    for hp in range(CB):
        S.op("vector", lambda e, hp=hp: e.tensor_copy(out=pb_(0, hp), in_=gs(0, hp)))
        S.edge("vector")
        S.op("vector", lambda e, hp=hp: e.tensor_scalar(
            t_st[:, hp * 256:(hp + 1) * 256], pb_(0, hp), c_sel[:, 0:1], None, ALU.mult))
        S.edge("vector")
        for m in range(1, n_shards):
            S.op("vector", lambda e, hp=hp, m=m: e.scalar_tensor_tensor(
                pb_(m % 2, hp), pb_((m - 1) % 2, hp), gd(m, hp), gs(m, hp),
                ALU.mult, ALU.add))
            S.edge("vector")
            S.op("vector", lambda e, hp=hp, m=m: e.scalar_tensor_tensor(
                t_st[:, hp * 256:(hp + 1) * 256], pb_(m % 2, hp), c_sel[:, m:m + 1],
                t_st[:, hp * 256:(hp + 1) * 256], ALU.mult, ALU.add))
            S.edge("vector")
    S.op("vector", lambda e: e.tensor_copy(out=t_sth, in_=t_st))
    S.edge("vector")
    S.op("vector", lambda e: e.tensor_sub(t_stl, t_st, t_sth), inc="VE")
    A.release("right")

    S.barrier()
    import os as _os
    if not _os.environ.get("KDBG_NO_CORR"):
        for k in range(min(NCH, _NLIM)):
            scan_chunk(k, corr=True)
    S.barrier()
    A.release("right")

    A.mark("right")
    pview = t_parts.rearrange("p (g k c) -> p g k c", g=2, c=CI)
    for g in range(2):
        for h in range(NH):
            S.op("vector", lambda e, g=g, h=h: e.reduce_sum(
                t_tiny[:, 32 + g * 8 + h:32 + g * 8 + h + 1],
                pview[:, g, :, 2 * h:2 * h + 2], axis=AX.XY),
                 inc="VE" if (g == 1 and h == NH - 1) else None)
    S.wait("tensor", "VE")
    S.op("tensor", lambda e: e.matmul(psX[0:1, 0:16], c_o128, t_tiny[:, 32:48],
                                      start=True, stop=True), inc="TE")
    S.wait("vector", "TE")
    S.op("vector", lambda e: e.tensor_copy(out=t_tiny[0:1, 0:16], in_=psX[0:1, 0:16]),
         inc="VE")
    S.wait("gpsimd", "VE")
    S.dma("gpsimd", lambda e: e.dma_start(out=ar_in.ap()[:, :], in_=t_tiny[0:1, 0:16]), "SP")
    S.wait("gpsimd", "SP")
    if n_shards > 1:
        S.op("gpsimd", lambda e: e.collective_compute(
            "AllReduce", ALU.add, replica_groups=groups,
            ins=[ar_in.ap().opt()], outs=[ar_out.ap().opt()]), inc="CC")
        S.wait("gpsimd", "CC")
    else:
        S.dma("gpsimd", lambda e: e.dma_start(out=ar_out.ap()[:, :], in_=ar_in.ap()[:, :]),
              "SP")
        S.wait("gpsimd", "SP")
    S.dma("gpsimd", lambda e: e.dma_start(out=t_tiny[0:1, 16:32], in_=ar_out.ap()[:, :]),
          "SP")
    S.wait("tensor", "SP")
    S.op("tensor", lambda e: e.transpose(psX[0:8, 0:1], t_tiny[0:1, 16:24], c_idf[0:1, 0:1]))
    S.op("tensor", lambda e: e.transpose(psX[0:8, 1:2], t_tiny[0:1, 24:32], c_idf[0:1, 0:1]),
         inc="TE")
    S.wait("vector", "TE")
    S.op("vector", lambda e: e.tensor_scalar(t_tiny[0:8, 48:49], psX[0:8, 0:1],
                                             1.0 / CNT_GN, None, ALU.mult))
    S.op("vector", lambda e: e.tensor_scalar(t_tiny[0:8, 47:48], psX[0:8, 1:2],
                                             1.0 / CNT_GN, None, ALU.mult))
    S.op("vector", lambda e: e.tensor_mul(t_tiny[0:8, 49:50], t_tiny[0:8, 48:49],
                                          t_tiny[0:8, 48:49]))
    S.op("vector", lambda e: e.tensor_sub(t_tiny[0:8, 50:51], t_tiny[0:8, 47:48],
                                          t_tiny[0:8, 49:50]), inc="VE")
    S.wait("scalar", "VE")
    S.op("scalar", lambda e: e.activation(out=t_tiny[0:8, 54:55], in_=t_tiny[0:8, 50:51],
                                          func=AF.Sqrt, bias=c_eps8), inc="SE")
    S.wait("vector", "SE")
    S.op("vector", lambda e: e.reciprocal(t_tiny[0:8, 51:52], t_tiny[0:8, 54:55]))
    S.edge("vector")
    S.op("vector", lambda e: e.tensor_mul(t_tiny[0:8, 52:53], t_tiny[0:8, 48:49],
                                          t_tiny[0:8, 51:52]))
    S.edge("vector")
    S.op("vector", lambda e: e.tensor_scalar(t_tiny[0:8, 53:54], t_tiny[0:8, 52:53],
                                             -1.0, None, ALU.mult))
    S.edge("vector")
    S.op("vector", lambda e: e.tensor_scalar(t_tiny[0:8, 56:56 + CI], c_mci,
                                             t_tiny[0:8, 51:52], None, ALU.mult))
    S.edge("vector")
    S.op("vector", lambda e: e.tensor_scalar(t_tiny[0:8, 56 + CI:56 + 2 * CI], c_mci,
                                             t_tiny[0:8, 53:54], None, ALU.mult), inc="VE")
    S.wait("tensor", "VE")
    S.op("tensor", lambda e: e.matmul(psX[:, 0:CI], c_o8, t_tiny[0:8, 56:56 + CI],
                                      start=True, stop=True))
    S.op("tensor", lambda e: e.matmul(psX[:, CI:2 * CI], c_o8,
                                      t_tiny[0:8, 56 + CI:56 + 2 * CI],
                                      start=True, stop=True), inc="TE")
    S.wait("vector", "TE")
    S.op("vector", lambda e: e.tensor_mul(t_stc[:, 0:CI], psX[:, 0:CI], c_gnw))
    S.edge("vector")
    S.op("vector", lambda e: e.tensor_mul(t_tiny[:, 60:60 + CI], psX[:, CI:2 * CI], c_gnw))
    S.edge("vector")
    S.op("vector", lambda e: e.tensor_add(t_stc[:, CI:2 * CI], t_tiny[:, 60:60 + CI],
                                          c_gnb), inc="VE")
    S.barrier()
    A.release("right")
    A.release("left")

    A.mark("left")
    o_wrh = sb(CD * D * 2); o_wrl = sb(CD * D * 2)
    o_wvh = sb(CD * D * 2); o_wvl = sb(CD * D * 2)
    o_wkh = sb(CD * CD * 256); o_wkl = sb(CD * CD * 256)
    t_wrh = ap(o_wrh, CD * D, BF16); t_wrl = ap(o_wrl, CD * D, BF16)
    t_wvh = ap(o_wvh, CD * D, BF16); t_wvl = ap(o_wvl, CD * D, BF16)
    t_wkh = ap(o_wkh, CD * CD * 128, BF16); t_wkl = ap(o_wkl, CD * CD * 128, BF16)
    A.mark("right")
    o_wst = A.alloc(D * 4, "right")
    o_xnt = A.alloc(2 * CD * 257 * 2, "right")
    o_xnr = A.alloc(CD * 257 * 4, "right")
    o_mx2 = A.alloc(2 * FT * 4, "right")
    o_xrh = A.alloc(CD * FT * 2, "right"); o_xrl = A.alloc(CD * FT * 2, "right")
    o_xkh = A.alloc(CD * FT * 2, "right"); o_xkl = A.alloc(CD * FT * 2, "right")
    o_kq = A.alloc(FT * 4, "right")
    o_kqh = A.alloc(CD * FT * 2, "right"); o_kql = A.alloc(CD * FT * 2, "right")
    o_rt = A.alloc((FT // 128) * D * 4, "right")
    o_ffo = A.alloc(D * 4, "right")
    o_xr2 = A.alloc((FT // 128) * D * 4, "right")
    wst = ap(o_wst, D, F32)

    for (dsrc, th_, tl_, isp) in ((dwr, t_wrh, t_wrl, False), (dwv, t_wvh, t_wvl, False),
                                  (dwk, t_wkh, t_wkl, True)):
        for oc in range(CD):
            S.wait("sync", "VE")
            par = newld()
            if isp:
                for cc in range(CD):
                    S.dma("sync", lambda e, oc=oc, cc=cc, s=dsrc: e.dma_start(
                        out=wst[:, cc * 128:(cc + 1) * 128], in_=s[oc, cc]), par)
            else:
                S.dma("sync", lambda e, oc=oc, s=dsrc: e.dma_start(
                    out=wst, in_=s[oc]), par)
            S.wait("vector", par)
            S.op("vector", lambda e, oc=oc, d=th_: e.tensor_copy(
                out=d[:, oc * D:(oc + 1) * D], in_=wst))
            S.op("vector", lambda e, oc=oc, dh=th_, dl=tl_: e.tensor_sub(
                dl[:, oc * D:(oc + 1) * D], wst, dh[:, oc * D:(oc + 1) * D]), inc="VE")

    nft = T // FT
    ve_gf = []
    gfi = [0]
    for tt in range(nft):
        c0 = 2 + tt * FT
        xnth2 = ap(o_xnt, CD * 257, BF16)
        xntl2 = ap(o_xnt + CD * 257 * 2, CD * 257, BF16)
        xnr = ap(o_xnr, CD * 257, F32)
        xrh = ap(o_xrh, CD * FT, BF16); xrl = ap(o_xrl, CD * FT, BF16)
        xkh = ap(o_xkh, CD * FT, BF16); xkl = ap(o_xkl, CD * FT, BF16)
        kq = ap(o_kq, FT, F32)
        kqh = ap(o_kqh, CD * FT, BF16); kql = ap(o_kql, CD * FT, BF16)
        rt = ap(o_rt, (FT // 128) * D, F32)
        ffo = ap(o_ffo, D, F32)
        S.qbar("sync", ("tensor", "vector"))
        par = newld()
        for cc in range(CD):
            S.dma("sync", lambda e, cc=cc, c0=c0, d=xnth2: e.dma_start(
                out=d[:, cc * 257:(cc + 1) * 257],
                in_=sp_xnh[:, cc * TH + c0:cc * TH + c0 + FT + 1]), par)
            S.dma("sync", lambda e, cc=cc, c0=c0, d=xntl2: e.dma_start(
                out=d[:, cc * 257:(cc + 1) * 257],
                in_=sp_xnl[:, cc * TH + c0:cc * TH + c0 + FT + 1]), par)
        for tb in range(FT // 128):
            S.dma("gpsimd", lambda e, tt=tt, tb=tb: e.dma_start(
                out=ap(o_xr2 + tb * D * 4, D, F32),
                in_=dx[tt * FT + tb * 128:tt * FT + tb * 128 + 128, :]), "SP")
        S.wait("vector", par)
        S.wait("vector", "SP")
        S.op("vector", lambda e: e.tensor_add(xnr, xnth2, xntl2))
        for cc in range(CD):
            for si, (ho, lo_, mu, mu1) in enumerate(((xrh, xrl, o_mur, o_mur1),
                                                     (xkh, xkl, o_muk, o_muk1))):
                mxf = ap(o_mx2 + si * FT * 4, FT, F32)
                S.op("vector", lambda e, cc=cc, d=mxf, mu1=mu1: e.tensor_scalar(
                    d, xnr[:, cc * 257:cc * 257 + FT],
                    ap(mu1 + cc * 4, 1, F32), None, ALU.mult))
                S.op("vector", lambda e, cc=cc, d=mxf, mu=mu: e.scalar_tensor_tensor(
                    d, xnr[:, cc * 257 + 1:cc * 257 + FT + 1],
                    ap(mu + cc * 4, 1, F32), d, ALU.mult, ALU.add))
                S.op("vector", lambda e, cc=cc, d=mxf, h=ho: e.tensor_copy(
                    out=h[:, cc * FT:(cc + 1) * FT], in_=d))
                S.op("vector", lambda e, cc=cc, d=mxf, h=ho, l=lo_: e.tensor_sub(
                    l[:, cc * FT:(cc + 1) * FT], d, h[:, cc * FT:(cc + 1) * FT]),
                     inc="VE" if (cc == CD - 1 and si == 1) else None)
        S.wait("tensor", "VE")
        for oc in range(CD):
            g = gfi[0]; gfi[0] += 1
            pb = psF[g % 4]
            if g >= 4:
                S.wait("tensor", ve_gf[g - 4][0], ve_gf[g - 4][1])
            seq = [(a, b, cc) for a, b in ((t_wkh, xkh), (t_wkh, xkl), (t_wkl, xkh))
                   for cc in range(CD)]
            for i, (wop, aop, cc) in enumerate(seq):
                S.op("tensor", lambda e, p=pb, wop=wop, aop=aop, oc=oc, cc=cc,
                     fs=(i == 0), ls=(i == len(seq) - 1): e.matmul(
                         p[:, 0:FT], wop[:, (oc * CD + cc) * 128:(oc * CD + cc) * 128 + 128],
                         aop[:, cc * FT:(cc + 1) * FT], start=fs, stop=ls),
                     inc="TE" if i == len(seq) - 1 else None)
            S.wait("vector", "TE")
            S.op("vector", lambda e, p=pb: e.tensor_scalar_max(kq, p[:, 0:FT], 0.0),
                 inc="VE")
            ve_gf.append(("VE", S.now("VE")))
            S.wait("scalar", "VE")
            S.op("scalar", lambda e: e.activation(
                out=pool[:, o_kq:o_kq + FT * 4].bitcast(F32), in_=kq, func=AF.Square))
            S.op("scalar", lambda e, oc=oc: e.copy(
                out=kqh[:, oc * FT:(oc + 1) * FT], in_=kq), inc="SE")
            S.wait("vector", "SE")
            S.op("vector", lambda e, oc=oc: e.tensor_sub(
                kql[:, oc * FT:(oc + 1) * FT], kq, kqh[:, oc * FT:(oc + 1) * FT]),
                 inc="VE")
        for tb in range(FT // 128):
            for half in range(2):
                g = gfi[0]; gfi[0] += 1
                pb = psF[g % 4]
                if g >= 4:
                    S.wait("tensor", ve_gf[g - 4][0], ve_gf[g - 4][1])
                seq = [(a, b, cc) for a, b in ((xrh, t_wrh), (xrh, t_wrl), (xrl, t_wrh))
                       for cc in range(CD)]
                for i, (aop, wop, cc) in enumerate(seq):
                    S.op("tensor", lambda e, p=pb, aop=aop, wop=wop, cc=cc, tb=tb,
                         half=half, fs=(i == 0), ls=(i == len(seq) - 1): e.matmul(
                             p[:, 0:512],
                             aop[:, cc * FT + tb * 128:cc * FT + tb * 128 + 128],
                             wop[:, cc * D + half * 512:cc * D + half * 512 + 512],
                             start=fs, stop=ls),
                         inc="TE" if i == len(seq) - 1 else None)
                S.wait("scalar", "TE")
                S.op("scalar", lambda e, p=pb, tb=tb, half=half: e.activation(
                    out=rt[:, tb * D + half * 512:tb * D + half * 512 + 512],
                    in_=p[:, 0:512], func=AF.Sigmoid), inc="SE")
                ve_gf.append(("SE", S.now("SE")))
        S.wait("tensor", "SE")
        S.wait("tensor", "VE")
        for tb in range(FT // 128):
            for half in range(2):
                g = gfi[0]; gfi[0] += 1
                pb = psF[g % 4]
                if g >= 4:
                    S.wait("tensor", ve_gf[g - 4][0], ve_gf[g - 4][1])
                seq = [(a, b, cc) for a, b in ((kqh, t_wvh), (kqh, t_wvl), (kql, t_wvh))
                       for cc in range(CD)]
                for i, (aop, wop, cc) in enumerate(seq):
                    S.op("tensor", lambda e, p=pb, aop=aop, wop=wop, cc=cc, tb=tb,
                         half=half, fs=(i == 0), ls=(i == len(seq) - 1): e.matmul(
                             p[:, 0:512],
                             aop[:, cc * FT + tb * 128:cc * FT + tb * 128 + 128],
                             wop[:, cc * D + half * 512:cc * D + half * 512 + 512],
                             start=fs, stop=ls),
                         inc="TE" if i == len(seq) - 1 else None)
                S.wait("vector", "TE")
                S.op("vector", lambda e, p=pb, tb=tb, half=half: e.tensor_mul(
                    ffo[:, half * 512:half * 512 + 512], p[:, 0:512],
                    rt[:, tb * D + half * 512:tb * D + half * 512 + 512]))
                S.op("vector", lambda e, tb=tb, half=half: e.tensor_add(
                    ffo[:, half * 512:half * 512 + 512],
                    ffo[:, half * 512:half * 512 + 512],
                    ap(o_xr2 + tb * D * 4, D, F32)[:, half * 512:half * 512 + 512]),
                     inc="VE")
                ve_gf.append(("VE", S.now("VE")))
            S.wait("gpsimd", "VE")
            S.dma("gpsimd", lambda e, tt=tt, tb=tb: e.dma_start(
                out=sp_ffn[tt * FT + tb * 128:tt * FT + tb * 128 + 128, :], in_=ffo),
                "SP")
            S.wait("vector", "SP")
    S.barrier()
    A.release("right")
    A.release("left")

    A.mark("right")
    o_wz = A.alloc(CI * CD * 256, "right")
    o_wzl = A.alloc(CI * CD * 256, "right")
    o_wo = A.alloc(CI * D * 2, "right")
    o_wol = A.alloc(CI * D * 2, "right")
    o_wst3 = A.alloc(CD * 512, "right")
    o_wst4 = A.alloc(D * 4, "right")
    o_xnt3 = A.alloc(2 * 2 * CD * 256, "right")
    o_ymt3 = A.alloc(2 * CI * Q * 4, "right")
    o_zt = A.alloc(2 * Q * 4, "right")
    o_yt = A.alloc(2 * Q * 4, "right")
    o_mtg = A.alloc(2 * 2 * CI * 256, "right")
    o_ffr = A.alloc(2 * D * 4, "right")
    t_wzh = ap(o_wz, CI * CD * 128, BF16)
    t_wzl = ap(o_wzl, CI * CD * 128, BF16)
    t_woh = ap(o_wo, CI * D, BF16)
    t_wol = ap(o_wol, CI * D, BF16)
    wst3 = ap(o_wst3, CD * 128, F32)
    wst4 = ap(o_wst4, D, F32)

    wz_ve = {}
    wo_ve = {}
    for i in range(CI):
        S.wait("scalar", "VE")
        parz = newls()
        for cc in range(CD):
            S.dma("scalar", lambda e, oc=i, cc=cc: e.dma_start(
                out=wst3[:, cc * 128:(cc + 1) * 128], in_=dwz[oc, cc]), parz)
        S.wait("vector", parz)
        S.op("vector", lambda e, oc=i: e.tensor_copy(
            out=t_wzh[:, oc * CD * 128:(oc + 1) * CD * 128], in_=wst3))
        S.op("vector", lambda e, oc=i: e.tensor_sub(
            t_wzl[:, oc * CD * 128:(oc + 1) * CD * 128], wst3,
            t_wzh[:, oc * CD * 128:(oc + 1) * CD * 128]), inc="VE")
        wz_ve[i] = S.now("VE")
        S.wait("gpsimd", "VE")
        S.dma("gpsimd", lambda e, cc=i: e.dma_start(out=wst4, in_=dwout[cc]), "SP")
        S.wait("vector", "SP")
        S.op("vector", lambda e, cc=i: e.tensor_copy(
            out=t_woh[:, cc * D:(cc + 1) * D], in_=wst4))
        S.op("vector", lambda e, cc=i: e.tensor_sub(
            t_wol[:, cc * D:(cc + 1) * D], wst4,
            t_woh[:, cc * D:(cc + 1) * D]), inc="VE")
        wo_ve[i] = S.now("VE")

    ntl3 = NCH
    te_wz = {}
    te_out = {}
    ve_mix = {}
    ve_out = {}
    sp_out = {}
    pf_par = {}
    se_z = {}
    ve_oc = {}
    gi3 = [0]

    def sl3(t):
        sl = t % 2
        return (ap(o_xnt3 + sl * 2 * CD * 256, CD * 128, BF16),
                ap(o_xnt3 + sl * 2 * CD * 256 + CD * 256, CD * 128, BF16),
                ap(o_ymt3 + sl * CI * Q * 4, CI * Q, F32),
                ap(o_ffr + sl * D * 4, D, F32),
                ap(o_mtg + sl * 2 * CI * 256, CI * 128, BF16),
                ap(o_mtg + sl * 2 * CI * 256 + CI * 256, CI * 128, BF16))

    def pf3(t):
        if t >= ntl3:
            return
        xnh3, xnl3, ymt, ffr, _, _ = sl3(t)
        if t >= 2:
            S.wait("sync", "TE", te_wz[t - 2])
            S.wait("sync", "VE", ve_mix[t - 2])
            S.wait("scalar", "SP", sp_out[t - 2])
        par = newld()
        for cc in range(CD):
            S.dma("sync", lambda e, cc=cc, t=t, d=xnh3: e.dma_start(
                out=d[:, cc * 128:(cc + 1) * 128],
                in_=sp_xnh[:, cc * TH + 3 + t * Q:cc * TH + 3 + (t + 1) * Q]), par)
            S.dma("sync", lambda e, cc=cc, t=t, d=xnl3: e.dma_start(
                out=d[:, cc * 128:(cc + 1) * 128],
                in_=sp_xnl[:, cc * TH + 3 + t * Q:cc * TH + 3 + (t + 1) * Q]), par)
        S.dma("sync", lambda e, t=t, d=ymt: e.dma_start(out=d, in_=sp_ym[t]), par)
        pars = newls()
        S.dma("scalar", lambda e, t=t, d=ffr: e.dma_start(
            out=d, in_=sp_ffn[t * Q:(t + 1) * Q, :]), pars)
        pf_par[t] = (par, pars)

    pf3(0)
    for t in range(ntl3 + 1):
        if t >= 1:
            tp = t - 1
            _, _, _, ffrp, mghp, mglp = sl3(tp)
            S.wait("tensor", "VE", ve_mix[tp])
            if tp == 0:
                S.wait("tensor", "VE", wo_ve[CI - 1])
            for half in range(2):
                pb = psF[4 + half]
                if tp >= 1:
                    S.wait("tensor", "VE", ve_out[tp - 1])
                seq = [(a, b, cc) for a, b in ((mghp, t_woh), (mghp, t_wol),
                                               (mglp, t_woh)) for cc in range(CI)]
                for i, (aop, wop, cc) in enumerate(seq):
                    S.op("tensor", lambda e, p=pb, aop=aop, wop=wop, cc=cc, half=half,
                         fs=(i == 0), ls=(i == len(seq) - 1): e.matmul(
                             p[:, 0:512], aop[:, cc * 128:(cc + 1) * 128],
                             wop[:, cc * D + half * 512:cc * D + half * 512 + 512],
                             start=fs, stop=ls),
                         inc="TE" if i == len(seq) - 1 else None)
            te_out[tp] = S.now("TE")
            S.wait("vector", "TE", te_out[tp])
            S.wait("vector", pf_par[tp][0])
            S.wait("vector", pf_par[tp][1])
            for half in range(2):
                S.op("vector", lambda e, p=psF[4 + half], d=ffrp, half=half:
                     e.tensor_add(d[:, half * 512:half * 512 + 512], p[:, 0:512],
                                  d[:, half * 512:half * 512 + 512]),
                     inc="VE" if half == 1 else None)
            ve_out[tp] = S.now("VE")
            S.wait("gpsimd", "VE", ve_out[tp])
            S.dma("gpsimd", lambda e, tp=tp, d=ffrp: e.dma_start(
                out=dout[tp * Q:(tp + 1) * Q, :], in_=d), "SP")
            sp_out[tp] = S.now("SP")
        if t >= ntl3:
            continue
        xnh3, xnl3, ymt, ffr, mgh, mgl = sl3(t)
        pf3(t + 1)
        S.wait("tensor", pf_par[t][0])
        first_mix = True
        for oc in range(CI):
            g = gi3[0]
            gi3[0] += 1
            pb = psF[g % 4]
            if t == 0:
                S.wait("tensor", "VE", wz_ve[oc])
            if g >= 4:
                S.wait("tensor", "SE", se_z[g - 4])
            seq = [(a, b, cc) for a, b in ((t_wzh, xnh3), (t_wzh, xnl3), (t_wzl, xnh3))
                   for cc in range(CD)]
            for i, (wop, aop, cc) in enumerate(seq):
                S.op("tensor", lambda e, p=pb, wop=wop, aop=aop, oc=oc, cc=cc,
                     fs=(i == 0), ls=(i == len(seq) - 1): e.matmul(
                         p[:, 0:Q],
                         wop[:, (oc * CD + cc) * 128:(oc * CD + cc) * 128 + 128],
                         aop[:, cc * 128:(cc + 1) * 128], start=fs, stop=ls),
                     inc="TE" if i == len(seq) - 1 else None)
            if oc == CI - 1:
                te_wz[t] = S.now("TE")
            zt = ap(o_zt + (g % 2) * Q * 4, Q, F32)
            yt = ap(o_yt + (g % 2) * Q * 4, Q, F32)
            S.wait("scalar", "TE")
            if g >= 2:
                S.wait("scalar", "VE", ve_oc[g - 2])
            S.op("scalar", lambda e, p=pb, d=zt: e.activation(
                out=d, in_=p[:, 0:Q], func=AF.Silu), inc="SE")
            se_z[g] = S.now("SE")
            S.wait("vector", "SE", se_z[g])
            if first_mix:
                S.wait("vector", pf_par[t][0])
                if t >= 2:
                    S.wait("vector", "TE", te_out[t - 2])
                first_mix = False
            S.op("vector", lambda e, oc=oc, d=yt, ymt=ymt: e.tensor_scalar(
                d, ymt[:, oc * Q:(oc + 1) * Q], ap(o_stc + oc * 4, 1, F32),
                ap(o_stc + (CI + oc) * 4, 1, F32), ALU.mult, ALU.add))
            S.op("vector", lambda e, d=yt, z=zt: e.tensor_mul(d, d, z))
            S.op("vector", lambda e, oc=oc, d=yt, m=mgh: e.tensor_copy(
                out=m[:, oc * 128:(oc + 1) * 128], in_=d))
            S.op("vector", lambda e, oc=oc, d=yt, mh=mgh, ml=mgl: e.tensor_sub(
                ml[:, oc * 128:(oc + 1) * 128], d, mh[:, oc * 128:(oc + 1) * 128]),
                 inc="VE")
            ve_oc[g] = S.now("VE")
        ve_mix[t] = ve_oc[gi3[0] - 1]
    S.wait("gpsimd", "SP")
    A.release("right")

    nc._dbg = dict(o_bh=o_bh, o_bl=o_bl,
                   o_ch=o_ch, o_cl=o_cl, o_st=o_st, o_parts=o_parts,
                   o_expdA=o_expdA, o_stc=o_stc, o_tiny=o_tiny,
                   
                   o_atn=o_atn, o_dtt=o_dtt, o_dwt=o_dwt, o_dtc=o_dtc,
                   o_epc=o_epc, o_dsh=o_dsh, o_eppf=o_eppf)

    with nc.Block() as block:
        def mk(eng_name):
            def go(e):
                for it in S.items[eng_name]:
                    if it[0] == "op":
                        ins = it[1](e)
                        if it[2] is not None:
                            ins.then_inc(S.handles[it[2]], it[3])
                    else:
                        e.wait_ge(S.handles[it[1]], it[2])
            return go
        block.sync(mk("sync"))
        block.tensor(mk("tensor"))
        block.vector(mk("vector"))
        block.scalar(mk("scalar"))
        block.gpsimd(mk("gpsimd"))
    es.close()
    return nc



def _pack(inputs, T, n_shards, n_batches):
    f32 = np.float32
    wx = np.ascontiguousarray(inputs["L0_attn_Wx"], f32)
    wz = np.ascontiguousarray(inputs["L0_attn_Wz"], f32)
    wb = np.ascontiguousarray(inputs["L0_attn_Wb"], f32)
    wc = np.ascontiguousarray(inputs["L0_attn_Wc"], f32)
    wdt = np.ascontiguousarray(inputs["L0_attn_Wdt"], f32)
    wout = np.ascontiguousarray(inputs["L0_attn_Wout"], f32)
    wr = np.ascontiguousarray(inputs["L0_ffn_W_r"], f32)
    wk = np.ascontiguousarray(inputs["L0_ffn_W_k"], f32)
    wv = np.ascontiguousarray(inputs["L0_ffn_W_v"], f32)
    import ml_dtypes
    shared = {
        "wxp": np.ascontiguousarray(wx.reshape(CI, 128, CD, 128).transpose(0, 2, 3, 1)),
        "wzp": np.ascontiguousarray(wz.reshape(CI, 128, CD, 128).transpose(0, 2, 3, 1)),
        "wbp": np.ascontiguousarray(wb.reshape(CB, 128, CI, 128).transpose(0, 2, 3, 1)),
        "wcp": np.ascontiguousarray(wc.reshape(CB, 128, CI, 128).transpose(0, 2, 3, 1)),
        "wdtp": np.ascontiguousarray(wdt.reshape(8, CI, 128).transpose(1, 2, 0)),
        "woutr": np.ascontiguousarray(wout.reshape(D, CI, 128).transpose(1, 2, 0)),
        "wrr": np.ascontiguousarray(wr.reshape(D, CD, 128).transpose(1, 2, 0)),
        "wvr": np.ascontiguousarray(wv.reshape(D, CD, 128).transpose(1, 2, 0)),
        "wkp": np.ascontiguousarray(wk.reshape(CD, 128, CD, 128).transpose(0, 2, 3, 1)),
        "convw": np.ascontiguousarray(
            np.asarray(inputs["L0_attn_conv_w"], f32)[:, 0, :]
            .reshape(CI, 128, KCV).transpose(1, 0, 2).reshape(128, CI * KCV)),
        "convb": np.ascontiguousarray(
            np.asarray(inputs["L0_attn_conv_b"], f32).reshape(CI, 128).T),
        "gnw": np.ascontiguousarray(
            np.asarray(inputs["L0_attn_gn_w"], f32).reshape(CI, 128).T),
        "gnb": np.ascontiguousarray(
            np.asarray(inputs["L0_attn_gn_b"], f32).reshape(CI, 128).T),
        "dprep": np.ascontiguousarray(
            np.repeat(np.asarray(inputs["L0_attn_d_param"], f32), PHD)
            .reshape(CI, 128).T),
        "murc": np.ascontiguousarray(
            np.asarray(inputs["L0_ffn_mu_r"], f32).reshape(CD, 128).T),
        "mukc": np.ascontiguousarray(
            np.asarray(inputs["L0_ffn_mu_k"], f32).reshape(CD, 128).T),
        "bdt": np.asarray(inputs["L0_attn_b_dt"], f32).reshape(8, 1),
        "loga": np.asarray(inputs["L0_attn_log_a"], f32).reshape(8, 1),
        "identb": np.eye(128, dtype=ml_dtypes.bfloat16),
        "identf": np.eye(128, dtype=f32),
        "maskneg": np.where(np.arange(128)[None, :] >= np.arange(128)[:, None],
                            np.float32(0.0), np.float32(NEG)),
        "ones8x128": np.ones((8, 128), f32),
        "ones128": np.ones((128, 1), f32),
        "maskci": (np.arange(CI)[None, :] // 2 == np.arange(8)[:, None]).astype(f32),
    }
    selp = np.zeros((8, CB * 128), f32)
    for pc in range(CB):
        for m in range(128):
            selp[2 * pc + (m >= 64), pc * 128 + m] = 1.0
    shared["selp"] = selp
    selh = np.zeros((8, NH * 128), f32)
    for h in range(NH):
        selh[h, h * 128:(h + 1) * 128] = 1.0
    shared["selh"] = selh

    x = np.ascontiguousarray(np.asarray(inputs["L0_attn_input_tensor"], f32))
    per_core = []
    for core in range(n_shards * n_batches):
        b, s = core // n_shards, core % n_shards
        m = dict(shared)
        m["x"] = np.ascontiguousarray(x[b, s * T:(s + 1) * T])
        if s == 0:
            m["xh3"] = np.zeros((3, D), f32)
        else:
            m["xh3"] = np.ascontiguousarray(x[b, s * T - 3:s * T])
        sel = np.zeros((128, 4), f32)
        if s > 0:
            sel[:, s - 1] = 1.0
        m["selshard"] = sel
        per_core.append(m)
    return per_core


def kernel(**inputs):
    B, L, _ = inputs["L0_attn_input_tensor"].shape
    n_batches = B
    n_shards = 8 // n_batches
    T = L // n_shards
    in_maps = _pack(inputs, T, n_shards, n_batches)
    nc = build_program(T, n_shards, n_batches)
    res = run_bass_kernel_spmd(nc, in_maps, core_ids=list(range(8)))
    out = np.empty((B, L, D), np.float32)
    for core in range(8):
        b, s = core // n_shards, core % n_shards
        out[b, s * T:(s + 1) * T] = res.results[core]["out"]
    return out


# version 22
# speedup vs baseline: 1.0487x; 1.0248x over previous
import numpy as np
from contextlib import ExitStack

import concourse.bass as bass
import concourse.mybir as mybir
from concourse.bass_utils import run_bass_kernel_spmd

F32 = mybir.dt.float32
BF16 = mybir.dt.bfloat16
U8 = mybir.dt.uint8
AF = mybir.ActivationFunctionType
ALU = mybir.AluOpType
AX = mybir.AxisListType

D = 1024
INNER = 2048
NH = 8
NST = 64
PHD = 256
KCV = 4
Q = 128
CD = D // 128
CI = INNER // 128
CB = (NH * NST) // 128
EPS = 1e-5
NEG = -1e30

ENGS = ("sync", "tensor", "vector", "scalar", "gpsimd")
SEMS = ("LD", "LDA", "LDB", "LDC", "LS", "LSA", "SP", "TE", "VE", "SE", "GE", "CC", "TT", "VV", "SS", "GG")
SELF_SEM = {"tensor": "TT", "vector": "VV", "scalar": "SS", "gpsimd": "GG",
            "sync": "VV"}


def _tiles(n, m=512):
    k = -(-n // m)
    base, rem = n // k, n % k
    out, off = [], 0
    for i in range(k):
        w = base + (1 if i < rem else 0)
        out.append((off, w))
        off += w
    return out


class Sched:

    def __init__(self):
        self.items = {e: [] for e in ENGS}
        self.cnt = {s: 0 for s in SEMS}
        self.handles = {}
        self.auto_edge = True

    def op(self, eng, fn, inc=None, amt=1):
        if self.auto_edge and eng in ("vector", "scalar", "gpsimd"):
            m = self.mark(eng)
            if m:
                self.items[eng].append(["wait", m[0], m[1], None])
        if inc is not None:
            self.cnt[inc] += amt
        self.items[eng].append(["op", fn, inc, amt])

    def dma(self, eng, fn, inc):
        ae = self.auto_edge
        self.auto_edge = False
        self.op(eng, fn, inc, 16)
        self.auto_edge = ae

    def wait(self, eng, sem, val=None):
        v = self.cnt[sem] if val is None else val
        if v > 0:
            self.items[eng].append(["wait", sem, v, None])

    def now(self, sem):
        return self.cnt[sem]

    def mark(self, eng):
        for it in reversed(self.items[eng]):
            if it[0] == "op":
                if it[2] is None:
                    s = SELF_SEM[eng]
                    it[2], it[3] = s, 1
                    self.cnt[s] += 1
                return (it[2], self.cnt[it[2]])
        return None

    def edge(self, eng):
        m = self.mark(eng)
        if m:
            self.wait(eng, m[0], m[1])

    def qbar(self, eng, others=("tensor", "vector", "scalar", "gpsimd")):
        for o in others:
            if o == eng:
                continue
            m = self.mark(o)
            if m:
                self.wait(eng, m[0], m[1])

    def barrier(self, engs=ENGS):
        marks = [m for m in (self.mark(e) for e in engs) if m]
        for e in engs:
            for sem, v in marks:
                self.wait(e, sem, v)


class Arena:
    def __init__(self, size):
        self.size = size
        self.left = 0
        self.right = size
        self.lstack = []
        self.rstack = []
        self.peak = 0

    def alloc(self, nbytes, side="left", align=64):
        if side == "left":
            off = -(-self.left // align) * align
            self.left = off + nbytes
        else:
            off = (self.right - nbytes) // align * align
            self.right = off
        assert self.left <= self.right, (
            f"SBUF arena overflow: left={self.left} right={self.right}")
        assert self.left <= self.right, (
            f"SBUF pool overflow: left={self.left} right={self.right}")
        self.peak = max(self.peak, self.left + (self.size - self.right))
        return off

    def mark(self, side="right"):
        st = self.lstack if side == "left" else self.rstack
        st.append(self.left if side == "left" else self.right)

    def release(self, side="right"):
        if side == "left":
            self.left = self.lstack.pop()
        else:
            self.right = self.rstack.pop()


def build_program(T, n_shards, n_batches, part_bytes=200704):
    NCH = T // Q
    TH = T + 3
    CNT_GN = float(n_shards * T * PHD)
    FT = min(256, T)

    nc = bass.Bass("TRN2", target_bir_lowering=False, debug=False)

    dx = nc.dram_tensor("x", [T, D], F32, kind="ExternalInput").ap()
    dxh = nc.dram_tensor("xh3", [3, D], F32, kind="ExternalInput").ap()
    dwx = nc.dram_tensor("wxp", [CI, CD, 128, 128], F32, kind="ExternalInput").ap()
    dwz = nc.dram_tensor("wzp", [CI, CD, 128, 128], F32, kind="ExternalInput").ap()
    dwb = nc.dram_tensor("wbp", [CB, CI, 128, 128], F32, kind="ExternalInput").ap()
    dwc = nc.dram_tensor("wcp", [CB, CI, 128, 128], F32, kind="ExternalInput").ap()
    dwdt = nc.dram_tensor("wdtp", [CI, 128, 8], F32, kind="ExternalInput").ap()
    dwout = nc.dram_tensor("woutr", [CI, 128, D], F32, kind="ExternalInput").ap()
    dwr = nc.dram_tensor("wrr", [CD, 128, D], F32, kind="ExternalInput").ap()
    dwv = nc.dram_tensor("wvr", [CD, 128, D], F32, kind="ExternalInput").ap()
    dwk = nc.dram_tensor("wkp", [CD, CD, 128, 128], F32, kind="ExternalInput").ap()
    dconvw = nc.dram_tensor("convw", [128, CI * KCV], F32, kind="ExternalInput").ap()
    dconvb = nc.dram_tensor("convb", [128, CI], F32, kind="ExternalInput").ap()
    dgnw = nc.dram_tensor("gnw", [128, CI], F32, kind="ExternalInput").ap()
    dgnb = nc.dram_tensor("gnb", [128, CI], F32, kind="ExternalInput").ap()
    ddp = nc.dram_tensor("dprep", [128, CI], F32, kind="ExternalInput").ap()
    dmur = nc.dram_tensor("murc", [128, CD], F32, kind="ExternalInput").ap()
    dmuk = nc.dram_tensor("mukc", [128, CD], F32, kind="ExternalInput").ap()
    dbdt = nc.dram_tensor("bdt", [8, 1], F32, kind="ExternalInput").ap()
    dloga = nc.dram_tensor("loga", [8, 1], F32, kind="ExternalInput").ap()
    didb = nc.dram_tensor("identb", [128, 128], BF16, kind="ExternalInput").ap()
    didf = nc.dram_tensor("identf", [128, 128], F32, kind="ExternalInput").ap()
    dmaskn = nc.dram_tensor("maskneg", [128, 128], F32, kind="ExternalInput").ap()
    dselp = nc.dram_tensor("selp", [8, CB * 128], F32, kind="ExternalInput").ap()
    dselh = nc.dram_tensor("selh", [8, NH * 128], F32, kind="ExternalInput").ap()
    do8 = nc.dram_tensor("ones8x128", [8, 128], F32, kind="ExternalInput").ap()
    do128 = nc.dram_tensor("ones128", [128, 1], F32, kind="ExternalInput").ap()
    dmci = nc.dram_tensor("maskci", [8, CI], F32, kind="ExternalInput").ap()
    dsel = nc.dram_tensor("selshard", [128, 4], F32, kind="ExternalInput").ap()
    dout = nc.dram_tensor("out", [T, D], F32, kind="ExternalOutput").ap()

    sp_xnh = nc.dram_tensor("sp_xnh", [128, CD * TH], BF16).ap()
    sp_xnl = nc.dram_tensor("sp_xnl", [128, CD * TH], BF16).ap()
    sp_ym = nc.dram_tensor("sp_ym", [NCH, 128, CI * Q], F32).ap()
    sp_xph = nc.dram_tensor("sp_xph", [128, CI * T], BF16).ap()
    sp_xpl = nc.dram_tensor("sp_xpl", [128, CI * T], BF16).ap()
    sp_zsh = nc.dram_tensor("sp_zsh", [128, CI * T], BF16).ap()
    sp_zsl = nc.dram_tensor("sp_zsl", [128, CI * T], BF16).ap()
    sp_ffn = nc.dram_tensor("sp_ffn", [T, D], F32).ap()
    cc_in = nc.dram_tensor("cc_in", [CB, 128, 257], F32)
    cc_out = nc.dram_tensor("cc_out", [n_shards, CB, 128, 257], F32)
    dbg_t = nc.dram_tensor("dbg", [16, 128, 512], F32).ap()
    ar_in = nc.dram_tensor("ar_in", [1, 16], F32)
    ar_out = nc.dram_tensor("ar_out", [1, 16], F32)
    groups = [list(range(b * n_shards, (b + 1) * n_shards))
              for b in range(n_batches)]

    S = Sched()
    es = ExitStack()
    pool_h = es.enter_context(nc.sbuf_tensor("pool", [128, part_bytes], U8))
    pool = pool_h[:, :]
    psF = [es.enter_context(nc.psum_tensor(f"ps{i}", [128, 512], F32))
           for i in range(6)]
    psT = es.enter_context(nc.psum_tensor("psT", [128, 1024], BF16))
    psX = es.enter_context(nc.psum_tensor("psX", [128, 512], F32))
    for s in SEMS:
        S.handles[s] = es.enter_context(nc.semaphore(s))

    A = Arena(part_bytes)
    _ldpool = ["LD", "LDA", "LDB", "LDC"]
    _ldi = [0]
    _lspool = ["LS", "LSA"]
    _lsi = [0]

    def newld():
        s = _ldpool[_ldi[0] % 4]
        _ldi[0] += 1
        for q in ("sync", "gpsimd"):
            S.wait(q, s)
        return s

    def newls():
        s = _lspool[_lsi[0] % 2]
        _lsi[0] += 1
        S.wait("scalar", s)
        return s

    def ap(off, n, dt, p0=0, p=128):
        sz = 2 if dt == BF16 else 4
        return pool[p0:p0 + p, off:off + n * sz].bitcast(dt)

    def sb(n):
        return A.alloc(n, "left")
    o_idb = sb(256); o_idf = sb(512); o_mkn = sb(512)
    o_o8 = sb(512); o_o128 = sb(64); o_mci = sb(CI * 4); o_sel = sb(16)
    o_cvw = sb(CI * KCV * 4); o_cvb = sb(CI * 4); o_gnw = sb(CI * 4)
    o_gnb = sb(CI * 4); o_dp = sb(CI * 4)
    o_mur = sb(CD * 4); o_muk = sb(CD * 4); o_mur1 = sb(CD * 4); o_muk1 = sb(CD * 4)
    o_bdt = sb(64); o_loga = sb(64); o_aneg = sb(64)
    o_eppf = sb(max(NCH, 16) * 4)
    o_atn = sb(NCH * 32); o_dtt = sb(NCH * 32); o_dwt = sb(NCH * 32)
    o_dtc = sb(CB * NCH * 4); o_epc = sb(NH * NCH * 4); o_dsh = sb(CB * 4)
    o_stc = sb(2 * CI * 4)
    o_tiny = sb(128 * 4)
    o_one8 = sb(64); o_eps8 = sb(64)

    c_idb = ap(o_idb, 128, BF16); c_idf = ap(o_idf, 128, F32)
    c_mkn = ap(o_mkn, 128, F32)
    c_o8 = ap(o_o8, 128, F32, p=8); c_o128 = ap(o_o128, 1, F32)
    c_mci = ap(o_mci, CI, F32, p=8); c_sel = ap(o_sel, 4, F32)
    c_cvb = ap(o_cvb, CI, F32)
    c_gnw = ap(o_gnw, CI, F32); c_gnb = ap(o_gnb, CI, F32)
    c_mur = ap(o_mur, CD, F32); c_muk = ap(o_muk, CD, F32)
    c_mur1 = ap(o_mur1, CD, F32); c_muk1 = ap(o_muk1, CD, F32)
    c_bdt = ap(o_bdt, 1, F32, p=8); c_loga = ap(o_loga, 1, F32, p=8)
    c_aneg = ap(o_aneg, 1, F32, p=8)
    t_eppf = ap(o_eppf, NCH, F32, p=8)
    t_atn = ap(o_atn, NCH * 8, F32)
    t_dtt = ap(o_dtt, NCH * 8, F32)
    t_dwt = ap(o_dwt, NCH * 8, F32)
    t_dtc = ap(o_dtc, CB * NCH, F32)
    t_epc = ap(o_epc, NH * NCH, F32)
    t_dsh = ap(o_dsh, CB, F32)
    t_stc = ap(o_stc, 2 * CI, F32)
    t_tiny = ap(o_tiny, 128, F32)
    c_one8 = ap(o_one8, 1, F32, p=8); c_eps8 = ap(o_eps8, 1, F32, p=8)

    def mm3(eng_psum, lh, ll, rh, rl, kslices, out_sl, extra_first=False):
        seq = []
        for wop, aop in ((lh, rh), (lh, rl), (ll, rh)):
            for ks in kslices:
                seq.append((wop, aop, ks))
        return seq

    _c0par = newld()
    for dst, src in [
        (c_idb, didb), (c_idf, didf), (c_mkn, dmaskn),
        (c_o8, do8), (c_o128, do128), (c_mci, dmci),
        (c_sel, dsel), (ap(o_cvw, CI * KCV, F32), dconvw), (c_cvb, dconvb),
        (c_gnw, dgnw), (c_gnb, dgnb), (ap(o_dp, CI, F32), ddp),
        (c_mur, dmur), (c_muk, dmuk), (c_bdt, dbdt), (c_loga, dloga),
    ]:
        S.dma("sync", (lambda e, d=dst, s=src: e.dma_start(out=d, in_=s[:, :])), _c0par)
    S.wait("vector", _c0par)
    S.op("vector", lambda e: e.tensor_scalar(c_mur1, c_mur, -1.0, 1.0, ALU.mult, ALU.add))
    S.op("vector", lambda e: e.tensor_scalar(c_muk1, c_muk, -1.0, 1.0, ALU.mult, ALU.add))
    S.op("vector", lambda e: e.memset(c_one8, 1.0))
    S.op("vector", lambda e: e.memset(c_eps8, EPS), inc="VE")

    A.mark("right")
    o_xnh = A.alloc(CD * TH * 2, "right")
    o_xnl = A.alloc(CD * TH * 2, "right")
    t_xnh = ap(o_xnh, CD * TH, BF16)
    t_xnl = ap(o_xnl, CD * TH, BF16)
    A.mark("right")
    o_xt = A.alloc(2 * D * 4, "right")
    o_xsq = A.alloc(D * 4, "right")
    o_lnc = A.alloc(8 * 4, "right")
    o_xnth = A.alloc(D * 2, "right")
    o_xntl = A.alloc(D * 2, "right")

    nrows = TH
    rchunks = [(i * 128, min(128, nrows - i * 128)) for i in range(-(-nrows // 128))]
    ve_rows = []
    for r, (row0, rn) in enumerate(rchunks):
        xt = ap(o_xt + (r % 2) * D * 4, D, F32, p=rn)
        xsq = ap(o_xsq, D, F32, p=rn)
        xnth = ap(o_xnth, D, BF16, p=rn)
        xntl = ap(o_xntl, D, BF16, p=rn)

        def lc(i, p=rn):
            return ap(o_lnc + i * 4, 1, F32, p=p)
        par = newld()
        if r >= 2:
            S.wait("sync", "VE", ve_rows[r - 2])
        if r == 0:
            S.dma("sync", lambda e: e.dma_start(
                out=pool[0:3, o_xt:o_xt + D * 4].bitcast(F32), in_=dxh[:, :]), par)
            S.dma("sync", lambda e, b=min(125, T): e.dma_start(
                out=pool[3:3 + b, o_xt:o_xt + D * 4].bitcast(F32), in_=dx[0:b, :]), par)
        else:
            S.dma("sync", lambda e, d=xt, a=row0 - 3, b=rn: e.dma_start(
                out=d, in_=dx[a:a + b, :]), par)
        S.wait("vector", par)
        S.wait("scalar", par)
        S.op("vector", lambda e, xt=xt, c=lc(0): e.reduce_sum(c, xt, axis=AX.X))
        S.op("vector", lambda e, xt=xt, xs=xsq: e.tensor_mul(xs, xt, xt))
        S.edge("vector")
        S.op("vector", lambda e, xs=xsq, c=lc(1): e.reduce_sum(c, xs, axis=AX.X))
        S.edge("vector")
        S.op("vector", lambda e, c=lc(0), o=lc(2): e.tensor_scalar(o, c, -1.0 / D, None, ALU.mult))
        S.edge("vector")
        S.op("vector", lambda e, c=lc(2), o=lc(3): e.tensor_mul(o, c, c))
        S.edge("vector")
        S.op("vector", lambda e, c=lc(3), o=lc(4): e.tensor_scalar(o, c, -1.0, EPS, ALU.mult, ALU.add),
             inc="VE")
        S.wait("scalar", "VE")
        S.op("scalar", lambda e, c=lc(1), b=lc(4), o=lc(6): e.activation(
            out=o, in_=c, func=AF.Sqrt, bias=b, scale=1.0 / D), inc="SE")
        S.wait("vector", "SE")
        S.op("vector", lambda e, s=lc(6), o=lc(5): e.reciprocal(o, s))
        S.edge("vector")
        S.op("vector", lambda e, xt=xt, nm=lc(2), rs=lc(5): e.tensor_scalar(
            xt, xt, nm, rs, ALU.add, ALU.mult))
        S.edge("vector")
        S.op("vector", lambda e, xt=xt, h=xnth: e.tensor_copy(out=h, in_=xt))
        S.edge("vector")
        S.op("vector", lambda e, xt=xt, h=xnth, l=xntl: e.tensor_sub(l, xt, h), inc="VE")
        ve_rows.append(S.now("VE"))
        S.wait("tensor", "VE")
        for src, dstt in ((xnth, t_xnh), (xntl, t_xnl)):
            for cc in range(CD):
                S.op("tensor", lambda e, s=src, cc=cc, rn=rn: e.transpose(
                    psT[:, cc * 128:cc * 128 + rn], s[:, cc * 128:(cc + 1) * 128],
                    c_idb[0:rn, 0:rn]),
                     inc="TE" if cc == CD - 1 else None)
            S.wait("scalar", "TE")
            for cc in range(CD):
                S.op("scalar", lambda e, cc=cc, d=dstt, row0=row0, rn=rn: e.copy(
                    out=d[:, cc * TH + row0:cc * TH + row0 + rn],
                    in_=psT[:, cc * 128:cc * 128 + rn]),
                     inc="SE" if cc == CD - 1 else None)
            S.wait("tensor", "SE")
    S.wait("gpsimd", "SE")
    S.dma("gpsimd", lambda e: e.dma_start(out=sp_xnh, in_=t_xnh), "SP")
    S.dma("gpsimd", lambda e: e.dma_start(out=sp_xnl, in_=t_xnl), "SP")
    S.barrier()
    A.release("right")

    A.mark("left")
    A.mark("right")
    o_xst = A.alloc(2 * 2 * T * 2, "right")
    o_ws = A.alloc(2 * CD * 512, "right")
    o_wh = A.alloc(2 * CD * 256, "right")
    o_wl = A.alloc(2 * CD * 256, "right")
    o_xpr = A.alloc(2 * TH * 4, "right")
    o_cv = A.alloc(2 * T * 4, "right")
    o_sil = A.alloc(2 * T * 4, "right")
    o_cvt = A.alloc(4 * T * 4, "right")
    wtiles = _tiles(TH, 512)
    te_oc = []
    sp_oc = []
    ge_oc = []
    se_oc = []
    ve_sub = []
    ve_nt = {}
    g1i = [0]
    for oc in range(CI):
        ws = ap(o_ws + (oc % 2) * CD * 512, CD * 128, F32)
        wh = ap(o_wh + (oc % 2) * CD * 256, CD * 128, BF16)
        wl = ap(o_wl + (oc % 2) * CD * 256, CD * 128, BF16)
        xpr = ap(o_xpr + (oc % 2) * TH * 4, TH, F32)
        cv = ap(o_cv + (oc % 2) * T * 4, T, F32)
        sil = ap(o_sil + (oc % 2) * T * 4, T, F32)
        par = newld()
        if oc >= 2:
            S.wait("sync", "TE", te_oc[oc - 2])
        S.dma("sync", lambda e, oc=oc, d=ws: e.dma_start(
            out=d.rearrange("p (c q) -> p c q", q=128),
            in_=dwx[oc].rearrange("c p q -> p c q")), par)
        S.wait("vector", par)
        if oc >= 2:
            S.wait("vector", "TE", te_oc[oc - 2])
        S.op("vector", lambda e, s=ws, h=wh: e.tensor_copy(out=h, in_=s))
        S.edge("vector")
        S.op("vector", lambda e, s=ws, h=wh, l=wl: e.tensor_sub(l, s, h), inc="VE")
        S.wait("tensor", "VE")
        for nt, (off, w) in enumerate(wtiles):
            g1 = g1i[0]; g1i[0] += 1
            pb = psF[g1 % 4]
            if g1 >= 4:
                S.wait("tensor", "VE", ve_nt[g1 - 4])
            seq = [(a, b, cc) for a, b in ((wh, t_xnh), (wh, t_xnl), (wl, t_xnh))
                   for cc in range(CD)]
            for i, (wop, aop, cc) in enumerate(seq):
                S.op("tensor", lambda e, p=pb, wop=wop, aop=aop, cc=cc, off=off, w=w,
                     fs=(i == 0), ls=(i == len(seq) - 1): e.matmul(
                         p[:, 0:w], wop[:, cc * 128:(cc + 1) * 128],
                         aop[:, cc * TH + off:cc * TH + off + w], start=fs, stop=ls),
                     inc="TE" if i == len(seq) - 1 else None)
            S.wait("vector", "TE")
            if nt == 0 and oc >= 2:
                S.wait("vector", "GE", ge_oc[oc - 2])
                S.wait("vector", "SE", se_oc[oc - 2])
            S.op("vector", lambda e, p=pb, off=off, w=w, x=xpr: e.tensor_copy(
                out=x[:, off:off + w], in_=p[:, 0:w]), inc="VE")
            ve_nt[g1] = S.now("VE")
        te_oc.append(S.now("TE"))
        tmp1 = ap(o_cvt + (oc % 2) * 2 * T * 4, T, F32)
        tmp2 = ap(o_cvt + (oc % 2) * 2 * T * 4 + T * 4, T, F32)
        wc_ = [ap(o_cvw + (oc * KCV + k) * 4, 1, F32) for k in range(KCV)]
        S.wait("scalar", "VE", ve_nt[g1i[0] - 1])
        if oc >= 2:
            S.wait("scalar", "GE", ge_oc[oc - 2])
            S.wait("scalar", "VE", ve_sub[oc - 2])
        S.op("scalar", lambda e, x=xpr, o=tmp1, wc=wc_[0]: e.activation(
            out=o, in_=x[:, 0:T], func=AF.Identity, scale=wc))
        S.edge("scalar")
        S.op("scalar", lambda e, x=xpr, o=tmp2, wc=wc_[1]: e.activation(
            out=o, in_=x[:, 1:1 + T], func=AF.Identity, scale=wc), inc="SE")
        S.wait("gpsimd", "SE")
        S.op("gpsimd", lambda e, c=cv, a=tmp1, b=tmp2: e.tensor_add(c, a, b), inc="GE")
        S.wait("scalar", "GE")
        S.op("scalar", lambda e, x=xpr, o=tmp1, wc=wc_[2]: e.activation(
            out=o, in_=x[:, 2:2 + T], func=AF.Identity, scale=wc))
        S.edge("scalar")
        S.op("scalar", lambda e, x=xpr, o=tmp2, wc=wc_[3]: e.activation(
            out=o, in_=x[:, 3:3 + T], func=AF.Identity, scale=wc), inc="SE")
        S.wait("gpsimd", "SE")
        S.op("gpsimd", lambda e, c=cv, a=tmp1: e.tensor_add(c, c, a))
        S.edge("gpsimd")
        S.op("gpsimd", lambda e, c=cv, b=tmp2: e.tensor_add(c, c, b), inc="GE")
        S.wait("scalar", "GE")
        ge_oc.append(S.now("GE"))
        S.op("scalar", lambda e, c=cv, s=sil, b=ap(o_cvb + oc * 4, 1, F32): e.activation(
            out=s, in_=c, func=AF.Silu, bias=b, scale=1.0))
        S.edge("scalar")
        xsh = ap(o_xst + (oc % 2) * 2 * T * 2, T, BF16)
        xsl = ap(o_xst + (oc % 2) * 2 * T * 2 + T * 2, T, BF16)
        if oc >= 2:
            S.wait("scalar", "SP", sp_oc[oc - 2])
        S.op("scalar", lambda e, s=sil, d=xsh: e.copy(out=d, in_=s), inc="SE")
        S.wait("vector", "SE")
        S.op("vector", lambda e, s=sil, h=xsh, l=xsl: e.tensor_sub(l, s, h), inc="VE")
        ve_sub.append(S.now("VE"))
        se_oc.append(S.now("SE"))
        S.wait("gpsimd", "VE")
        S.dma("gpsimd", lambda e, oc=oc, h=xsh: e.dma_start(
            out=sp_xph[:, oc * T:(oc + 1) * T], in_=h), "SP")
        S.dma("gpsimd", lambda e, oc=oc, l=xsl: e.dma_start(
            out=sp_xpl[:, oc * T:(oc + 1) * T], in_=l), "SP")
        sp_oc.append(S.now("SP"))
    S.barrier()
    A.release("right")
    S.barrier()
    A.release("right")

    o_selp = sb(CB * 512); o_selh = sb(NH * 512)
    o_st = sb(CB * 1024)
    o_sth = sb(CB * 512); o_stl = sb(CB * 512)
    o_parts = sb(2 * NCH * CI * 4)
    o_expdA = sb(T * 4)
    o_A = sb((T + 4) * 4)
    c_selp = ap(o_selp, CB * 128, F32, p=8)
    c_selh = ap(o_selh, NH * 128, F32, p=8)
    t_st = ap(o_st, CB * 256, F32)
    t_sth = ap(o_sth, CB * 256, BF16)
    t_stl = ap(o_stl, CB * 256, BF16)
    t_parts = ap(o_parts, 2 * NCH * CI, F32)
    t_expdA = ap(o_expdA, T, F32, p=8)
    _selpar = newld()
    S.dma("sync", lambda e: e.dma_start(out=c_selp, in_=dselp[:, :]), _selpar)
    S.dma("sync", lambda e: e.dma_start(out=c_selh, in_=dselh[:, :]), _selpar)
    S.wait("vector", _selpar)
    S.op("vector", lambda e: e.memset(t_st, 0.0))
    S.op("vector", lambda e: e.memset(t_parts, 0.0), inc="VE")
    S.wait("tensor", _selpar)
    o_bh = sb(CB * T * 2); o_bl = sb(CB * T * 2)
    o_ch = sb(CB * T * 2); o_cl = sb(CB * T * 2)
    t_bh = ap(o_bh, CB * T, BF16); t_bl = ap(o_bl, CB * T, BF16)
    t_ch = ap(o_ch, CB * T, BF16); t_cl = ap(o_cl, CB * T, BF16)
    t_A = ap(o_A, T + 1, F32, p=8)
    A.mark("right")
    o_wdth = A.alloc(CI * 16, "right"); o_wdtl = A.alloc(CI * 16, "right")
    o_wbh = A.alloc(CB * CI * 256, "right"); o_wbl = A.alloc(CB * CI * 256, "right")
    o_wch = A.alloc(CB * CI * 256, "right"); o_wcl = A.alloc(CB * CI * 256, "right")
    o_ws2 = A.alloc(CI * 512, "right")
    XT2 = min(128, T)
    o_xpt = A.alloc(2 * 2 * CI * XT2 * 2, "right")
    o_dtraw = o_expdA
    wdth = ap(o_wdth, CI * 8, BF16); wdtl = ap(o_wdtl, CI * 8, BF16)
    ws2 = ap(o_ws2, CI * 128, F32)
    t_dtraw = ap(o_dtraw, T, F32, p=8)

    par = newld()
    for cc in range(CI):
        S.dma("sync", lambda e, cc=cc: e.dma_start(
            out=pool[:, o_ws2 + cc * 32:o_ws2 + cc * 32 + 32].bitcast(F32),
            in_=dwdt[cc]), par)
    S.wait("vector", par)
    wdt_s = ap(o_ws2, CI * 8, F32)
    S.op("vector", lambda e: e.tensor_copy(out=wdth, in_=wdt_s))
    S.op("vector", lambda e: e.tensor_sub(wdtl, wdt_s, wdth), inc="VE")
    for dwt_, oh_, ol_ in ((dwb, o_wbh, o_wbl), (dwc, o_wch, o_wcl)):
        for ob in range(CB):
            S.qbar("sync", ("vector",))
            par = newld()
            S.dma("sync", lambda e, dw=dwt_, ob=ob: e.dma_start(
                out=ws2.rearrange("p (c q) -> p c q", q=128),
                in_=dw[ob].rearrange("c p q -> p c q")), par)
            S.wait("vector", par)
            wh_ = pool[:, oh_ + ob * CI * 256:oh_ + (ob + 1) * CI * 256].bitcast(BF16)
            wl_ = pool[:, ol_ + ob * CI * 256:ol_ + (ob + 1) * CI * 256].bitcast(BF16)
            S.op("vector", lambda e, h=wh_: e.tensor_copy(out=h, in_=ws2))
            S.op("vector", lambda e, h=wh_, l=wl_: e.tensor_sub(l, ws2, h), inc="VE")
    t_wbh = ap(o_wbh, CB * CI * 128, BF16); t_wbl = ap(o_wbl, CB * CI * 128, BF16)
    t_wch = ap(o_wch, CB * CI * 128, BF16); t_wcl = ap(o_wcl, CB * CI * 128, BF16)
    ztiles = _tiles(T, XT2)
    ve_pnt = []
    se_dt = []
    te_tile = []
    for nt, (off, w) in enumerate(ztiles):
        xpth = ap(o_xpt + (nt % 2) * 2 * CI * XT2 * 2, CI * XT2, BF16)
        xptl = ap(o_xpt + (nt % 2) * 2 * CI * XT2 * 2 + CI * XT2 * 2, CI * XT2, BF16)
        if nt >= 2:
            S.wait("sync", "TE", te_tile[nt - 2])
        par = newld()
        for cc in range(CI):
            S.dma("sync", lambda e, cc=cc, off=off, w=w, d=xpth: e.dma_start(
                out=d[:, cc * XT2:cc * XT2 + w],
                in_=sp_xph[:, cc * T + off:cc * T + off + w]), par)
            S.dma("sync", lambda e, cc=cc, off=off, w=w, d=xptl: e.dma_start(
                out=d[:, cc * XT2:cc * XT2 + w],
                in_=sp_xpl[:, cc * T + off:cc * T + off + w]), par)
        S.wait("tensor", par)
        if nt >= 1:
            S.wait("tensor", "SE", se_dt[-1])
        pb = psF[2]
        seq = [(a, b, cc) for a, b in ((wdth, xpth), (wdth, xptl), (wdtl, xpth))
               for cc in range(CI)]
        for i, (wop, aop, cc) in enumerate(seq):
            S.op("tensor", lambda e, p=pb, wop=wop, aop=aop, cc=cc, w=w,
                 fs=(i == 0), ls=(i == len(seq) - 1): e.matmul(
                     p[0:8, 0:w], wop[:, cc * 8:(cc + 1) * 8],
                     aop[:, cc * XT2:cc * XT2 + w], start=fs, stop=ls),
                 inc="TE" if i == len(seq) - 1 else None)
        S.wait("scalar", "TE")
        S.op("scalar", lambda e, p=pb, off=off, w=w: e.activation(
            out=t_dtraw[:, off:off + w], in_=p[0:8, 0:w], func=AF.Identity,
            bias=c_bdt, scale=1.0), inc="SE")
        se_dt.append(S.now("SE"))
        for wsel, (whh, wll, th_, tl_) in enumerate(
                ((t_wbh, t_wbl, t_bh, t_bl), (t_wch, t_wcl, t_ch, t_cl))):
            for ob in range(CB):
                gi = nt * 2 * CB + wsel * CB + ob
                pb = (psF[0], psF[1], psF[3], psF[4])[gi % 4]
                if gi >= 4:
                    S.wait("tensor", "VE", ve_pnt[gi - 4])
                seq = [(a, b, cc) for a, b in ((whh, xpth), (whh, xptl), (wll, xpth))
                       for cc in range(CI)]
                for i, (wop, aop, cc) in enumerate(seq):
                    S.op("tensor", lambda e, p=pb, wop=wop, aop=aop, ob=ob, cc=cc, w=w,
                         fs=(i == 0), ls=(i == len(seq) - 1): e.matmul(
                             p[:, 0:w], wop[:, (ob * CI + cc) * 128:(ob * CI + cc) * 128 + 128],
                             aop[:, cc * XT2:cc * XT2 + w], start=fs, stop=ls),
                         inc="TE" if i == len(seq) - 1 else None)
                S.wait("scalar", "TE")
                S.op("scalar", lambda e, p=pb, d=th_, ob=ob, off=off, w=w: e.copy(
                    out=d[:, ob * T + off:ob * T + off + w], in_=p[:, 0:w]), inc="SE")
                S.wait("vector", "SE")
                S.op("vector", lambda e, p=pb, dh=th_, dl=tl_, ob=ob, off=off, w=w:
                     e.tensor_sub(dl[:, ob * T + off:ob * T + off + w], p[:, 0:w],
                                  dh[:, ob * T + off:ob * T + off + w]), inc="VE")
                ve_pnt.append(S.now("VE"))
        te_tile.append(S.now("TE"))
    S.barrier()
    A.release("right")

    A.mark("right")
    o_dt = A.alloc(T * 4, "right")
    o_8a = A.alloc(T * 4, "right")
    o_8b = A.alloc(T * 4, "right")
    t_dt = ap(o_dt, T, F32, p=8)
    t_8a = ap(o_8a, T, F32, p=8)
    t_8b = ap(o_8b, T, F32, p=8)
    S.op("scalar", lambda e: e.activation(out=t_8b, in_=t_dtraw, func=AF.Abs))
    S.edge("scalar")
    S.op("scalar", lambda e: e.activation(out=t_8b, in_=t_8b, func=AF.Exp, scale=-1.0))
    S.edge("scalar")
    S.op("scalar", lambda e: e.activation(out=t_8b, in_=t_8b, func=AF.Ln, bias=c_one8), inc="SE")
    S.wait("vector", "SE")
    S.op("vector", lambda e: e.tensor_relu(t_dt, t_dtraw))
    S.edge("vector")
    S.op("vector", lambda e: e.tensor_add(t_dt, t_dt, t_8b), inc="VE")
    S.wait("scalar", "VE")
    S.op("scalar", lambda e: e.activation(out=t_tiny[0:8, 8:9], in_=c_loga, func=AF.Exp),
         inc="SE")
    S.wait("vector", "SE")
    S.op("vector", lambda e: e.tensor_scalar(c_aneg, t_tiny[0:8, 8:9], -1.0, None, ALU.mult))
    S.op("vector", lambda e: e.memset(t_8b, 1.0))
    S.op("vector", lambda e: e.memset(t_A[:, 0:1], 0.0))
    S.edge("vector")
    S.op("vector", lambda e: e.tensor_scalar(t_8a, t_dt, c_aneg, None, ALU.mult))
    S.edge("vector")
    S.op("vector", lambda e: e.tensor_tensor_scan(t_A[:, 1:T + 1], t_8b, t_8a, 0.0,
                                                  ALU.mult, ALU.add))
    S.edge("vector")
    aprev = t_A[:, 0:(NCH - 1) * Q + 1:Q].unsqueeze(2).broadcast_to([8, NCH, Q])
    acur3 = t_A[:, 1:T + 1].rearrange("p (c q) -> p c q", q=Q)
    out3a = t_8a.rearrange("p (c q) -> p c q", q=Q)
    S.op("vector", lambda e: e.tensor_tensor(out3a, acur3, aprev, ALU.subtract), inc="VE")
    S.wait("scalar", "VE")
    S.op("scalar", lambda e: e.activation(out=t_expdA, in_=t_8a, func=AF.Exp), inc="SE")
    aend = t_A[:, Q:T + 1:Q].unsqueeze(2).broadcast_to([8, NCH, Q])
    S.wait("vector", "SE")
    S.op("vector", lambda e: e.tensor_tensor(out3a, aend, acur3, ALU.subtract), inc="VE")
    S.wait("scalar", "VE")
    S.op("scalar", lambda e: e.activation(out=t_8a, in_=t_8a, func=AF.Exp), inc="SE")
    S.wait("vector", "SE")
    S.op("vector", lambda e: e.tensor_mul(t_8a, t_8a, t_dt), inc="VE")
    S.wait("scalar", "VE")
    S.op("scalar", lambda e: e.activation(out=t_eppf, in_=t_A[:, 0:(NCH - 1) * Q + 1:Q],
                                          func=AF.Exp), inc="SE")
    S.wait("vector", "SE")
    S.op("vector", lambda e: e.tensor_sub(t_tiny[0:8, 16:16 + NCH], t_A[:, Q:T + 1:Q],
                                          t_A[:, 0:(NCH - 1) * Q + 1:Q]), inc="VE")
    S.wait("scalar", "VE")
    S.op("scalar", lambda e: e.activation(out=t_tiny[0:8, 48:48 + NCH],
                                          in_=t_tiny[0:8, 16:16 + NCH], func=AF.Exp))
    S.edge("scalar")
    S.op("scalar", lambda e: e.activation(out=t_tiny[0:8, 78:79], in_=t_A[:, T:T + 1],
                                          func=AF.Exp), inc="SE")
    S.wait("tensor", "SE")
    for k in range(NCH):
        for srcf, dsto, negf in ((t_A[:, 1 + k * Q:1 + (k + 1) * Q], t_atn, -1.0),
                                 (t_dt[:, k * Q:(k + 1) * Q], t_dtt, 1.0),
                                 (t_8a[:, k * Q:(k + 1) * Q], t_dwt, 1.0)):
            S.op("tensor", lambda e, s=srcf: e.transpose(psX[:, 0:8], s, c_idf[0:8, 0:8]), inc="TE")
            S.wait("vector", "TE")
            S.op("vector", lambda e, d=dsto, k=k, ng=negf: e.tensor_scalar(
                d[:, k * 8:(k + 1) * 8], psX[:, 0:8], ng, None, ALU.mult), inc="VE")
            S.wait("tensor", "VE")
    for cbi in range(CB):
        S.op("tensor", lambda e, cbi=cbi: e.matmul(
            psX[:, 0:NCH], c_selp[:, cbi * 128:(cbi + 1) * 128],
            t_tiny[0:8, 48:48 + NCH], start=True, stop=True), inc="TE")
        S.wait("vector", "TE")
        S.op("vector", lambda e, cbi=cbi: e.tensor_copy(
            out=t_dtc[:, cbi * NCH:(cbi + 1) * NCH], in_=psX[:, 0:NCH]), inc="VE")
        S.wait("tensor", "VE")
        S.op("tensor", lambda e, cbi=cbi: e.matmul(
            psX[:, 0:1], c_selp[:, cbi * 128:(cbi + 1) * 128],
            t_tiny[0:8, 78:79], start=True, stop=True), inc="TE")
        S.wait("vector", "TE")
        S.op("vector", lambda e, cbi=cbi: e.tensor_copy(
            out=t_dsh[:, cbi:cbi + 1], in_=psX[:, 0:1]), inc="VE")
        S.wait("tensor", "VE")
    for h in range(NH):
        S.op("tensor", lambda e, h=h: e.matmul(
            psX[:, 0:NCH], c_selh[:, h * 128:(h + 1) * 128], t_eppf,
            start=True, stop=True), inc="TE")
        S.wait("vector", "TE")
        S.op("vector", lambda e, h=h: e.tensor_copy(
            out=t_epc[:, h * NCH:(h + 1) * NCH], in_=psX[:, 0:NCH]), inc="VE")
        S.wait("tensor", "VE")
    S.barrier()
    A.release("right")

    A.mark("right")
    o_xpc = A.alloc(2 * 2 * CI * Q * 2, "right")
    o_xtk = A.alloc(4 * 1024, "right")
    o_bwt = A.alloc(2 * 2 * 256, "right")
    o_ctk = A.alloc(2 * 2 * 256, "right")
    o_sp = A.alloc(4 * 512, "right")
    o_dp2 = A.alloc(4 * 512, "right")
    o_u2 = A.alloc(4 * 512, "right")
    o_scr = A.alloc(8 * 512, "right")
    o_xrc = A.alloc(CI * Q * 4, "right")
    o_crc = A.alloc(CB * Q * 4, "right")
    o_ymst = A.alloc(2 * CI * Q * 4, "right")

    def xpcb(par, half):
        return (ap(o_xpc + par * 2 * CI * Q * 2, CI * Q, BF16),
                ap(o_xpc + par * 2 * CI * Q * 2 + CI * Q * 2, CI * Q, BF16))

    def ymstb(par):
        return ap(o_ymst + par * CI * Q * 4, CI * Q, F32)

    def xtk_h(j):
        return ap(o_xtk + j * 1024, 256, BF16)

    def xtk_l(j):
        return ap(o_xtk + j * 1024 + 512, 256, BF16)

    def bwt(hl, i):
        return ap(o_bwt + hl * 512 + i * 256, 128, BF16)

    def bwt_all(hl):
        return ap(o_bwt + hl * 512, 256, BF16)

    def ctk(hl, i):
        return ap(o_ctk + hl * 512 + i * 256, 128, BF16)

    def ctk_all(hl):
        return ap(o_ctk + hl * 512, 256, BF16)

    def sp_t(hl, j):
        return ap(o_sp + hl * 1024 + j * 256, 128, BF16)

    def sp_all(hl):
        return ap(o_sp + hl * 1024, 512, BF16)

    def u_t(j):
        return ap(o_u2 + j * 512, 128, F32)

    def dp_t(j):
        return ap(o_dp2 + j * 512, 128, F32)

    scan_marks = {}
    gp_gate = [None]

    import os as _osd
    _HVY = _osd.environ.get("KDBG_SCANBAR")
    _GSP = _osd.environ.get("KDBG_GRPSPLIT")
    _SLVL = int(_osd.environ.get("KDBG_SLVL", 9))
    _S1M = int(_osd.environ.get("KDBG_S1MASK", 15))

    def scan_chunk(k, corr):
        if _HVY:
            S.barrier()
        par = k % 2
        ymst = ymstb(par)
        if corr:
            if k >= 2:
                S.wait("sync", "SP", scan_marks[("wb", k - 2)])
                S.wait("sync", "SE", scan_marks[("sq", k - 2)])
            ld = newld()
            S.dma("sync", lambda e, k=k, d=ymst: e.dma_start(out=d, in_=sp_ym[k]), ld)
            S.wait("vector", ld)
            if k >= 1:
                S.wait("gpsimd", "VE", gp_gate[0][1])
            for hp in range(CB):
                S.op("gpsimd", lambda e, hp=hp, k=k: e.tensor_add(
                    ap(o_crc + hp * Q * 4, Q, F32),
                    t_ch[:, hp * T + k * Q:hp * T + (k + 1) * Q],
                    t_cl[:, hp * T + k * Q:hp * T + (k + 1) * Q]), inc="GE")
        else:
            xph_, xpl_ = xpcb(par, 0)
            if k >= 2:
                m = scan_marks[("xte", k - 2)]
                S.wait("sync", m[0], m[1])
                m = scan_marks[("xve", k - 2)]
                S.wait("sync", m[0], m[1])
                S.wait("vector", "SP", scan_marks[("wb", k - 2)])
            ld = newld()
            for ci in range(CI):
                S.dma("sync", lambda e, ci=ci, k=k, d=xph_: e.dma_start(
                    out=d[:, ci * Q:(ci + 1) * Q],
                    in_=sp_xph[:, ci * T + k * Q:ci * T + (k + 1) * Q]), ld)
                S.dma("sync", lambda e, ci=ci, k=k, d=xpl_: e.dma_start(
                    out=d[:, ci * Q:(ci + 1) * Q],
                    in_=sp_xpl[:, ci * T + k * Q:ci * T + (k + 1) * Q]), ld)
            S.op("vector", lambda e: e.tensor_copy(out=t_sth, in_=t_st))
            S.op("vector", lambda e: e.tensor_sub(t_stl, t_st, t_sth), inc="VE")
            S.wait("gpsimd", ld)
            if k >= 1:
                S.wait("gpsimd", "VE", gp_gate[0][1])
            for ci in range(CI):
                S.op("gpsimd", lambda e, ci=ci: e.tensor_add(
                    ap(o_xrc + ci * Q * 4, Q, F32),
                    xph_[:, ci * Q:(ci + 1) * Q], xpl_[:, ci * Q:(ci + 1) * Q]))
            for hp in range(CB):
                S.op("gpsimd", lambda e, hp=hp, k=k: e.tensor_add(
                    ap(o_crc + hp * Q * 4, Q, F32),
                    t_ch[:, hp * T + k * Q:hp * T + (k + 1) * Q],
                    t_cl[:, hp * T + k * Q:hp * T + (k + 1) * Q]), inc="GE")
        for half in range(2):
            if _HVY:
                S.barrier()
            hps = (2 * half, 2 * half + 1)
            heads = [4 * half + j for j in range(4)]
            if not corr:
                S.wait("tensor", ld)
            S.wait("tensor", "VE")
            S.wait("tensor", "SE")
            for i, hp in (enumerate(hps) if (_S1M & 1) else []):
                S.op("tensor", lambda e, hp=hp, k=k, i=i: e.matmul(
                    psF[0][:, i * 128:(i + 1) * 128],
                    c_selp[:, hp * 128:(hp + 1) * 128],
                    t_expdA[:, k * Q:(k + 1) * Q],
                    start=True if _GSP else (i == 0),
                    stop=True if _GSP else (i == 1)))
            if not corr:
                for i, hp in enumerate(hps):
                    S.op("tensor", lambda e, hp=hp, k=k, i=i: e.transpose(
                        psT[:, i * 256:i * 256 + 128],
                        t_bh[:, hp * T + k * Q:hp * T + (k + 1) * Q], c_idb))
                    S.op("tensor", lambda e, hp=hp, k=k, i=i: e.transpose(
                        psT[:, i * 256 + 128:i * 256 + 256],
                        t_bl[:, hp * T + k * Q:hp * T + (k + 1) * Q], c_idb))
                g_seq = [(t_bh, t_ch), (t_bh, t_cl), (t_bl, t_ch)]
                for half_j, jset in ((0, (0, 2)), (1, (1, 3))):
                    nmm = 0
                    for j in jset:
                        h = heads[j]
                        hp, r0 = h // 2, (h % 2) * 64
                        for gg, (ph, pr) in enumerate(g_seq):
                            S.op("tensor", lambda e, ph=ph, pr=pr, hp=hp, r0=r0, k=k, j=j,
                                 fs=(nmm == 0), ls=(nmm == 5): e.matmul(
                                     psF[2][:, j * 128:(j + 1) * 128],
                                     ph[r0:r0 + 64, hp * T + k * Q:hp * T + (k + 1) * Q],
                                     pr[r0:r0 + 64, hp * T + k * Q:hp * T + (k + 1) * Q],
                                     start=fs, stop=ls))
                            nmm += 1
                    if half_j == 0:
                        for j, h in enumerate(heads):
                            S.op("tensor", lambda e, h=h, k=k, j=j: e.matmul(
                                psF[5][:, j * 128:(j + 1) * 128],
                                c_selh[:, h * 128:(h + 1) * 128],
                                t_A[0:8, 1 + k * Q:1 + (k + 1) * Q],
                                start=(j == 0), stop=(j == 3)))
                S.op("tensor", lambda e: e.matmul(
                    psX[:, 0:NCH], c_selp[:, 0:128], t_eppf, start=True, stop=True),
                     inc="TE")
                m_s1 = S.now("TE")
                if _SLVL < 2:
                    continue
                S.wait("vector", "TE", m_s1)
                bwf = ap(o_scr + 1024, 256, F32)
                bwtmp = ap(o_scr + 2048, 256, F32)
                psT3h = psT[:, 0:512].rearrange("p (i t q) -> p i t q", i=2, t=2)
                S.op("vector", lambda e: e.tensor_copy(
                    out=bwtmp, in_=psT3h[:, :, 0, :]))
                S.op("vector", lambda e: e.tensor_add(
                    bwf, bwtmp, psT3h[:, :, 1, :]))
                dwt_bc = t_dwt[:, k * 8 + 4 * half:k * 8 + 4 * half + 4].unsqueeze(2).broadcast_to([128, 4, 64])
                S.op("vector", lambda e: e.tensor_tensor(
                    bwf.rearrange("p (c u) -> p c u", u=64), bwf.rearrange(
                        "p (c u) -> p c u", u=64), dwt_bc, ALU.mult))
                S.op("vector", lambda e: e.tensor_copy(out=bwt_all(0), in_=bwf))
                S.op("vector", lambda e: e.tensor_sub(bwt_all(1), bwf, bwt_all(0)),
                     inc="VE")
                m_bw = S.now("VE")
                S.wait("tensor", "VE", m_bw)
                for j in (0, 1):
                    h = heads[j]
                    for hl, src in ((0, xph_), (1, xpl_)):
                        for pc in range(2):
                            ci = h * 2 + pc
                            S.op("tensor", lambda e, s=src, ci=ci, j=j, hl=hl, pc=pc:
                                 e.transpose(psT[:, j * 512 + hl * 256 + pc * 128:
                                                 j * 512 + hl * 256 + pc * 128 + 128],
                                             s[:, ci * Q:(ci + 1) * Q], c_idb),
                                 inc="TE" if (j == 1 and hl == 1 and pc == 1) else None)
                m_w1 = S.now("TE")
                if _SLVL < 3:
                    continue
                S.wait("vector", "GE")
                crec = ap(o_scr, 256, F32)
                S.op("vector", lambda e, hp0=hps[0]: e.tensor_mul(
                    crec, ap(o_crc + hp0 * Q * 4, 2 * Q, F32), psF[0][:, 0:256]))
                S.op("vector", lambda e: e.tensor_copy(out=ctk_all(0), in_=crec))
                S.op("vector", lambda e: e.tensor_sub(ctk_all(1), crec, ctk_all(0)),
                     inc="VE")
                for j, h in enumerate(heads):
                    S.op("vector", lambda e, j=j, k=k, h=h: e.tensor_scalar(
                        u_t(j), psF[5][:, j * 128:(j + 1) * 128],
                        t_atn[:, k * 8 + h:k * 8 + h + 1], 0.0, ALU.add, ALU.min))
                    S.op("vector", lambda e, j=j: e.tensor_add(u_t(j), u_t(j), c_mkn),
                         inc="VE" if j == 3 else None)
                m_u = S.now("VE")
                S.wait("scalar", "VE", m_u)
                for j in range(4):
                    S.op("scalar", lambda e, j=j: e.activation(
                        out=dp_t(j), in_=u_t(j), func=AF.Exp), inc="SE" if j == 3 else None)
                m_dp = S.now("SE")
                S.wait("vector", "SE", m_dp)
                for j, h in enumerate(heads):
                    P1j = u_t(j)
                    S.op("vector", lambda e, P1=P1j, j=j: e.tensor_mul(
                        P1, psF[2][:, j * 128:(j + 1) * 128], dp_t(j)))
                    S.op("vector", lambda e, P1=P1j, j=j, k=k, h=h: e.tensor_scalar(
                        sp_t(0, j), P1, t_dtt[:, k * 8 + h:k * 8 + h + 1], None, ALU.mult))
                    S.op("vector", lambda e, P1=P1j, j=j, k=k, h=h: e.scalar_tensor_tensor(
                        sp_t(1, j), P1, t_dtt[:, k * 8 + h:k * 8 + h + 1], sp_t(0, j),
                        ALU.mult, ALU.subtract), inc="VE" if j == 3 else None)
                m_sp = S.now("VE")
                if _SLVL < 4:
                    continue
                S.wait("scalar", "TE", m_w1)
                for j in (0, 1):
                    S.op("scalar", lambda e, j=j: e.copy(
                        out=pool[:, o_xtk + j * 1024:o_xtk + j * 1024 + 1024].bitcast(BF16),
                        in_=psT[:, j * 512:j * 512 + 512]), inc="SE" if j == 1 else None)
                m_xe1 = S.now("SE")
                S.wait("tensor", "SE", m_xe1)
                for j in (2, 3):
                    h = heads[j]
                    for hl, src in ((0, xph_), (1, xpl_)):
                        for pc in range(2):
                            ci = h * 2 + pc
                            S.op("tensor", lambda e, s=src, ci=ci, j=j, hl=hl, pc=pc:
                                 e.transpose(psT[:, (j - 2) * 512 + hl * 256 + pc * 128:
                                                 (j - 2) * 512 + hl * 256 + pc * 128 + 128],
                                             s[:, ci * Q:(ci + 1) * Q], c_idb),
                                 inc="TE" if (j == 3 and hl == 1 and pc == 1) else None)
                m_w2 = S.now("TE")
                S.wait("scalar", "TE", m_w2)
                for j in (2, 3):
                    S.op("scalar", lambda e, j=j: e.copy(
                        out=pool[:, o_xtk + j * 1024:o_xtk + j * 1024 + 1024].bitcast(BF16),
                        in_=psT[:, (j - 2) * 512:(j - 2) * 512 + 512]),
                         inc="SE" if j == 3 else None)
                m_xe2 = S.now("SE")
                if _SLVL < 5:
                    continue
                S.wait("tensor", "VE", m_sp)
                S.wait("tensor", "SE", m_xe2)
                if k > 0 or half > 0 or True:
                    S.wait("tensor", "VE")
                for j, h in enumerate(heads):
                    hp, r0 = h // 2, (h % 2) * 64
                    bank = psF[3] if j < 2 else psF[1]
                    c0 = (j % 2) * 256
                    nmm = 0
                    for pc in range(2):
                        for lh, rh in ((xtk_h(j), sp_t(0, j)), (xtk_h(j), sp_t(1, j)),
                                       (xtk_l(j), sp_t(0, j))):
                            S.op("tensor", lambda e, lh=lh, rh=rh, pc=pc, c0=c0, bank=bank,
                                 fs=(nmm == 0), ls=False: e.matmul(
                                     bank[:, c0 + pc * 128:c0 + pc * 128 + 128],
                                     lh[:, pc * 128:pc * 128 + 128], rh,
                                     start=fs, stop=False))
                            nmm += 1
                    i = hp - 2 * half
                    for pc in range(2):
                        for lh, rh in ((t_sth, ctk(0, i)), (t_sth, ctk(1, i)),
                                       (t_stl, ctk(0, i))):
                            nmm += 1
                            S.op("tensor", lambda e, lh=lh, rh=rh, pc=pc, hp=hp, r0=r0,
                                 c0=c0, bank=bank, i=i, ls=(nmm == 12): e.matmul(
                                     bank[:, c0 + pc * 128:c0 + pc * 128 + 128],
                                     lh[r0:r0 + 64, hp * 256 + pc * 128:hp * 256 + pc * 128 + 128],
                                     rh[r0:r0 + 64, :], start=False, stop=ls),
                                 inc="TE" if (j == 3 and nmm == 12) else None)
                for i, hp in enumerate(hps):
                    for hh in range(2):
                        j = 2 * i + hh
                        r0 = hh * 64
                        cs_seq = [(bwt(0, i), xtk_h(j)), (bwt(0, i), xtk_l(j)),
                                  (bwt(1, i), xtk_h(j))]
                        for jj, (lh, rh) in enumerate(cs_seq):
                            S.op("tensor", lambda e, lh=lh, rh=rh, i=i, r0=r0, hh=hh,
                                 fs=(jj == 0), ls=(jj == 2): e.matmul(
                                     psF[4][r0:r0 + 64, i * 256:i * 256 + 256],
                                     lh[:, hh * 64:hh * 64 + 64], rh,
                                     start=fs, stop=ls),
                                 inc="TE" if (i == 1 and hh == 1 and jj == 2) else None)
                m_s3 = S.now("TE")
                if _SLVL < 6:
                    continue
                S.wait("vector", "TE", m_s3)
                S.wait("vector", "GE")
                for j, h in enumerate(heads):
                    bank = psF[3] if j < 2 else psF[1]
                    c0 = (j % 2) * 256
                    S.op("vector", lambda e, h=h, c0=c0, bank=bank, ymst=ymst:
                         e.scalar_tensor_tensor(
                             ymst[:, h * 256:(h + 1) * 256],
                             ap(o_xrc + h * 2 * Q * 4, 256, F32),
                             ap(o_dp + h * 2 * 4, 1, F32),
                             bank[:, c0:c0 + 256], ALU.mult, ALU.add))
                for i, hp in enumerate(hps):
                    S.op("vector", lambda e, hp=hp, k=k, i=i: e.scalar_tensor_tensor(
                        t_st[:, hp * 256:(hp + 1) * 256], t_st[:, hp * 256:(hp + 1) * 256],
                        t_dtc[:, hp * NCH + k:hp * NCH + k + 1],
                        psF[4][:, i * 256:i * 256 + 256], ALU.mult, ALU.add),
                         inc="VE" if i == 1 else None)
            else:
                m_s1 = S.mark("tensor")
                if m_s1:
                    S.wait("vector", m_s1[0], m_s1[1])
                for i, hp in enumerate(hps):
                    crec = ap(o_scr, 128, F32)
                    S.op("vector", lambda e, hp=hp, k=k, o=crec: e.tensor_add(
                        o, t_ch[:, hp * T + k * Q:hp * T + (k + 1) * Q],
                        t_cl[:, hp * T + k * Q:hp * T + (k + 1) * Q]))
                    S.op("vector", lambda e, o=crec, i=i: e.tensor_mul(
                        o, o, psF[0][:, i * 128:(i + 1) * 128]))
                    S.op("vector", lambda e, o=crec, i=i: e.tensor_copy(out=ctk(0, i), in_=o))
                    S.op("vector", lambda e, o=crec, i=i: e.tensor_sub(ctk(1, i), o, ctk(0, i)),
                         inc="VE" if i == 1 else None)
                m_ct = S.now("VE")
                S.wait("tensor", "VE", m_ct)
                for half_j, jset in ((0, (0, 2)), (1, (1, 3))):
                    for j in jset:
                        h = heads[j]
                        hp, r0 = h // 2, (h % 2) * 64
                        i = hp - 2 * half
                        bank = psF[3] if j < 2 else psF[1]
                        c0 = (j % 2) * 256
                        nmm = 0
                        for pc in range(2):
                            for lh, rh in ((t_sth, ctk(0, i)), (t_sth, ctk(1, i)),
                                           (t_stl, ctk(0, i))):
                                nmm += 1
                                S.op("tensor", lambda e, lh=lh, rh=rh, pc=pc, hp=hp, r0=r0,
                                     c0=c0, bank=bank, fs=(nmm == 1), ls=(nmm == 6):
                                     e.matmul(
                                         bank[:, c0 + pc * 128:c0 + pc * 128 + 128],
                                         lh[r0:r0 + 64, hp * 256 + pc * 128:
                                            hp * 256 + pc * 128 + 128],
                                         rh[r0:r0 + 64, :], start=fs, stop=ls))
                    S.op("tensor", lambda e: e.matmul(
                        psX[:, 0:NCH], c_selp[:, 0:128], t_eppf, start=True, stop=True),
                         inc="TE" if half_j == 1 else None)
                m_s3 = S.now("TE")
                S.wait("vector", "TE", m_s3)
                for j, h in enumerate(heads):
                    bank = psF[3] if j < 2 else psF[1]
                    c0 = (j % 2) * 256
                    for pc in range(2):
                        ci = h * 2 + pc
                        S.op("vector", lambda e, ci=ci, pc=pc, k=k, h=h, c0=c0, bank=bank,
                             ymst=ymst: e.scalar_tensor_tensor(
                                 ymst[:, ci * Q:(ci + 1) * Q],
                                 bank[:, c0 + pc * 128:c0 + pc * 128 + 128],
                                 t_epc[:, h * NCH + k:h * NCH + k + 1],
                                 ymst[:, ci * Q:(ci + 1) * Q], ALU.mult, ALU.add,
                                 accum_out=t_parts[:, k * CI + ci:k * CI + ci + 1]),
                             inc="VE" if (j == 3 and pc == 1) else None)
                m_ev = S.now("VE")
                S.wait("scalar", "VE", m_ev)
                for j, h in enumerate(heads):
                    for pc in range(2):
                        ci = h * 2 + pc
                        S.op("scalar", lambda e, ci=ci, k=k, ymst=ymst: e.activation(
                            out=ap(o_scr + 512, 128, F32), in_=ymst[:, ci * Q:(ci + 1) * Q],
                            func=AF.Square,
                            accum_out=t_parts[:, NCH * CI + k * CI + ci:
                                              NCH * CI + k * CI + ci + 1]),
                             inc="SE" if (j == 3 and pc == 1) else None)
        gp_gate[0] = ("VE", S.now("VE"))
        S.mark("vector")
        S.wait("gpsimd", "VV")
        S.wait("gpsimd", "VE")
        if corr:
            S.mark("scalar")
            S.wait("gpsimd", "SS")
            S.wait("gpsimd", "SE")
            scan_marks[("sq", k)] = S.now("SE")
        S.dma("gpsimd", lambda e, k=k, d=ymst: e.dma_start(out=sp_ym[k], in_=d), "SP")
        scan_marks[("wb", k)] = S.now("SP")
        if not corr:
            m = S.mark("tensor")
            scan_marks[("xte", k)] = m
            scan_marks[("xve", k)] = ("VE", S.now("VE"))


    _NLIM = int(_osd.environ.get("KDBG_NCHLIM", NCH))
    for k in range(min(NCH, _NLIM)):
        scan_chunk(k, corr=False)

    S.wait("gpsimd", "VE")
    for hp in range(CB):
        S.dma("gpsimd", lambda e, hp=hp: e.dma_start(
            out=cc_in.ap()[hp, :, 0:256], in_=t_st[:, hp * 256:(hp + 1) * 256]), "SP")
        S.dma("gpsimd", lambda e, hp=hp: e.dma_start(
            out=cc_in.ap()[hp, :, 256:257], in_=t_dsh[:, hp:hp + 1],
            allow_slow_non_contiguous=True), "SP")
    S.wait("gpsimd", "SP")
    if n_shards > 1:
        S.op("gpsimd", lambda e: e.collective_compute(
            "AllGather", ALU.bypass, replica_groups=groups,
            ins=[cc_in.ap().opt()], outs=[cc_out.ap().opt()]), inc="CC")
        S.wait("gpsimd", "CC")
    else:
        S.dma("gpsimd", lambda e: e.dma_start(out=cc_out.ap()[0], in_=cc_in.ap()[:, :, :]),
              "SP")
        S.wait("gpsimd", "SP")
    A.mark("right")
    o_gth = A.alloc(n_shards * CB * 257 * 4, "right")
    o_pfx = A.alloc(2 * CB * 1024, "right")
    t_gth = ap(o_gth, n_shards * CB * 257, F32)
    for m_ in range(n_shards):
        for hp_ in range(CB):
            S.dma("gpsimd", lambda e, m_=m_, hp_=hp_: e.dma_start(
                out=t_gth[:, (m_ * CB + hp_) * 257:(m_ * CB + hp_ + 1) * 257],
                in_=cc_out.ap()[m_, hp_]), "SP")
    S.wait("vector", "SP")

    def gs(m, hp):
        return ap(o_gth + (m * CB + hp) * 257 * 4, 256, F32)

    def gd(m, hp):
        return ap(o_gth + ((m * CB + hp) * 257 + 256) * 4, 1, F32)

    def pb_(par, hp):
        return ap(o_pfx + (par * CB + hp) * 1024, 256, F32)

    for hp in range(CB):
        S.op("vector", lambda e, hp=hp: e.tensor_copy(out=pb_(0, hp), in_=gs(0, hp)))
        S.edge("vector")
        S.op("vector", lambda e, hp=hp: e.tensor_scalar(
            t_st[:, hp * 256:(hp + 1) * 256], pb_(0, hp), c_sel[:, 0:1], None, ALU.mult))
        S.edge("vector")
        for m in range(1, n_shards):
            S.op("vector", lambda e, hp=hp, m=m: e.scalar_tensor_tensor(
                pb_(m % 2, hp), pb_((m - 1) % 2, hp), gd(m, hp), gs(m, hp),
                ALU.mult, ALU.add))
            S.edge("vector")
            S.op("vector", lambda e, hp=hp, m=m: e.scalar_tensor_tensor(
                t_st[:, hp * 256:(hp + 1) * 256], pb_(m % 2, hp), c_sel[:, m:m + 1],
                t_st[:, hp * 256:(hp + 1) * 256], ALU.mult, ALU.add))
            S.edge("vector")
    S.op("vector", lambda e: e.tensor_copy(out=t_sth, in_=t_st))
    S.edge("vector")
    S.op("vector", lambda e: e.tensor_sub(t_stl, t_st, t_sth), inc="VE")
    A.release("right")

    S.barrier()
    import os as _os
    if not _os.environ.get("KDBG_NO_CORR"):
        for k in range(min(NCH, _NLIM)):
            scan_chunk(k, corr=True)
    S.barrier()
    A.release("right")

    A.mark("right")
    pview = t_parts.rearrange("p (g k c) -> p g k c", g=2, c=CI)
    for g in range(2):
        for h in range(NH):
            S.op("vector", lambda e, g=g, h=h: e.reduce_sum(
                t_tiny[:, 32 + g * 8 + h:32 + g * 8 + h + 1],
                pview[:, g, :, 2 * h:2 * h + 2], axis=AX.XY),
                 inc="VE" if (g == 1 and h == NH - 1) else None)
    S.wait("tensor", "VE")
    S.op("tensor", lambda e: e.matmul(psX[0:1, 0:16], c_o128, t_tiny[:, 32:48],
                                      start=True, stop=True), inc="TE")
    S.wait("vector", "TE")
    S.op("vector", lambda e: e.tensor_copy(out=t_tiny[0:1, 0:16], in_=psX[0:1, 0:16]),
         inc="VE")
    S.wait("gpsimd", "VE")
    S.dma("gpsimd", lambda e: e.dma_start(out=ar_in.ap()[:, :], in_=t_tiny[0:1, 0:16]), "SP")
    S.wait("gpsimd", "SP")
    if n_shards > 1:
        S.op("gpsimd", lambda e: e.collective_compute(
            "AllReduce", ALU.add, replica_groups=groups,
            ins=[ar_in.ap().opt()], outs=[ar_out.ap().opt()]), inc="CC")
        S.wait("gpsimd", "CC")
    else:
        S.dma("gpsimd", lambda e: e.dma_start(out=ar_out.ap()[:, :], in_=ar_in.ap()[:, :]),
              "SP")
        S.wait("gpsimd", "SP")
    S.dma("gpsimd", lambda e: e.dma_start(out=t_tiny[0:1, 16:32], in_=ar_out.ap()[:, :]),
          "SP")
    S.wait("tensor", "SP")
    S.op("tensor", lambda e: e.transpose(psX[0:8, 0:1], t_tiny[0:1, 16:24], c_idf[0:1, 0:1]))
    S.op("tensor", lambda e: e.transpose(psX[0:8, 1:2], t_tiny[0:1, 24:32], c_idf[0:1, 0:1]),
         inc="TE")
    S.wait("vector", "TE")
    S.op("vector", lambda e: e.tensor_scalar(t_tiny[0:8, 48:49], psX[0:8, 0:1],
                                             1.0 / CNT_GN, None, ALU.mult))
    S.op("vector", lambda e: e.tensor_scalar(t_tiny[0:8, 47:48], psX[0:8, 1:2],
                                             1.0 / CNT_GN, None, ALU.mult))
    S.op("vector", lambda e: e.tensor_mul(t_tiny[0:8, 49:50], t_tiny[0:8, 48:49],
                                          t_tiny[0:8, 48:49]))
    S.op("vector", lambda e: e.tensor_sub(t_tiny[0:8, 50:51], t_tiny[0:8, 47:48],
                                          t_tiny[0:8, 49:50]), inc="VE")
    S.wait("scalar", "VE")
    S.op("scalar", lambda e: e.activation(out=t_tiny[0:8, 54:55], in_=t_tiny[0:8, 50:51],
                                          func=AF.Sqrt, bias=c_eps8), inc="SE")
    S.wait("vector", "SE")
    S.op("vector", lambda e: e.reciprocal(t_tiny[0:8, 51:52], t_tiny[0:8, 54:55]))
    S.edge("vector")
    S.op("vector", lambda e: e.tensor_mul(t_tiny[0:8, 52:53], t_tiny[0:8, 48:49],
                                          t_tiny[0:8, 51:52]))
    S.edge("vector")
    S.op("vector", lambda e: e.tensor_scalar(t_tiny[0:8, 53:54], t_tiny[0:8, 52:53],
                                             -1.0, None, ALU.mult))
    S.edge("vector")
    S.op("vector", lambda e: e.tensor_scalar(t_tiny[0:8, 56:56 + CI], c_mci,
                                             t_tiny[0:8, 51:52], None, ALU.mult))
    S.edge("vector")
    S.op("vector", lambda e: e.tensor_scalar(t_tiny[0:8, 56 + CI:56 + 2 * CI], c_mci,
                                             t_tiny[0:8, 53:54], None, ALU.mult), inc="VE")
    S.wait("tensor", "VE")
    S.op("tensor", lambda e: e.matmul(psX[:, 0:CI], c_o8, t_tiny[0:8, 56:56 + CI],
                                      start=True, stop=True))
    S.op("tensor", lambda e: e.matmul(psX[:, CI:2 * CI], c_o8,
                                      t_tiny[0:8, 56 + CI:56 + 2 * CI],
                                      start=True, stop=True), inc="TE")
    S.wait("vector", "TE")
    S.op("vector", lambda e: e.tensor_mul(t_stc[:, 0:CI], psX[:, 0:CI], c_gnw))
    S.edge("vector")
    S.op("vector", lambda e: e.tensor_mul(t_tiny[:, 60:60 + CI], psX[:, CI:2 * CI], c_gnw))
    S.edge("vector")
    S.op("vector", lambda e: e.tensor_add(t_stc[:, CI:2 * CI], t_tiny[:, 60:60 + CI],
                                          c_gnb), inc="VE")
    S.barrier()
    A.release("right")
    A.release("left")

    A.mark("left")
    o_wrh = sb(CD * D * 2); o_wrl = sb(CD * D * 2)
    o_wvh = sb(CD * D * 2); o_wvl = sb(CD * D * 2)
    o_wkh = sb(CD * CD * 256); o_wkl = sb(CD * CD * 256)
    t_wrh = ap(o_wrh, CD * D, BF16); t_wrl = ap(o_wrl, CD * D, BF16)
    t_wvh = ap(o_wvh, CD * D, BF16); t_wvl = ap(o_wvl, CD * D, BF16)
    t_wkh = ap(o_wkh, CD * CD * 128, BF16); t_wkl = ap(o_wkl, CD * CD * 128, BF16)
    A.mark("right")
    o_wst = A.alloc(D * 4, "right")
    o_xnt = A.alloc(2 * CD * 257 * 2, "right")
    o_xnr = A.alloc(CD * 257 * 4, "right")
    o_mx2 = A.alloc(2 * FT * 4, "right")
    o_xrh = A.alloc(CD * FT * 2, "right"); o_xrl = A.alloc(CD * FT * 2, "right")
    o_xkh = A.alloc(CD * FT * 2, "right"); o_xkl = A.alloc(CD * FT * 2, "right")
    o_kq = A.alloc(FT * 4, "right")
    o_kqh = A.alloc(CD * FT * 2, "right"); o_kql = A.alloc(CD * FT * 2, "right")
    o_rt = A.alloc((FT // 128) * D * 4, "right")
    o_ffo = A.alloc(D * 4, "right")
    o_xr2 = A.alloc((FT // 128) * D * 4, "right")
    wst = ap(o_wst, D, F32)

    for (dsrc, th_, tl_, isp) in ((dwr, t_wrh, t_wrl, False), (dwv, t_wvh, t_wvl, False),
                                  (dwk, t_wkh, t_wkl, True)):
        for oc in range(CD):
            S.wait("sync", "VE")
            par = newld()
            if isp:
                S.dma("sync", lambda e, oc=oc, s=dsrc: e.dma_start(
                    out=wst[:, 0:CD * 128].rearrange("p (c q) -> p c q", q=128),
                    in_=s[oc].rearrange("c p q -> p c q")), par)
            else:
                S.dma("sync", lambda e, oc=oc, s=dsrc: e.dma_start(
                    out=wst, in_=s[oc]), par)
            S.wait("vector", par)
            S.op("vector", lambda e, oc=oc, d=th_: e.tensor_copy(
                out=d[:, oc * D:(oc + 1) * D], in_=wst))
            S.op("vector", lambda e, oc=oc, dh=th_, dl=tl_: e.tensor_sub(
                dl[:, oc * D:(oc + 1) * D], wst, dh[:, oc * D:(oc + 1) * D]), inc="VE")

    nft = T // FT
    ve_gf = []
    gfi = [0]
    for tt in range(nft):
        c0 = 2 + tt * FT
        xnth2 = ap(o_xnt, CD * 257, BF16)
        xntl2 = ap(o_xnt + CD * 257 * 2, CD * 257, BF16)
        xnr = ap(o_xnr, CD * 257, F32)
        xrh = ap(o_xrh, CD * FT, BF16); xrl = ap(o_xrl, CD * FT, BF16)
        xkh = ap(o_xkh, CD * FT, BF16); xkl = ap(o_xkl, CD * FT, BF16)
        kq = ap(o_kq, FT, F32)
        kqh = ap(o_kqh, CD * FT, BF16); kql = ap(o_kql, CD * FT, BF16)
        rt = ap(o_rt, (FT // 128) * D, F32)
        ffo = ap(o_ffo, D, F32)
        S.qbar("sync", ("tensor", "vector"))
        par = newld()
        for cc in range(CD):
            S.dma("sync", lambda e, cc=cc, c0=c0, d=xnth2: e.dma_start(
                out=d[:, cc * 257:(cc + 1) * 257],
                in_=sp_xnh[:, cc * TH + c0:cc * TH + c0 + FT + 1]), par)
            S.dma("sync", lambda e, cc=cc, c0=c0, d=xntl2: e.dma_start(
                out=d[:, cc * 257:(cc + 1) * 257],
                in_=sp_xnl[:, cc * TH + c0:cc * TH + c0 + FT + 1]), par)
        for tb in range(FT // 128):
            S.dma("gpsimd", lambda e, tt=tt, tb=tb: e.dma_start(
                out=ap(o_xr2 + tb * D * 4, D, F32),
                in_=dx[tt * FT + tb * 128:tt * FT + tb * 128 + 128, :]), "SP")
        S.wait("vector", par)
        S.wait("vector", "SP")
        S.op("vector", lambda e: e.tensor_add(xnr, xnth2, xntl2))
        for cc in range(CD):
            for si, (ho, lo_, mu, mu1) in enumerate(((xrh, xrl, o_mur, o_mur1),
                                                     (xkh, xkl, o_muk, o_muk1))):
                mxf = ap(o_mx2 + si * FT * 4, FT, F32)
                S.op("vector", lambda e, cc=cc, d=mxf, mu1=mu1: e.tensor_scalar(
                    d, xnr[:, cc * 257:cc * 257 + FT],
                    ap(mu1 + cc * 4, 1, F32), None, ALU.mult))
                S.op("vector", lambda e, cc=cc, d=mxf, mu=mu: e.scalar_tensor_tensor(
                    d, xnr[:, cc * 257 + 1:cc * 257 + FT + 1],
                    ap(mu + cc * 4, 1, F32), d, ALU.mult, ALU.add))
                S.op("vector", lambda e, cc=cc, d=mxf, h=ho: e.tensor_copy(
                    out=h[:, cc * FT:(cc + 1) * FT], in_=d))
                S.op("vector", lambda e, cc=cc, d=mxf, h=ho, l=lo_: e.tensor_sub(
                    l[:, cc * FT:(cc + 1) * FT], d, h[:, cc * FT:(cc + 1) * FT]),
                     inc="VE" if (cc == CD - 1 and si == 1) else None)
        S.wait("tensor", "VE")
        for oc in range(CD):
            g = gfi[0]; gfi[0] += 1
            pb = psF[g % 4]
            if g >= 4:
                S.wait("tensor", ve_gf[g - 4][0], ve_gf[g - 4][1])
            seq = [(a, b, cc) for a, b in ((t_wkh, xkh), (t_wkh, xkl), (t_wkl, xkh))
                   for cc in range(CD)]
            for i, (wop, aop, cc) in enumerate(seq):
                S.op("tensor", lambda e, p=pb, wop=wop, aop=aop, oc=oc, cc=cc,
                     fs=(i == 0), ls=(i == len(seq) - 1): e.matmul(
                         p[:, 0:FT], wop[:, (oc * CD + cc) * 128:(oc * CD + cc) * 128 + 128],
                         aop[:, cc * FT:(cc + 1) * FT], start=fs, stop=ls),
                     inc="TE" if i == len(seq) - 1 else None)
            S.wait("vector", "TE")
            S.op("vector", lambda e, p=pb: e.tensor_scalar_max(kq, p[:, 0:FT], 0.0),
                 inc="VE")
            ve_gf.append(("VE", S.now("VE")))
            S.wait("scalar", "VE")
            S.op("scalar", lambda e: e.activation(
                out=pool[:, o_kq:o_kq + FT * 4].bitcast(F32), in_=kq, func=AF.Square))
            S.op("scalar", lambda e, oc=oc: e.copy(
                out=kqh[:, oc * FT:(oc + 1) * FT], in_=kq), inc="SE")
            S.wait("vector", "SE")
            S.op("vector", lambda e, oc=oc: e.tensor_sub(
                kql[:, oc * FT:(oc + 1) * FT], kq, kqh[:, oc * FT:(oc + 1) * FT]),
                 inc="VE")
        for tb in range(FT // 128):
            for half in range(2):
                g = gfi[0]; gfi[0] += 1
                pb = psF[g % 4]
                if g >= 4:
                    S.wait("tensor", ve_gf[g - 4][0], ve_gf[g - 4][1])
                seq = [(a, b, cc) for a, b in ((xrh, t_wrh), (xrh, t_wrl), (xrl, t_wrh))
                       for cc in range(CD)]
                for i, (aop, wop, cc) in enumerate(seq):
                    S.op("tensor", lambda e, p=pb, aop=aop, wop=wop, cc=cc, tb=tb,
                         half=half, fs=(i == 0), ls=(i == len(seq) - 1): e.matmul(
                             p[:, 0:512],
                             aop[:, cc * FT + tb * 128:cc * FT + tb * 128 + 128],
                             wop[:, cc * D + half * 512:cc * D + half * 512 + 512],
                             start=fs, stop=ls),
                         inc="TE" if i == len(seq) - 1 else None)
                S.wait("scalar", "TE")
                S.op("scalar", lambda e, p=pb, tb=tb, half=half: e.activation(
                    out=rt[:, tb * D + half * 512:tb * D + half * 512 + 512],
                    in_=p[:, 0:512], func=AF.Sigmoid), inc="SE")
                ve_gf.append(("SE", S.now("SE")))
        S.wait("tensor", "SE")
        S.wait("tensor", "VE")
        for tb in range(FT // 128):
            for half in range(2):
                g = gfi[0]; gfi[0] += 1
                pb = psF[g % 4]
                if g >= 4:
                    S.wait("tensor", ve_gf[g - 4][0], ve_gf[g - 4][1])
                seq = [(a, b, cc) for a, b in ((kqh, t_wvh), (kqh, t_wvl), (kql, t_wvh))
                       for cc in range(CD)]
                for i, (aop, wop, cc) in enumerate(seq):
                    S.op("tensor", lambda e, p=pb, aop=aop, wop=wop, cc=cc, tb=tb,
                         half=half, fs=(i == 0), ls=(i == len(seq) - 1): e.matmul(
                             p[:, 0:512],
                             aop[:, cc * FT + tb * 128:cc * FT + tb * 128 + 128],
                             wop[:, cc * D + half * 512:cc * D + half * 512 + 512],
                             start=fs, stop=ls),
                         inc="TE" if i == len(seq) - 1 else None)
                S.wait("vector", "TE")
                S.op("vector", lambda e, p=pb, tb=tb, half=half: e.tensor_mul(
                    ffo[:, half * 512:half * 512 + 512], p[:, 0:512],
                    rt[:, tb * D + half * 512:tb * D + half * 512 + 512]))
                S.op("vector", lambda e, tb=tb, half=half: e.tensor_add(
                    ffo[:, half * 512:half * 512 + 512],
                    ffo[:, half * 512:half * 512 + 512],
                    ap(o_xr2 + tb * D * 4, D, F32)[:, half * 512:half * 512 + 512]),
                     inc="VE")
                ve_gf.append(("VE", S.now("VE")))
            S.wait("gpsimd", "VE")
            S.dma("gpsimd", lambda e, tt=tt, tb=tb: e.dma_start(
                out=sp_ffn[tt * FT + tb * 128:tt * FT + tb * 128 + 128, :], in_=ffo),
                "SP")
            S.wait("vector", "SP")
    S.barrier()
    A.release("right")
    A.release("left")

    A.mark("right")
    o_wz = A.alloc(CI * CD * 256, "right")
    o_wzl = A.alloc(CI * CD * 256, "right")
    o_wo = A.alloc(CI * D * 2, "right")
    o_wol = A.alloc(CI * D * 2, "right")
    o_wst3 = A.alloc(CD * 512, "right")
    o_wst4 = A.alloc(D * 4, "right")
    o_xnt3 = A.alloc(2 * 2 * CD * 256, "right")
    o_ymt3 = A.alloc(2 * CI * Q * 4, "right")
    o_zt = A.alloc(2 * Q * 4, "right")
    o_yt = A.alloc(2 * Q * 4, "right")
    o_mtg = A.alloc(2 * 2 * CI * 256, "right")
    o_ffr = A.alloc(2 * D * 4, "right")
    t_wzh = ap(o_wz, CI * CD * 128, BF16)
    t_wzl = ap(o_wzl, CI * CD * 128, BF16)
    t_woh = ap(o_wo, CI * D, BF16)
    t_wol = ap(o_wol, CI * D, BF16)
    wst3 = ap(o_wst3, CD * 128, F32)
    wst4 = ap(o_wst4, D, F32)

    wz_ve = {}
    wo_ve = {}
    for i in range(CI):
        S.wait("scalar", "VE")
        parz = newls()
        S.dma("scalar", lambda e, oc=i: e.dma_start(
            out=wst3.rearrange("p (c q) -> p c q", q=128),
            in_=dwz[oc].rearrange("c p q -> p c q")), parz)
        S.wait("vector", parz)
        S.op("vector", lambda e, oc=i: e.tensor_copy(
            out=t_wzh[:, oc * CD * 128:(oc + 1) * CD * 128], in_=wst3))
        S.op("vector", lambda e, oc=i: e.tensor_sub(
            t_wzl[:, oc * CD * 128:(oc + 1) * CD * 128], wst3,
            t_wzh[:, oc * CD * 128:(oc + 1) * CD * 128]), inc="VE")
        wz_ve[i] = S.now("VE")
        S.wait("gpsimd", "VE")
        S.dma("gpsimd", lambda e, cc=i: e.dma_start(out=wst4, in_=dwout[cc]), "SP")
        S.wait("vector", "SP")
        S.op("vector", lambda e, cc=i: e.tensor_copy(
            out=t_woh[:, cc * D:(cc + 1) * D], in_=wst4))
        S.op("vector", lambda e, cc=i: e.tensor_sub(
            t_wol[:, cc * D:(cc + 1) * D], wst4,
            t_woh[:, cc * D:(cc + 1) * D]), inc="VE")
        wo_ve[i] = S.now("VE")

    ntl3 = NCH
    te_wz = {}
    te_out = {}
    ve_mix = {}
    ve_out = {}
    sp_out = {}
    pf_par = {}
    se_z = {}
    ve_oc = {}
    gi3 = [0]

    def sl3(t):
        sl = t % 2
        return (ap(o_xnt3 + sl * 2 * CD * 256, CD * 128, BF16),
                ap(o_xnt3 + sl * 2 * CD * 256 + CD * 256, CD * 128, BF16),
                ap(o_ymt3 + sl * CI * Q * 4, CI * Q, F32),
                ap(o_ffr + sl * D * 4, D, F32),
                ap(o_mtg + sl * 2 * CI * 256, CI * 128, BF16),
                ap(o_mtg + sl * 2 * CI * 256 + CI * 256, CI * 128, BF16))

    def pf3(t):
        if t >= ntl3:
            return
        xnh3, xnl3, ymt, ffr, _, _ = sl3(t)
        if t >= 2:
            S.wait("sync", "TE", te_wz[t - 2])
            S.wait("sync", "VE", ve_mix[t - 2])
            S.wait("scalar", "SP", sp_out[t - 2])
        par = newld()
        for cc in range(CD):
            S.dma("sync", lambda e, cc=cc, t=t, d=xnh3: e.dma_start(
                out=d[:, cc * 128:(cc + 1) * 128],
                in_=sp_xnh[:, cc * TH + 3 + t * Q:cc * TH + 3 + (t + 1) * Q]), par)
            S.dma("sync", lambda e, cc=cc, t=t, d=xnl3: e.dma_start(
                out=d[:, cc * 128:(cc + 1) * 128],
                in_=sp_xnl[:, cc * TH + 3 + t * Q:cc * TH + 3 + (t + 1) * Q]), par)
        S.dma("sync", lambda e, t=t, d=ymt: e.dma_start(out=d, in_=sp_ym[t]), par)
        pars = newls()
        S.dma("scalar", lambda e, t=t, d=ffr: e.dma_start(
            out=d, in_=sp_ffn[t * Q:(t + 1) * Q, :]), pars)
        pf_par[t] = (par, pars)

    pf3(0)
    for t in range(ntl3 + 1):
        if t >= 1:
            tp = t - 1
            _, _, _, ffrp, mghp, mglp = sl3(tp)
            S.wait("tensor", "VE", ve_mix[tp])
            if tp == 0:
                S.wait("tensor", "VE", wo_ve[CI - 1])
            for half in range(2):
                pb = psF[4 + half]
                if tp >= 1:
                    S.wait("tensor", "VE", ve_out[tp - 1])
                seq = [(a, b, cc) for a, b in ((mghp, t_woh), (mghp, t_wol),
                                               (mglp, t_woh)) for cc in range(CI)]
                for i, (aop, wop, cc) in enumerate(seq):
                    S.op("tensor", lambda e, p=pb, aop=aop, wop=wop, cc=cc, half=half,
                         fs=(i == 0), ls=(i == len(seq) - 1): e.matmul(
                             p[:, 0:512], aop[:, cc * 128:(cc + 1) * 128],
                             wop[:, cc * D + half * 512:cc * D + half * 512 + 512],
                             start=fs, stop=ls),
                         inc="TE" if i == len(seq) - 1 else None)
            te_out[tp] = S.now("TE")
            S.wait("vector", "TE", te_out[tp])
            S.wait("vector", pf_par[tp][0])
            S.wait("vector", pf_par[tp][1])
            for half in range(2):
                S.op("vector", lambda e, p=psF[4 + half], d=ffrp, half=half:
                     e.tensor_add(d[:, half * 512:half * 512 + 512], p[:, 0:512],
                                  d[:, half * 512:half * 512 + 512]),
                     inc="VE" if half == 1 else None)
            ve_out[tp] = S.now("VE")
            S.wait("gpsimd", "VE", ve_out[tp])
            S.dma("gpsimd", lambda e, tp=tp, d=ffrp: e.dma_start(
                out=dout[tp * Q:(tp + 1) * Q, :], in_=d), "SP")
            sp_out[tp] = S.now("SP")
        if t >= ntl3:
            continue
        xnh3, xnl3, ymt, ffr, mgh, mgl = sl3(t)
        pf3(t + 1)
        S.wait("tensor", pf_par[t][0])
        first_mix = True
        for oc in range(CI):
            g = gi3[0]
            gi3[0] += 1
            pb = psF[g % 4]
            if t == 0:
                S.wait("tensor", "VE", wz_ve[oc])
            if g >= 4:
                S.wait("tensor", "SE", se_z[g - 4])
            seq = [(a, b, cc) for a, b in ((t_wzh, xnh3), (t_wzh, xnl3), (t_wzl, xnh3))
                   for cc in range(CD)]
            for i, (wop, aop, cc) in enumerate(seq):
                S.op("tensor", lambda e, p=pb, wop=wop, aop=aop, oc=oc, cc=cc,
                     fs=(i == 0), ls=(i == len(seq) - 1): e.matmul(
                         p[:, 0:Q],
                         wop[:, (oc * CD + cc) * 128:(oc * CD + cc) * 128 + 128],
                         aop[:, cc * 128:(cc + 1) * 128], start=fs, stop=ls),
                     inc="TE" if i == len(seq) - 1 else None)
            if oc == CI - 1:
                te_wz[t] = S.now("TE")
            zt = ap(o_zt + (g % 2) * Q * 4, Q, F32)
            yt = ap(o_yt + (g % 2) * Q * 4, Q, F32)
            S.wait("scalar", "TE")
            if g >= 2:
                S.wait("scalar", "VE", ve_oc[g - 2])
            S.op("scalar", lambda e, p=pb, d=zt: e.activation(
                out=d, in_=p[:, 0:Q], func=AF.Silu), inc="SE")
            se_z[g] = S.now("SE")
            S.wait("vector", "SE", se_z[g])
            if first_mix:
                S.wait("vector", pf_par[t][0])
                if t >= 2:
                    S.wait("vector", "TE", te_out[t - 2])
                first_mix = False
            S.op("vector", lambda e, oc=oc, d=yt, ymt=ymt: e.tensor_scalar(
                d, ymt[:, oc * Q:(oc + 1) * Q], ap(o_stc + oc * 4, 1, F32),
                ap(o_stc + (CI + oc) * 4, 1, F32), ALU.mult, ALU.add))
            S.op("vector", lambda e, d=yt, z=zt: e.tensor_mul(d, d, z))
            S.op("vector", lambda e, oc=oc, d=yt, m=mgh: e.tensor_copy(
                out=m[:, oc * 128:(oc + 1) * 128], in_=d))
            S.op("vector", lambda e, oc=oc, d=yt, mh=mgh, ml=mgl: e.tensor_sub(
                ml[:, oc * 128:(oc + 1) * 128], d, mh[:, oc * 128:(oc + 1) * 128]),
                 inc="VE")
            ve_oc[g] = S.now("VE")
        ve_mix[t] = ve_oc[gi3[0] - 1]
    S.wait("gpsimd", "SP")
    A.release("right")

    nc._dbg = dict(o_bh=o_bh, o_bl=o_bl,
                   o_ch=o_ch, o_cl=o_cl, o_st=o_st, o_parts=o_parts,
                   o_expdA=o_expdA, o_stc=o_stc, o_tiny=o_tiny,
                   
                   o_atn=o_atn, o_dtt=o_dtt, o_dwt=o_dwt, o_dtc=o_dtc,
                   o_epc=o_epc, o_dsh=o_dsh, o_eppf=o_eppf)

    with nc.Block() as block:
        def mk(eng_name):
            def go(e):
                for it in S.items[eng_name]:
                    if it[0] == "op":
                        ins = it[1](e)
                        if it[2] is not None:
                            ins.then_inc(S.handles[it[2]], it[3])
                    else:
                        e.wait_ge(S.handles[it[1]], it[2])
            return go
        block.sync(mk("sync"))
        block.tensor(mk("tensor"))
        block.vector(mk("vector"))
        block.scalar(mk("scalar"))
        block.gpsimd(mk("gpsimd"))
    es.close()
    return nc



def _pack(inputs, T, n_shards, n_batches):
    f32 = np.float32
    wx = np.ascontiguousarray(inputs["L0_attn_Wx"], f32)
    wz = np.ascontiguousarray(inputs["L0_attn_Wz"], f32)
    wb = np.ascontiguousarray(inputs["L0_attn_Wb"], f32)
    wc = np.ascontiguousarray(inputs["L0_attn_Wc"], f32)
    wdt = np.ascontiguousarray(inputs["L0_attn_Wdt"], f32)
    wout = np.ascontiguousarray(inputs["L0_attn_Wout"], f32)
    wr = np.ascontiguousarray(inputs["L0_ffn_W_r"], f32)
    wk = np.ascontiguousarray(inputs["L0_ffn_W_k"], f32)
    wv = np.ascontiguousarray(inputs["L0_ffn_W_v"], f32)
    import ml_dtypes
    shared = {
        "wxp": np.ascontiguousarray(wx.reshape(CI, 128, CD, 128).transpose(0, 2, 3, 1)),
        "wzp": np.ascontiguousarray(wz.reshape(CI, 128, CD, 128).transpose(0, 2, 3, 1)),
        "wbp": np.ascontiguousarray(wb.reshape(CB, 128, CI, 128).transpose(0, 2, 3, 1)),
        "wcp": np.ascontiguousarray(wc.reshape(CB, 128, CI, 128).transpose(0, 2, 3, 1)),
        "wdtp": np.ascontiguousarray(wdt.reshape(8, CI, 128).transpose(1, 2, 0)),
        "woutr": np.ascontiguousarray(wout.reshape(D, CI, 128).transpose(1, 2, 0)),
        "wrr": np.ascontiguousarray(wr.reshape(D, CD, 128).transpose(1, 2, 0)),
        "wvr": np.ascontiguousarray(wv.reshape(D, CD, 128).transpose(1, 2, 0)),
        "wkp": np.ascontiguousarray(wk.reshape(CD, 128, CD, 128).transpose(0, 2, 3, 1)),
        "convw": np.ascontiguousarray(
            np.asarray(inputs["L0_attn_conv_w"], f32)[:, 0, :]
            .reshape(CI, 128, KCV).transpose(1, 0, 2).reshape(128, CI * KCV)),
        "convb": np.ascontiguousarray(
            np.asarray(inputs["L0_attn_conv_b"], f32).reshape(CI, 128).T),
        "gnw": np.ascontiguousarray(
            np.asarray(inputs["L0_attn_gn_w"], f32).reshape(CI, 128).T),
        "gnb": np.ascontiguousarray(
            np.asarray(inputs["L0_attn_gn_b"], f32).reshape(CI, 128).T),
        "dprep": np.ascontiguousarray(
            np.repeat(np.asarray(inputs["L0_attn_d_param"], f32), PHD)
            .reshape(CI, 128).T),
        "murc": np.ascontiguousarray(
            np.asarray(inputs["L0_ffn_mu_r"], f32).reshape(CD, 128).T),
        "mukc": np.ascontiguousarray(
            np.asarray(inputs["L0_ffn_mu_k"], f32).reshape(CD, 128).T),
        "bdt": np.asarray(inputs["L0_attn_b_dt"], f32).reshape(8, 1),
        "loga": np.asarray(inputs["L0_attn_log_a"], f32).reshape(8, 1),
        "identb": np.eye(128, dtype=ml_dtypes.bfloat16),
        "identf": np.eye(128, dtype=f32),
        "maskneg": np.where(np.arange(128)[None, :] >= np.arange(128)[:, None],
                            np.float32(0.0), np.float32(NEG)),
        "ones8x128": np.ones((8, 128), f32),
        "ones128": np.ones((128, 1), f32),
        "maskci": (np.arange(CI)[None, :] // 2 == np.arange(8)[:, None]).astype(f32),
    }
    selp = np.zeros((8, CB * 128), f32)
    for pc in range(CB):
        for m in range(128):
            selp[2 * pc + (m >= 64), pc * 128 + m] = 1.0
    shared["selp"] = selp
    selh = np.zeros((8, NH * 128), f32)
    for h in range(NH):
        selh[h, h * 128:(h + 1) * 128] = 1.0
    shared["selh"] = selh

    x = np.ascontiguousarray(np.asarray(inputs["L0_attn_input_tensor"], f32))
    per_core = []
    for core in range(n_shards * n_batches):
        b, s = core // n_shards, core % n_shards
        m = dict(shared)
        m["x"] = np.ascontiguousarray(x[b, s * T:(s + 1) * T])
        if s == 0:
            m["xh3"] = np.zeros((3, D), f32)
        else:
            m["xh3"] = np.ascontiguousarray(x[b, s * T - 3:s * T])
        sel = np.zeros((128, 4), f32)
        if s > 0:
            sel[:, s - 1] = 1.0
        m["selshard"] = sel
        per_core.append(m)
    return per_core


def kernel(**inputs):
    B, L, _ = inputs["L0_attn_input_tensor"].shape
    n_batches = B
    n_shards = 8 // n_batches
    T = L // n_shards
    in_maps = _pack(inputs, T, n_shards, n_batches)
    nc = build_program(T, n_shards, n_batches)
    res = run_bass_kernel_spmd(nc, in_maps, core_ids=list(range(8)))
    out = np.empty((B, L, D), np.float32)
    for core in range(8):
        b, s = core // n_shards, core % n_shards
        out[b, s * T:(s + 1) * T] = res.results[core]["out"]
    return out


# version 23 (final)
# speedup vs baseline: 1.0547x; 1.0057x over previous
import numpy as np
from contextlib import ExitStack

import concourse.bass as bass
import concourse.mybir as mybir
from concourse.bass_utils import run_bass_kernel_spmd

F32 = mybir.dt.float32
BF16 = mybir.dt.bfloat16
U8 = mybir.dt.uint8
AF = mybir.ActivationFunctionType
ALU = mybir.AluOpType
AX = mybir.AxisListType

D = 1024
INNER = 2048
NH = 8
NST = 64
PHD = 256
KCV = 4
Q = 128
CD = D // 128
CI = INNER // 128
CB = (NH * NST) // 128
EPS = 1e-5
NEG = -1e30

ENGS = ("sync", "tensor", "vector", "scalar", "gpsimd")
SEMS = ("LD", "LDA", "LDB", "LDC", "LS", "LSA", "SP", "TE", "VE", "SE", "GE", "CC", "TT", "VV", "SS", "GG")
SELF_SEM = {"tensor": "TT", "vector": "VV", "scalar": "SS", "gpsimd": "GG",
            "sync": "VV"}


def _tiles(n, m=512):
    k = -(-n // m)
    base, rem = n // k, n % k
    out, off = [], 0
    for i in range(k):
        w = base + (1 if i < rem else 0)
        out.append((off, w))
        off += w
    return out


class Sched:

    def __init__(self):
        self.items = {e: [] for e in ENGS}
        self.cnt = {s: 0 for s in SEMS}
        self.handles = {}
        self.auto_edge = True

    def op(self, eng, fn, inc=None, amt=1):
        if self.auto_edge and eng in ("vector", "scalar", "gpsimd"):
            m = self.mark(eng)
            if m:
                self.items[eng].append(["wait", m[0], m[1], None])
        if inc is not None:
            self.cnt[inc] += amt
        self.items[eng].append(["op", fn, inc, amt])

    def dma(self, eng, fn, inc):
        ae = self.auto_edge
        self.auto_edge = False
        self.op(eng, fn, inc, 16)
        self.auto_edge = ae

    def wait(self, eng, sem, val=None):
        v = self.cnt[sem] if val is None else val
        if v > 0:
            self.items[eng].append(["wait", sem, v, None])

    def now(self, sem):
        return self.cnt[sem]

    def mark(self, eng):
        for it in reversed(self.items[eng]):
            if it[0] == "op":
                if it[2] is None:
                    s = SELF_SEM[eng]
                    it[2], it[3] = s, 1
                    self.cnt[s] += 1
                return (it[2], self.cnt[it[2]])
        return None

    def edge(self, eng):
        m = self.mark(eng)
        if m:
            self.wait(eng, m[0], m[1])

    def qbar(self, eng, others=("tensor", "vector", "scalar", "gpsimd")):
        for o in others:
            if o == eng:
                continue
            m = self.mark(o)
            if m:
                self.wait(eng, m[0], m[1])

    def barrier(self, engs=ENGS):
        marks = [m for m in (self.mark(e) for e in engs) if m]
        for e in engs:
            for sem, v in marks:
                self.wait(e, sem, v)


class Arena:
    def __init__(self, size):
        self.size = size
        self.left = 0
        self.right = size
        self.lstack = []
        self.rstack = []
        self.peak = 0

    def alloc(self, nbytes, side="left", align=64):
        if side == "left":
            off = -(-self.left // align) * align
            self.left = off + nbytes
        else:
            off = (self.right - nbytes) // align * align
            self.right = off
        assert self.left <= self.right, (
            f"SBUF arena overflow: left={self.left} right={self.right}")
        assert self.left <= self.right, (
            f"SBUF pool overflow: left={self.left} right={self.right}")
        self.peak = max(self.peak, self.left + (self.size - self.right))
        return off

    def mark(self, side="right"):
        st = self.lstack if side == "left" else self.rstack
        st.append(self.left if side == "left" else self.right)

    def release(self, side="right"):
        if side == "left":
            self.left = self.lstack.pop()
        else:
            self.right = self.rstack.pop()


def build_program(T, n_shards, n_batches, part_bytes=200704):
    NCH = T // Q
    TH = T + 3
    CNT_GN = float(n_shards * T * PHD)
    FT = min(256, T)

    nc = bass.Bass("TRN2", target_bir_lowering=False, debug=False)

    dx = nc.dram_tensor("x", [T, D], F32, kind="ExternalInput").ap()
    dxh = nc.dram_tensor("xh3", [3, D], F32, kind="ExternalInput").ap()
    dwx = nc.dram_tensor("wxp", [CI, CD, 128, 128], F32, kind="ExternalInput").ap()
    dwz = nc.dram_tensor("wzp", [CI, CD, 128, 128], F32, kind="ExternalInput").ap()
    dwb = nc.dram_tensor("wbp", [CB, CI, 128, 128], F32, kind="ExternalInput").ap()
    dwc = nc.dram_tensor("wcp", [CB, CI, 128, 128], F32, kind="ExternalInput").ap()
    dwdt = nc.dram_tensor("wdtp", [CI, 128, 8], F32, kind="ExternalInput").ap()
    dwout = nc.dram_tensor("woutr", [CI, 128, D], F32, kind="ExternalInput").ap()
    dwr = nc.dram_tensor("wrr", [CD, 128, D], F32, kind="ExternalInput").ap()
    dwv = nc.dram_tensor("wvr", [CD, 128, D], F32, kind="ExternalInput").ap()
    dwk = nc.dram_tensor("wkp", [CD, CD, 128, 128], F32, kind="ExternalInput").ap()
    dconvw = nc.dram_tensor("convw", [128, CI * KCV], F32, kind="ExternalInput").ap()
    dconvb = nc.dram_tensor("convb", [128, CI], F32, kind="ExternalInput").ap()
    dgnw = nc.dram_tensor("gnw", [128, CI], F32, kind="ExternalInput").ap()
    dgnb = nc.dram_tensor("gnb", [128, CI], F32, kind="ExternalInput").ap()
    ddp = nc.dram_tensor("dprep", [128, CI], F32, kind="ExternalInput").ap()
    dmur = nc.dram_tensor("murc", [128, CD], F32, kind="ExternalInput").ap()
    dmuk = nc.dram_tensor("mukc", [128, CD], F32, kind="ExternalInput").ap()
    dbdt = nc.dram_tensor("bdt", [8, 1], F32, kind="ExternalInput").ap()
    dloga = nc.dram_tensor("loga", [8, 1], F32, kind="ExternalInput").ap()
    didb = nc.dram_tensor("identb", [128, 128], BF16, kind="ExternalInput").ap()
    didf = nc.dram_tensor("identf", [128, 128], F32, kind="ExternalInput").ap()
    dmaskn = nc.dram_tensor("maskneg", [128, 128], F32, kind="ExternalInput").ap()
    dselp = nc.dram_tensor("selp", [8, CB * 128], F32, kind="ExternalInput").ap()
    dselh = nc.dram_tensor("selh", [8, NH * 128], F32, kind="ExternalInput").ap()
    do8 = nc.dram_tensor("ones8x128", [8, 128], F32, kind="ExternalInput").ap()
    do128 = nc.dram_tensor("ones128", [128, 1], F32, kind="ExternalInput").ap()
    dmci = nc.dram_tensor("maskci", [8, CI], F32, kind="ExternalInput").ap()
    dsel = nc.dram_tensor("selshard", [128, 4], F32, kind="ExternalInput").ap()
    dout = nc.dram_tensor("out", [T, D], F32, kind="ExternalOutput").ap()

    sp_xnh = nc.dram_tensor("sp_xnh", [128, CD * TH], BF16).ap()
    sp_xnl = nc.dram_tensor("sp_xnl", [128, CD * TH], BF16).ap()
    sp_ym = nc.dram_tensor("sp_ym", [NCH, 128, CI * Q], F32).ap()
    sp_xph = nc.dram_tensor("sp_xph", [128, CI * T], BF16).ap()
    sp_xpl = nc.dram_tensor("sp_xpl", [128, CI * T], BF16).ap()
    sp_zsh = nc.dram_tensor("sp_zsh", [128, CI * T], BF16).ap()
    sp_zsl = nc.dram_tensor("sp_zsl", [128, CI * T], BF16).ap()
    sp_ffn = nc.dram_tensor("sp_ffn", [T, D], F32).ap()
    cc_in = nc.dram_tensor("cc_in", [CB, 128, 257], F32)
    cc_out = nc.dram_tensor("cc_out", [n_shards, CB, 128, 257], F32)
    dbg_t = nc.dram_tensor("dbg", [16, 128, 512], F32).ap()
    ar_in = nc.dram_tensor("ar_in", [1, 16], F32)
    ar_out = nc.dram_tensor("ar_out", [1, 16], F32)
    groups = [list(range(b * n_shards, (b + 1) * n_shards))
              for b in range(n_batches)]

    S = Sched()
    es = ExitStack()
    pool_h = es.enter_context(nc.sbuf_tensor("pool", [128, part_bytes], U8))
    pool = pool_h[:, :]
    psF = [es.enter_context(nc.psum_tensor(f"ps{i}", [128, 512], F32))
           for i in range(6)]
    psT = es.enter_context(nc.psum_tensor("psT", [128, 1024], BF16))
    psX = es.enter_context(nc.psum_tensor("psX", [128, 512], F32))
    for s in SEMS:
        S.handles[s] = es.enter_context(nc.semaphore(s))

    A = Arena(part_bytes)
    _ldpool = ["LD", "LDA", "LDB", "LDC"]
    _ldi = [0]
    _lspool = ["LS", "LSA"]
    _lsi = [0]

    def newld():
        s = _ldpool[_ldi[0] % 4]
        _ldi[0] += 1
        for q in ("sync", "gpsimd"):
            S.wait(q, s)
        return s

    def newls():
        s = _lspool[_lsi[0] % 2]
        _lsi[0] += 1
        S.wait("scalar", s)
        return s

    def ap(off, n, dt, p0=0, p=128):
        sz = 2 if dt == BF16 else 4
        return pool[p0:p0 + p, off:off + n * sz].bitcast(dt)

    def sb(n):
        return A.alloc(n, "left")
    o_idb = sb(256); o_idf = sb(512); o_mkn = sb(512)
    o_o8 = sb(512); o_o128 = sb(64); o_mci = sb(CI * 4); o_sel = sb(16)
    o_cvw = sb(CI * KCV * 4); o_cvb = sb(CI * 4); o_gnw = sb(CI * 4)
    o_gnb = sb(CI * 4); o_dp = sb(CI * 4)
    o_mur = sb(CD * 4); o_muk = sb(CD * 4); o_mur1 = sb(CD * 4); o_muk1 = sb(CD * 4)
    o_bdt = sb(64); o_loga = sb(64); o_aneg = sb(64)
    o_eppf = sb(max(NCH, 16) * 4)
    o_atn = sb(NCH * 32); o_dtt = sb(NCH * 32); o_dwt = sb(NCH * 32)
    o_dtc = sb(CB * NCH * 4); o_epc = sb(NH * NCH * 4); o_dsh = sb(CB * 4)
    o_stc = sb(2 * CI * 4)
    o_tiny = sb(128 * 4)
    o_one8 = sb(64); o_eps8 = sb(64)

    c_idb = ap(o_idb, 128, BF16); c_idf = ap(o_idf, 128, F32)
    c_mkn = ap(o_mkn, 128, F32)
    c_o8 = ap(o_o8, 128, F32, p=8); c_o128 = ap(o_o128, 1, F32)
    c_mci = ap(o_mci, CI, F32, p=8); c_sel = ap(o_sel, 4, F32)
    c_cvb = ap(o_cvb, CI, F32)
    c_gnw = ap(o_gnw, CI, F32); c_gnb = ap(o_gnb, CI, F32)
    c_mur = ap(o_mur, CD, F32); c_muk = ap(o_muk, CD, F32)
    c_mur1 = ap(o_mur1, CD, F32); c_muk1 = ap(o_muk1, CD, F32)
    c_bdt = ap(o_bdt, 1, F32, p=8); c_loga = ap(o_loga, 1, F32, p=8)
    c_aneg = ap(o_aneg, 1, F32, p=8)
    t_eppf = ap(o_eppf, NCH, F32, p=8)
    t_atn = ap(o_atn, NCH * 8, F32)
    t_dtt = ap(o_dtt, NCH * 8, F32)
    t_dwt = ap(o_dwt, NCH * 8, F32)
    t_dtc = ap(o_dtc, CB * NCH, F32)
    t_epc = ap(o_epc, NH * NCH, F32)
    t_dsh = ap(o_dsh, CB, F32)
    t_stc = ap(o_stc, 2 * CI, F32)
    t_tiny = ap(o_tiny, 128, F32)
    c_one8 = ap(o_one8, 1, F32, p=8); c_eps8 = ap(o_eps8, 1, F32, p=8)

    def mm3(eng_psum, lh, ll, rh, rl, kslices, out_sl, extra_first=False):
        seq = []
        for wop, aop in ((lh, rh), (lh, rl), (ll, rh)):
            for ks in kslices:
                seq.append((wop, aop, ks))
        return seq

    _c0par = newld()
    for dst, src in [
        (c_idb, didb), (c_idf, didf), (c_mkn, dmaskn),
        (c_o8, do8), (c_o128, do128), (c_mci, dmci),
        (c_sel, dsel), (ap(o_cvw, CI * KCV, F32), dconvw), (c_cvb, dconvb),
        (c_gnw, dgnw), (c_gnb, dgnb), (ap(o_dp, CI, F32), ddp),
        (c_mur, dmur), (c_muk, dmuk), (c_bdt, dbdt), (c_loga, dloga),
    ]:
        S.dma("sync", (lambda e, d=dst, s=src: e.dma_start(out=d, in_=s[:, :])), _c0par)
    S.wait("vector", _c0par)
    S.op("vector", lambda e: e.tensor_scalar(c_mur1, c_mur, -1.0, 1.0, ALU.mult, ALU.add))
    S.op("vector", lambda e: e.tensor_scalar(c_muk1, c_muk, -1.0, 1.0, ALU.mult, ALU.add))
    S.op("vector", lambda e: e.memset(c_one8, 1.0))
    S.op("vector", lambda e: e.memset(c_eps8, EPS), inc="VE")

    A.mark("right")
    o_xnh = A.alloc(CD * TH * 2, "right")
    o_xnl = A.alloc(CD * TH * 2, "right")
    t_xnh = ap(o_xnh, CD * TH, BF16)
    t_xnl = ap(o_xnl, CD * TH, BF16)
    A.mark("right")
    o_xt = A.alloc(2 * D * 4, "right")
    o_xsq = A.alloc(D * 4, "right")
    o_lnc = A.alloc(8 * 4, "right")
    o_xnth = A.alloc(D * 2, "right")
    o_xntl = A.alloc(D * 2, "right")

    nrows = TH
    rchunks = [(i * 128, min(128, nrows - i * 128)) for i in range(-(-nrows // 128))]
    ve_rows = []
    for r, (row0, rn) in enumerate(rchunks):
        xt = ap(o_xt + (r % 2) * D * 4, D, F32, p=rn)
        xsq = ap(o_xsq, D, F32, p=rn)
        xnth = ap(o_xnth, D, BF16, p=rn)
        xntl = ap(o_xntl, D, BF16, p=rn)

        def lc(i, p=rn):
            return ap(o_lnc + i * 4, 1, F32, p=p)
        par = newld()
        if r >= 2:
            S.wait("sync", "VE", ve_rows[r - 2])
        if r == 0:
            S.dma("sync", lambda e: e.dma_start(
                out=pool[0:3, o_xt:o_xt + D * 4].bitcast(F32), in_=dxh[:, :]), par)
            S.dma("sync", lambda e, b=min(125, T): e.dma_start(
                out=pool[3:3 + b, o_xt:o_xt + D * 4].bitcast(F32), in_=dx[0:b, :]), par)
        else:
            S.dma("sync", lambda e, d=xt, a=row0 - 3, b=rn: e.dma_start(
                out=d, in_=dx[a:a + b, :]), par)
        S.wait("vector", par)
        S.wait("scalar", par)
        S.op("vector", lambda e, xt=xt, c=lc(0): e.reduce_sum(c, xt, axis=AX.X))
        S.op("vector", lambda e, xt=xt, xs=xsq: e.tensor_mul(xs, xt, xt))
        S.edge("vector")
        S.op("vector", lambda e, xs=xsq, c=lc(1): e.reduce_sum(c, xs, axis=AX.X))
        S.edge("vector")
        S.op("vector", lambda e, c=lc(0), o=lc(2): e.tensor_scalar(o, c, -1.0 / D, None, ALU.mult))
        S.edge("vector")
        S.op("vector", lambda e, c=lc(2), o=lc(3): e.tensor_mul(o, c, c))
        S.edge("vector")
        S.op("vector", lambda e, c=lc(3), o=lc(4): e.tensor_scalar(o, c, -1.0, EPS, ALU.mult, ALU.add),
             inc="VE")
        S.wait("scalar", "VE")
        S.op("scalar", lambda e, c=lc(1), b=lc(4), o=lc(6): e.activation(
            out=o, in_=c, func=AF.Sqrt, bias=b, scale=1.0 / D), inc="SE")
        S.wait("vector", "SE")
        S.op("vector", lambda e, s=lc(6), o=lc(5): e.reciprocal(o, s))
        S.edge("vector")
        S.op("vector", lambda e, xt=xt, nm=lc(2), rs=lc(5): e.tensor_scalar(
            xt, xt, nm, rs, ALU.add, ALU.mult))
        S.edge("vector")
        S.op("vector", lambda e, xt=xt, h=xnth: e.tensor_copy(out=h, in_=xt))
        S.edge("vector")
        S.op("vector", lambda e, xt=xt, h=xnth, l=xntl: e.tensor_sub(l, xt, h), inc="VE")
        ve_rows.append(S.now("VE"))
        S.wait("tensor", "VE")
        for src, dstt in ((xnth, t_xnh), (xntl, t_xnl)):
            for cc in range(CD):
                S.op("tensor", lambda e, s=src, cc=cc, rn=rn: e.transpose(
                    psT[:, cc * 128:cc * 128 + rn], s[:, cc * 128:(cc + 1) * 128],
                    c_idb[0:rn, 0:rn]),
                     inc="TE" if cc == CD - 1 else None)
            S.wait("scalar", "TE")
            for cc in range(CD):
                S.op("scalar", lambda e, cc=cc, d=dstt, row0=row0, rn=rn: e.copy(
                    out=d[:, cc * TH + row0:cc * TH + row0 + rn],
                    in_=psT[:, cc * 128:cc * 128 + rn]),
                     inc="SE" if cc == CD - 1 else None)
            S.wait("tensor", "SE")
    S.wait("gpsimd", "SE")
    S.dma("gpsimd", lambda e: e.dma_start(out=sp_xnh, in_=t_xnh), "SP")
    S.dma("gpsimd", lambda e: e.dma_start(out=sp_xnl, in_=t_xnl), "SP")
    S.barrier()
    A.release("right")

    A.mark("left")
    A.mark("right")
    o_xst = A.alloc(2 * 2 * T * 2, "right")
    o_ws = A.alloc(2 * CD * 512, "right")
    o_wh = A.alloc(2 * CD * 256, "right")
    o_wl = A.alloc(2 * CD * 256, "right")
    o_xpr = A.alloc(2 * TH * 4, "right")
    o_cv = A.alloc(2 * T * 4, "right")
    o_sil = A.alloc(2 * T * 4, "right")
    o_cvt = A.alloc(4 * T * 4, "right")
    wtiles = _tiles(TH, 512)
    te_oc = []
    sp_oc = []
    ge_oc = []
    se_oc = []
    ve_sub = []
    ve_nt = {}
    g1i = [0]
    for oc in range(CI):
        ws = ap(o_ws + (oc % 2) * CD * 512, CD * 128, F32)
        wh = ap(o_wh + (oc % 2) * CD * 256, CD * 128, BF16)
        wl = ap(o_wl + (oc % 2) * CD * 256, CD * 128, BF16)
        xpr = ap(o_xpr + (oc % 2) * TH * 4, TH, F32)
        cv = ap(o_cv + (oc % 2) * T * 4, T, F32)
        sil = ap(o_sil + (oc % 2) * T * 4, T, F32)
        par = newld()
        if oc >= 2:
            S.wait("sync", "TE", te_oc[oc - 2])
        S.dma("sync", lambda e, oc=oc, d=ws: e.dma_start(
            out=d.rearrange("p (c q) -> p c q", q=128),
            in_=dwx[oc].rearrange("c p q -> p c q")), par)
        S.wait("vector", par)
        if oc >= 2:
            S.wait("vector", "TE", te_oc[oc - 2])
        S.op("vector", lambda e, s=ws, h=wh: e.tensor_copy(out=h, in_=s))
        S.edge("vector")
        S.op("vector", lambda e, s=ws, h=wh, l=wl: e.tensor_sub(l, s, h), inc="VE")
        S.wait("tensor", "VE")
        for nt, (off, w) in enumerate(wtiles):
            g1 = g1i[0]; g1i[0] += 1
            pb = psF[g1 % 4]
            if g1 >= 4:
                S.wait("tensor", "VE", ve_nt[g1 - 4])
            seq = [(a, b, cc) for a, b in ((wh, t_xnh), (wh, t_xnl), (wl, t_xnh))
                   for cc in range(CD)]
            for i, (wop, aop, cc) in enumerate(seq):
                S.op("tensor", lambda e, p=pb, wop=wop, aop=aop, cc=cc, off=off, w=w,
                     fs=(i == 0), ls=(i == len(seq) - 1): e.matmul(
                         p[:, 0:w], wop[:, cc * 128:(cc + 1) * 128],
                         aop[:, cc * TH + off:cc * TH + off + w], start=fs, stop=ls),
                     inc="TE" if i == len(seq) - 1 else None)
            S.wait("vector", "TE")
            if nt == 0 and oc >= 2:
                S.wait("vector", "GE", ge_oc[oc - 2])
                S.wait("vector", "SE", se_oc[oc - 2])
            S.op("vector", lambda e, p=pb, off=off, w=w, x=xpr: e.tensor_copy(
                out=x[:, off:off + w], in_=p[:, 0:w]), inc="VE")
            ve_nt[g1] = S.now("VE")
        te_oc.append(S.now("TE"))
        tmp1 = ap(o_cvt + (oc % 2) * 2 * T * 4, T, F32)
        tmp2 = ap(o_cvt + (oc % 2) * 2 * T * 4 + T * 4, T, F32)
        wc_ = [ap(o_cvw + (oc * KCV + k) * 4, 1, F32) for k in range(KCV)]
        S.wait("scalar", "VE", ve_nt[g1i[0] - 1])
        if oc >= 2:
            S.wait("scalar", "GE", ge_oc[oc - 2])
            S.wait("scalar", "VE", ve_sub[oc - 2])
        S.op("scalar", lambda e, x=xpr, o=tmp1, wc=wc_[0]: e.activation(
            out=o, in_=x[:, 0:T], func=AF.Identity, scale=wc))
        S.edge("scalar")
        S.op("scalar", lambda e, x=xpr, o=tmp2, wc=wc_[1]: e.activation(
            out=o, in_=x[:, 1:1 + T], func=AF.Identity, scale=wc), inc="SE")
        S.wait("gpsimd", "SE")
        S.op("gpsimd", lambda e, c=cv, a=tmp1, b=tmp2: e.tensor_add(c, a, b), inc="GE")
        S.wait("scalar", "GE")
        S.op("scalar", lambda e, x=xpr, o=tmp1, wc=wc_[2]: e.activation(
            out=o, in_=x[:, 2:2 + T], func=AF.Identity, scale=wc))
        S.edge("scalar")
        S.op("scalar", lambda e, x=xpr, o=tmp2, wc=wc_[3]: e.activation(
            out=o, in_=x[:, 3:3 + T], func=AF.Identity, scale=wc), inc="SE")
        S.wait("gpsimd", "SE")
        S.op("gpsimd", lambda e, c=cv, a=tmp1: e.tensor_add(c, c, a))
        S.edge("gpsimd")
        S.op("gpsimd", lambda e, c=cv, b=tmp2: e.tensor_add(c, c, b), inc="GE")
        S.wait("scalar", "GE")
        ge_oc.append(S.now("GE"))
        S.op("scalar", lambda e, c=cv, s=sil, b=ap(o_cvb + oc * 4, 1, F32): e.activation(
            out=s, in_=c, func=AF.Silu, bias=b, scale=1.0))
        S.edge("scalar")
        xsh = ap(o_xst + (oc % 2) * 2 * T * 2, T, BF16)
        xsl = ap(o_xst + (oc % 2) * 2 * T * 2 + T * 2, T, BF16)
        if oc >= 2:
            S.wait("scalar", "SP", sp_oc[oc - 2])
        S.op("scalar", lambda e, s=sil, d=xsh: e.copy(out=d, in_=s), inc="SE")
        S.wait("vector", "SE")
        S.op("vector", lambda e, s=sil, h=xsh, l=xsl: e.tensor_sub(l, s, h), inc="VE")
        ve_sub.append(S.now("VE"))
        se_oc.append(S.now("SE"))
        S.wait("gpsimd", "VE")
        S.dma("gpsimd", lambda e, oc=oc, h=xsh: e.dma_start(
            out=sp_xph[:, oc * T:(oc + 1) * T], in_=h), "SP")
        S.dma("gpsimd", lambda e, oc=oc, l=xsl: e.dma_start(
            out=sp_xpl[:, oc * T:(oc + 1) * T], in_=l), "SP")
        sp_oc.append(S.now("SP"))
    S.barrier()
    A.release("right")
    S.barrier()
    A.release("right")

    o_selp = sb(CB * 512); o_selh = sb(NH * 512)
    o_st = sb(CB * 1024)
    o_sth = sb(CB * 512); o_stl = sb(CB * 512)
    o_parts = sb(2 * NCH * CI * 4)
    o_expdA = sb(T * 4)
    o_A = sb((T + 4) * 4)
    c_selp = ap(o_selp, CB * 128, F32, p=8)
    c_selh = ap(o_selh, NH * 128, F32, p=8)
    t_st = ap(o_st, CB * 256, F32)
    t_sth = ap(o_sth, CB * 256, BF16)
    t_stl = ap(o_stl, CB * 256, BF16)
    t_parts = ap(o_parts, 2 * NCH * CI, F32)
    t_expdA = ap(o_expdA, T, F32, p=8)
    _selpar = newld()
    S.dma("sync", lambda e: e.dma_start(out=c_selp, in_=dselp[:, :]), _selpar)
    S.dma("sync", lambda e: e.dma_start(out=c_selh, in_=dselh[:, :]), _selpar)
    S.wait("vector", _selpar)
    S.op("vector", lambda e: e.memset(t_st, 0.0))
    S.op("vector", lambda e: e.memset(t_parts, 0.0), inc="VE")
    S.wait("tensor", _selpar)
    o_bh = sb(CB * T * 2); o_bl = sb(CB * T * 2)
    o_ch = sb(CB * T * 2); o_cl = sb(CB * T * 2)
    t_bh = ap(o_bh, CB * T, BF16); t_bl = ap(o_bl, CB * T, BF16)
    t_ch = ap(o_ch, CB * T, BF16); t_cl = ap(o_cl, CB * T, BF16)
    t_A = ap(o_A, T + 1, F32, p=8)
    A.mark("right")
    o_wdth = A.alloc(CI * 16, "right"); o_wdtl = A.alloc(CI * 16, "right")
    o_wbh = A.alloc(CB * CI * 256, "right"); o_wbl = A.alloc(CB * CI * 256, "right")
    o_wch = A.alloc(CB * CI * 256, "right"); o_wcl = A.alloc(CB * CI * 256, "right")
    o_ws2 = A.alloc(CI * 512, "right")
    XT2 = min(128, T)
    o_xpt = A.alloc(2 * 2 * CI * XT2 * 2, "right")
    o_dtraw = o_expdA
    wdth = ap(o_wdth, CI * 8, BF16); wdtl = ap(o_wdtl, CI * 8, BF16)
    ws2 = ap(o_ws2, CI * 128, F32)
    t_dtraw = ap(o_dtraw, T, F32, p=8)

    par = newld()
    for cc in range(CI):
        S.dma("sync", lambda e, cc=cc: e.dma_start(
            out=pool[:, o_ws2 + cc * 32:o_ws2 + cc * 32 + 32].bitcast(F32),
            in_=dwdt[cc]), par)
    S.wait("vector", par)
    wdt_s = ap(o_ws2, CI * 8, F32)
    S.op("vector", lambda e: e.tensor_copy(out=wdth, in_=wdt_s))
    S.op("vector", lambda e: e.tensor_sub(wdtl, wdt_s, wdth), inc="VE")
    for dwt_, oh_, ol_ in ((dwb, o_wbh, o_wbl), (dwc, o_wch, o_wcl)):
        for ob in range(CB):
            S.qbar("sync", ("vector",))
            par = newld()
            S.dma("sync", lambda e, dw=dwt_, ob=ob: e.dma_start(
                out=ws2.rearrange("p (c q) -> p c q", q=128),
                in_=dw[ob].rearrange("c p q -> p c q")), par)
            S.wait("vector", par)
            wh_ = pool[:, oh_ + ob * CI * 256:oh_ + (ob + 1) * CI * 256].bitcast(BF16)
            wl_ = pool[:, ol_ + ob * CI * 256:ol_ + (ob + 1) * CI * 256].bitcast(BF16)
            S.op("vector", lambda e, h=wh_: e.tensor_copy(out=h, in_=ws2))
            S.op("vector", lambda e, h=wh_, l=wl_: e.tensor_sub(l, ws2, h), inc="VE")
    t_wbh = ap(o_wbh, CB * CI * 128, BF16); t_wbl = ap(o_wbl, CB * CI * 128, BF16)
    t_wch = ap(o_wch, CB * CI * 128, BF16); t_wcl = ap(o_wcl, CB * CI * 128, BF16)
    ztiles = _tiles(T, XT2)
    ve_pnt = []
    se_dt = []
    te_tile = []
    for nt, (off, w) in enumerate(ztiles):
        xpth = ap(o_xpt + (nt % 2) * 2 * CI * XT2 * 2, CI * XT2, BF16)
        xptl = ap(o_xpt + (nt % 2) * 2 * CI * XT2 * 2 + CI * XT2 * 2, CI * XT2, BF16)
        if nt >= 2:
            S.wait("sync", "TE", te_tile[nt - 2])
        par = newld()
        S.dma("sync", lambda e, off=off, w=w, d=xpth: e.dma_start(
            out=d.rearrange("p (c t) -> p c t", t=XT2)[:, :, 0:w],
            in_=sp_xph.rearrange("p (c t) -> p c t", t=T)[:, :, off:off + w]), par)
        S.dma("sync", lambda e, off=off, w=w, d=xptl: e.dma_start(
            out=d.rearrange("p (c t) -> p c t", t=XT2)[:, :, 0:w],
            in_=sp_xpl.rearrange("p (c t) -> p c t", t=T)[:, :, off:off + w]), par)
        S.wait("tensor", par)
        if nt >= 1:
            S.wait("tensor", "SE", se_dt[-1])
        pb = psF[2]
        seq = [(a, b, cc) for a, b in ((wdth, xpth), (wdth, xptl), (wdtl, xpth))
               for cc in range(CI)]
        for i, (wop, aop, cc) in enumerate(seq):
            S.op("tensor", lambda e, p=pb, wop=wop, aop=aop, cc=cc, w=w,
                 fs=(i == 0), ls=(i == len(seq) - 1): e.matmul(
                     p[0:8, 0:w], wop[:, cc * 8:(cc + 1) * 8],
                     aop[:, cc * XT2:cc * XT2 + w], start=fs, stop=ls),
                 inc="TE" if i == len(seq) - 1 else None)
        S.wait("scalar", "TE")
        S.op("scalar", lambda e, p=pb, off=off, w=w: e.activation(
            out=t_dtraw[:, off:off + w], in_=p[0:8, 0:w], func=AF.Identity,
            bias=c_bdt, scale=1.0), inc="SE")
        se_dt.append(S.now("SE"))
        for wsel, (whh, wll, th_, tl_) in enumerate(
                ((t_wbh, t_wbl, t_bh, t_bl), (t_wch, t_wcl, t_ch, t_cl))):
            for ob in range(CB):
                gi = nt * 2 * CB + wsel * CB + ob
                pb = (psF[0], psF[1], psF[3], psF[4])[gi % 4]
                if gi >= 4:
                    S.wait("tensor", "VE", ve_pnt[gi - 4])
                seq = [(a, b, cc) for a, b in ((whh, xpth), (whh, xptl), (wll, xpth))
                       for cc in range(CI)]
                for i, (wop, aop, cc) in enumerate(seq):
                    S.op("tensor", lambda e, p=pb, wop=wop, aop=aop, ob=ob, cc=cc, w=w,
                         fs=(i == 0), ls=(i == len(seq) - 1): e.matmul(
                             p[:, 0:w], wop[:, (ob * CI + cc) * 128:(ob * CI + cc) * 128 + 128],
                             aop[:, cc * XT2:cc * XT2 + w], start=fs, stop=ls),
                         inc="TE" if i == len(seq) - 1 else None)
                S.wait("scalar", "TE")
                S.op("scalar", lambda e, p=pb, d=th_, ob=ob, off=off, w=w: e.copy(
                    out=d[:, ob * T + off:ob * T + off + w], in_=p[:, 0:w]), inc="SE")
                S.wait("vector", "SE")
                S.op("vector", lambda e, p=pb, dh=th_, dl=tl_, ob=ob, off=off, w=w:
                     e.tensor_sub(dl[:, ob * T + off:ob * T + off + w], p[:, 0:w],
                                  dh[:, ob * T + off:ob * T + off + w]), inc="VE")
                ve_pnt.append(S.now("VE"))
        te_tile.append(S.now("TE"))
    S.barrier()
    A.release("right")

    A.mark("right")
    o_dt = A.alloc(T * 4, "right")
    o_8a = A.alloc(T * 4, "right")
    o_8b = A.alloc(T * 4, "right")
    t_dt = ap(o_dt, T, F32, p=8)
    t_8a = ap(o_8a, T, F32, p=8)
    t_8b = ap(o_8b, T, F32, p=8)
    S.op("scalar", lambda e: e.activation(out=t_8b, in_=t_dtraw, func=AF.Abs))
    S.edge("scalar")
    S.op("scalar", lambda e: e.activation(out=t_8b, in_=t_8b, func=AF.Exp, scale=-1.0))
    S.edge("scalar")
    S.op("scalar", lambda e: e.activation(out=t_8b, in_=t_8b, func=AF.Ln, bias=c_one8), inc="SE")
    S.wait("vector", "SE")
    S.op("vector", lambda e: e.tensor_relu(t_dt, t_dtraw))
    S.edge("vector")
    S.op("vector", lambda e: e.tensor_add(t_dt, t_dt, t_8b), inc="VE")
    S.wait("scalar", "VE")
    S.op("scalar", lambda e: e.activation(out=t_tiny[0:8, 8:9], in_=c_loga, func=AF.Exp),
         inc="SE")
    S.wait("vector", "SE")
    S.op("vector", lambda e: e.tensor_scalar(c_aneg, t_tiny[0:8, 8:9], -1.0, None, ALU.mult))
    S.op("vector", lambda e: e.memset(t_8b, 1.0))
    S.op("vector", lambda e: e.memset(t_A[:, 0:1], 0.0))
    S.edge("vector")
    S.op("vector", lambda e: e.tensor_scalar(t_8a, t_dt, c_aneg, None, ALU.mult))
    S.edge("vector")
    S.op("vector", lambda e: e.tensor_tensor_scan(t_A[:, 1:T + 1], t_8b, t_8a, 0.0,
                                                  ALU.mult, ALU.add))
    S.edge("vector")
    aprev = t_A[:, 0:(NCH - 1) * Q + 1:Q].unsqueeze(2).broadcast_to([8, NCH, Q])
    acur3 = t_A[:, 1:T + 1].rearrange("p (c q) -> p c q", q=Q)
    out3a = t_8a.rearrange("p (c q) -> p c q", q=Q)
    S.op("vector", lambda e: e.tensor_tensor(out3a, acur3, aprev, ALU.subtract), inc="VE")
    S.wait("scalar", "VE")
    S.op("scalar", lambda e: e.activation(out=t_expdA, in_=t_8a, func=AF.Exp), inc="SE")
    aend = t_A[:, Q:T + 1:Q].unsqueeze(2).broadcast_to([8, NCH, Q])
    S.wait("vector", "SE")
    S.op("vector", lambda e: e.tensor_tensor(out3a, aend, acur3, ALU.subtract), inc="VE")
    S.wait("scalar", "VE")
    S.op("scalar", lambda e: e.activation(out=t_8a, in_=t_8a, func=AF.Exp), inc="SE")
    S.wait("vector", "SE")
    S.op("vector", lambda e: e.tensor_mul(t_8a, t_8a, t_dt), inc="VE")
    S.wait("scalar", "VE")
    S.op("scalar", lambda e: e.activation(out=t_eppf, in_=t_A[:, 0:(NCH - 1) * Q + 1:Q],
                                          func=AF.Exp), inc="SE")
    S.wait("vector", "SE")
    S.op("vector", lambda e: e.tensor_sub(t_tiny[0:8, 16:16 + NCH], t_A[:, Q:T + 1:Q],
                                          t_A[:, 0:(NCH - 1) * Q + 1:Q]), inc="VE")
    S.wait("scalar", "VE")
    S.op("scalar", lambda e: e.activation(out=t_tiny[0:8, 48:48 + NCH],
                                          in_=t_tiny[0:8, 16:16 + NCH], func=AF.Exp))
    S.edge("scalar")
    S.op("scalar", lambda e: e.activation(out=t_tiny[0:8, 78:79], in_=t_A[:, T:T + 1],
                                          func=AF.Exp), inc="SE")
    S.wait("tensor", "SE")
    for k in range(NCH):
        for srcf, dsto, negf in ((t_A[:, 1 + k * Q:1 + (k + 1) * Q], t_atn, -1.0),
                                 (t_dt[:, k * Q:(k + 1) * Q], t_dtt, 1.0),
                                 (t_8a[:, k * Q:(k + 1) * Q], t_dwt, 1.0)):
            S.op("tensor", lambda e, s=srcf: e.transpose(psX[:, 0:8], s, c_idf[0:8, 0:8]), inc="TE")
            S.wait("vector", "TE")
            S.op("vector", lambda e, d=dsto, k=k, ng=negf: e.tensor_scalar(
                d[:, k * 8:(k + 1) * 8], psX[:, 0:8], ng, None, ALU.mult), inc="VE")
            S.wait("tensor", "VE")
    for cbi in range(CB):
        S.op("tensor", lambda e, cbi=cbi: e.matmul(
            psX[:, 0:NCH], c_selp[:, cbi * 128:(cbi + 1) * 128],
            t_tiny[0:8, 48:48 + NCH], start=True, stop=True), inc="TE")
        S.wait("vector", "TE")
        S.op("vector", lambda e, cbi=cbi: e.tensor_copy(
            out=t_dtc[:, cbi * NCH:(cbi + 1) * NCH], in_=psX[:, 0:NCH]), inc="VE")
        S.wait("tensor", "VE")
        S.op("tensor", lambda e, cbi=cbi: e.matmul(
            psX[:, 0:1], c_selp[:, cbi * 128:(cbi + 1) * 128],
            t_tiny[0:8, 78:79], start=True, stop=True), inc="TE")
        S.wait("vector", "TE")
        S.op("vector", lambda e, cbi=cbi: e.tensor_copy(
            out=t_dsh[:, cbi:cbi + 1], in_=psX[:, 0:1]), inc="VE")
        S.wait("tensor", "VE")
    for h in range(NH):
        S.op("tensor", lambda e, h=h: e.matmul(
            psX[:, 0:NCH], c_selh[:, h * 128:(h + 1) * 128], t_eppf,
            start=True, stop=True), inc="TE")
        S.wait("vector", "TE")
        S.op("vector", lambda e, h=h: e.tensor_copy(
            out=t_epc[:, h * NCH:(h + 1) * NCH], in_=psX[:, 0:NCH]), inc="VE")
        S.wait("tensor", "VE")
    S.barrier()
    A.release("right")

    A.mark("right")
    o_xpc = A.alloc(2 * 2 * CI * Q * 2, "right")
    o_xtk = A.alloc(4 * 1024, "right")
    o_bwt = A.alloc(2 * 2 * 256, "right")
    o_ctk = A.alloc(2 * 2 * 256, "right")
    o_sp = A.alloc(4 * 512, "right")
    o_dp2 = A.alloc(4 * 512, "right")
    o_u2 = A.alloc(4 * 512, "right")
    o_scr = A.alloc(8 * 512, "right")
    o_xrc = A.alloc(CI * Q * 4, "right")
    o_crc = A.alloc(CB * Q * 4, "right")
    o_ymst = A.alloc(2 * CI * Q * 4, "right")

    def xpcb(par, half):
        return (ap(o_xpc + par * 2 * CI * Q * 2, CI * Q, BF16),
                ap(o_xpc + par * 2 * CI * Q * 2 + CI * Q * 2, CI * Q, BF16))

    def ymstb(par):
        return ap(o_ymst + par * CI * Q * 4, CI * Q, F32)

    def xtk_h(j):
        return ap(o_xtk + j * 1024, 256, BF16)

    def xtk_l(j):
        return ap(o_xtk + j * 1024 + 512, 256, BF16)

    def bwt(hl, i):
        return ap(o_bwt + hl * 512 + i * 256, 128, BF16)

    def bwt_all(hl):
        return ap(o_bwt + hl * 512, 256, BF16)

    def ctk(hl, i):
        return ap(o_ctk + hl * 512 + i * 256, 128, BF16)

    def ctk_all(hl):
        return ap(o_ctk + hl * 512, 256, BF16)

    def sp_t(hl, j):
        return ap(o_sp + hl * 1024 + j * 256, 128, BF16)

    def sp_all(hl):
        return ap(o_sp + hl * 1024, 512, BF16)

    def u_t(j):
        return ap(o_u2 + j * 512, 128, F32)

    def dp_t(j):
        return ap(o_dp2 + j * 512, 128, F32)

    scan_marks = {}
    gp_gate = [None]

    import os as _osd
    _HVY = _osd.environ.get("KDBG_SCANBAR")
    _GSP = _osd.environ.get("KDBG_GRPSPLIT")
    _SLVL = int(_osd.environ.get("KDBG_SLVL", 9))
    _S1M = int(_osd.environ.get("KDBG_S1MASK", 15))

    def scan_chunk(k, corr):
        if _HVY:
            S.barrier()
        par = k % 2
        ymst = ymstb(par)
        if corr:
            if k >= 2:
                S.wait("sync", "SP", scan_marks[("wb", k - 2)])
                S.wait("sync", "SE", scan_marks[("sq", k - 2)])
            ld = newld()
            S.dma("sync", lambda e, k=k, d=ymst: e.dma_start(out=d, in_=sp_ym[k]), ld)
            S.wait("vector", ld)
            if k >= 1:
                S.wait("gpsimd", "VE", gp_gate[0][1])
            for hp in range(CB):
                S.op("gpsimd", lambda e, hp=hp, k=k: e.tensor_add(
                    ap(o_crc + hp * Q * 4, Q, F32),
                    t_ch[:, hp * T + k * Q:hp * T + (k + 1) * Q],
                    t_cl[:, hp * T + k * Q:hp * T + (k + 1) * Q]), inc="GE")
        else:
            xph_, xpl_ = xpcb(par, 0)
            if k >= 2:
                m = scan_marks[("xte", k - 2)]
                S.wait("sync", m[0], m[1])
                m = scan_marks[("xve", k - 2)]
                S.wait("sync", m[0], m[1])
                S.wait("vector", "SP", scan_marks[("wb", k - 2)])
            ld = newld()
            S.dma("sync", lambda e, k=k, d=xph_: e.dma_start(
                out=d.rearrange("p (c q) -> p c q", q=Q),
                in_=sp_xph.rearrange("p (c t) -> p c t", t=T)[:, :, k * Q:(k + 1) * Q]), ld)
            S.dma("sync", lambda e, k=k, d=xpl_: e.dma_start(
                out=d.rearrange("p (c q) -> p c q", q=Q),
                in_=sp_xpl.rearrange("p (c t) -> p c t", t=T)[:, :, k * Q:(k + 1) * Q]), ld)
            S.op("vector", lambda e: e.tensor_copy(out=t_sth, in_=t_st))
            S.op("vector", lambda e: e.tensor_sub(t_stl, t_st, t_sth), inc="VE")
            S.wait("gpsimd", ld)
            if k >= 1:
                S.wait("gpsimd", "VE", gp_gate[0][1])
            for ci in range(CI):
                S.op("gpsimd", lambda e, ci=ci: e.tensor_add(
                    ap(o_xrc + ci * Q * 4, Q, F32),
                    xph_[:, ci * Q:(ci + 1) * Q], xpl_[:, ci * Q:(ci + 1) * Q]))
            for hp in range(CB):
                S.op("gpsimd", lambda e, hp=hp, k=k: e.tensor_add(
                    ap(o_crc + hp * Q * 4, Q, F32),
                    t_ch[:, hp * T + k * Q:hp * T + (k + 1) * Q],
                    t_cl[:, hp * T + k * Q:hp * T + (k + 1) * Q]), inc="GE")
        for half in range(2):
            if _HVY:
                S.barrier()
            hps = (2 * half, 2 * half + 1)
            heads = [4 * half + j for j in range(4)]
            if not corr:
                S.wait("tensor", ld)
            S.wait("tensor", "VE")
            S.wait("tensor", "SE")
            for i, hp in (enumerate(hps) if (_S1M & 1) else []):
                S.op("tensor", lambda e, hp=hp, k=k, i=i: e.matmul(
                    psF[0][:, i * 128:(i + 1) * 128],
                    c_selp[:, hp * 128:(hp + 1) * 128],
                    t_expdA[:, k * Q:(k + 1) * Q],
                    start=True if _GSP else (i == 0),
                    stop=True if _GSP else (i == 1)))
            if not corr:
                for i, hp in enumerate(hps):
                    S.op("tensor", lambda e, hp=hp, k=k, i=i: e.transpose(
                        psT[:, i * 256:i * 256 + 128],
                        t_bh[:, hp * T + k * Q:hp * T + (k + 1) * Q], c_idb))
                    S.op("tensor", lambda e, hp=hp, k=k, i=i: e.transpose(
                        psT[:, i * 256 + 128:i * 256 + 256],
                        t_bl[:, hp * T + k * Q:hp * T + (k + 1) * Q], c_idb))
                g_seq = [(t_bh, t_ch), (t_bh, t_cl), (t_bl, t_ch)]
                for half_j, jset in ((0, (0, 2)), (1, (1, 3))):
                    nmm = 0
                    for j in jset:
                        h = heads[j]
                        hp, r0 = h // 2, (h % 2) * 64
                        for gg, (ph, pr) in enumerate(g_seq):
                            S.op("tensor", lambda e, ph=ph, pr=pr, hp=hp, r0=r0, k=k, j=j,
                                 fs=(nmm == 0), ls=(nmm == 5): e.matmul(
                                     psF[2][:, j * 128:(j + 1) * 128],
                                     ph[r0:r0 + 64, hp * T + k * Q:hp * T + (k + 1) * Q],
                                     pr[r0:r0 + 64, hp * T + k * Q:hp * T + (k + 1) * Q],
                                     start=fs, stop=ls))
                            nmm += 1
                    if half_j == 0:
                        for j, h in enumerate(heads):
                            S.op("tensor", lambda e, h=h, k=k, j=j: e.matmul(
                                psF[5][:, j * 128:(j + 1) * 128],
                                c_selh[:, h * 128:(h + 1) * 128],
                                t_A[0:8, 1 + k * Q:1 + (k + 1) * Q],
                                start=(j == 0), stop=(j == 3)))
                S.op("tensor", lambda e: e.matmul(
                    psX[:, 0:NCH], c_selp[:, 0:128], t_eppf, start=True, stop=True),
                     inc="TE")
                m_s1 = S.now("TE")
                if _SLVL < 2:
                    continue
                S.wait("vector", "TE", m_s1)
                bwf = ap(o_scr + 1024, 256, F32)
                bwtmp = ap(o_scr + 2048, 256, F32)
                psT3h = psT[:, 0:512].rearrange("p (i t q) -> p i t q", i=2, t=2)
                S.op("vector", lambda e: e.tensor_copy(
                    out=bwtmp, in_=psT3h[:, :, 0, :]))
                S.op("vector", lambda e: e.tensor_add(
                    bwf, bwtmp, psT3h[:, :, 1, :]))
                dwt_bc = t_dwt[:, k * 8 + 4 * half:k * 8 + 4 * half + 4].unsqueeze(2).broadcast_to([128, 4, 64])
                S.op("vector", lambda e: e.tensor_tensor(
                    bwf.rearrange("p (c u) -> p c u", u=64), bwf.rearrange(
                        "p (c u) -> p c u", u=64), dwt_bc, ALU.mult))
                S.op("vector", lambda e: e.tensor_copy(out=bwt_all(0), in_=bwf))
                S.op("vector", lambda e: e.tensor_sub(bwt_all(1), bwf, bwt_all(0)),
                     inc="VE")
                m_bw = S.now("VE")
                S.wait("tensor", "VE", m_bw)
                for j in (0, 1):
                    h = heads[j]
                    for hl, src in ((0, xph_), (1, xpl_)):
                        for pc in range(2):
                            ci = h * 2 + pc
                            S.op("tensor", lambda e, s=src, ci=ci, j=j, hl=hl, pc=pc:
                                 e.transpose(psT[:, j * 512 + hl * 256 + pc * 128:
                                                 j * 512 + hl * 256 + pc * 128 + 128],
                                             s[:, ci * Q:(ci + 1) * Q], c_idb),
                                 inc="TE" if (j == 1 and hl == 1 and pc == 1) else None)
                m_w1 = S.now("TE")
                if _SLVL < 3:
                    continue
                S.wait("vector", "GE")
                crec = ap(o_scr, 256, F32)
                S.op("vector", lambda e, hp0=hps[0]: e.tensor_mul(
                    crec, ap(o_crc + hp0 * Q * 4, 2 * Q, F32), psF[0][:, 0:256]))
                S.op("vector", lambda e: e.tensor_copy(out=ctk_all(0), in_=crec))
                S.op("vector", lambda e: e.tensor_sub(ctk_all(1), crec, ctk_all(0)),
                     inc="VE")
                for j, h in enumerate(heads):
                    S.op("vector", lambda e, j=j, k=k, h=h: e.tensor_scalar(
                        u_t(j), psF[5][:, j * 128:(j + 1) * 128],
                        t_atn[:, k * 8 + h:k * 8 + h + 1], 0.0, ALU.add, ALU.min))
                    S.op("vector", lambda e, j=j: e.tensor_add(u_t(j), u_t(j), c_mkn),
                         inc="VE" if j == 3 else None)
                m_u = S.now("VE")
                S.wait("scalar", "VE", m_u)
                for j in range(4):
                    S.op("scalar", lambda e, j=j: e.activation(
                        out=dp_t(j), in_=u_t(j), func=AF.Exp), inc="SE" if j == 3 else None)
                m_dp = S.now("SE")
                S.wait("vector", "SE", m_dp)
                for j, h in enumerate(heads):
                    P1j = u_t(j)
                    S.op("vector", lambda e, P1=P1j, j=j: e.tensor_mul(
                        P1, psF[2][:, j * 128:(j + 1) * 128], dp_t(j)))
                    S.op("vector", lambda e, P1=P1j, j=j, k=k, h=h: e.tensor_scalar(
                        sp_t(0, j), P1, t_dtt[:, k * 8 + h:k * 8 + h + 1], None, ALU.mult))
                    S.op("vector", lambda e, P1=P1j, j=j, k=k, h=h: e.scalar_tensor_tensor(
                        sp_t(1, j), P1, t_dtt[:, k * 8 + h:k * 8 + h + 1], sp_t(0, j),
                        ALU.mult, ALU.subtract), inc="VE" if j == 3 else None)
                m_sp = S.now("VE")
                if _SLVL < 4:
                    continue
                S.wait("scalar", "TE", m_w1)
                for j in (0, 1):
                    S.op("scalar", lambda e, j=j: e.copy(
                        out=pool[:, o_xtk + j * 1024:o_xtk + j * 1024 + 1024].bitcast(BF16),
                        in_=psT[:, j * 512:j * 512 + 512]), inc="SE" if j == 1 else None)
                m_xe1 = S.now("SE")
                S.wait("tensor", "SE", m_xe1)
                for j in (2, 3):
                    h = heads[j]
                    for hl, src in ((0, xph_), (1, xpl_)):
                        for pc in range(2):
                            ci = h * 2 + pc
                            S.op("tensor", lambda e, s=src, ci=ci, j=j, hl=hl, pc=pc:
                                 e.transpose(psT[:, (j - 2) * 512 + hl * 256 + pc * 128:
                                                 (j - 2) * 512 + hl * 256 + pc * 128 + 128],
                                             s[:, ci * Q:(ci + 1) * Q], c_idb),
                                 inc="TE" if (j == 3 and hl == 1 and pc == 1) else None)
                m_w2 = S.now("TE")
                S.wait("scalar", "TE", m_w2)
                for j in (2, 3):
                    S.op("scalar", lambda e, j=j: e.copy(
                        out=pool[:, o_xtk + j * 1024:o_xtk + j * 1024 + 1024].bitcast(BF16),
                        in_=psT[:, (j - 2) * 512:(j - 2) * 512 + 512]),
                         inc="SE" if j == 3 else None)
                m_xe2 = S.now("SE")
                if _SLVL < 5:
                    continue
                S.wait("tensor", "VE", m_sp)
                S.wait("tensor", "SE", m_xe2)
                if k > 0 or half > 0 or True:
                    S.wait("tensor", "VE")
                for j, h in enumerate(heads):
                    hp, r0 = h // 2, (h % 2) * 64
                    bank = psF[3] if j < 2 else psF[1]
                    c0 = (j % 2) * 256
                    nmm = 0
                    for pc in range(2):
                        for lh, rh in ((xtk_h(j), sp_t(0, j)), (xtk_h(j), sp_t(1, j)),
                                       (xtk_l(j), sp_t(0, j))):
                            S.op("tensor", lambda e, lh=lh, rh=rh, pc=pc, c0=c0, bank=bank,
                                 fs=(nmm == 0), ls=False: e.matmul(
                                     bank[:, c0 + pc * 128:c0 + pc * 128 + 128],
                                     lh[:, pc * 128:pc * 128 + 128], rh,
                                     start=fs, stop=False))
                            nmm += 1
                    i = hp - 2 * half
                    for pc in range(2):
                        for lh, rh in ((t_sth, ctk(0, i)), (t_sth, ctk(1, i)),
                                       (t_stl, ctk(0, i))):
                            nmm += 1
                            S.op("tensor", lambda e, lh=lh, rh=rh, pc=pc, hp=hp, r0=r0,
                                 c0=c0, bank=bank, i=i, ls=(nmm == 12): e.matmul(
                                     bank[:, c0 + pc * 128:c0 + pc * 128 + 128],
                                     lh[r0:r0 + 64, hp * 256 + pc * 128:hp * 256 + pc * 128 + 128],
                                     rh[r0:r0 + 64, :], start=False, stop=ls),
                                 inc="TE" if (j == 3 and nmm == 12) else None)
                for i, hp in enumerate(hps):
                    for hh in range(2):
                        j = 2 * i + hh
                        r0 = hh * 64
                        cs_seq = [(bwt(0, i), xtk_h(j)), (bwt(0, i), xtk_l(j)),
                                  (bwt(1, i), xtk_h(j))]
                        for jj, (lh, rh) in enumerate(cs_seq):
                            S.op("tensor", lambda e, lh=lh, rh=rh, i=i, r0=r0, hh=hh,
                                 fs=(jj == 0), ls=(jj == 2): e.matmul(
                                     psF[4][r0:r0 + 64, i * 256:i * 256 + 256],
                                     lh[:, hh * 64:hh * 64 + 64], rh,
                                     start=fs, stop=ls),
                                 inc="TE" if (i == 1 and hh == 1 and jj == 2) else None)
                m_s3 = S.now("TE")
                if _SLVL < 6:
                    continue
                S.wait("vector", "TE", m_s3)
                S.wait("vector", "GE")
                for j, h in enumerate(heads):
                    bank = psF[3] if j < 2 else psF[1]
                    c0 = (j % 2) * 256
                    S.op("vector", lambda e, h=h, c0=c0, bank=bank, ymst=ymst:
                         e.scalar_tensor_tensor(
                             ymst[:, h * 256:(h + 1) * 256],
                             ap(o_xrc + h * 2 * Q * 4, 256, F32),
                             ap(o_dp + h * 2 * 4, 1, F32),
                             bank[:, c0:c0 + 256], ALU.mult, ALU.add))
                for i, hp in enumerate(hps):
                    S.op("vector", lambda e, hp=hp, k=k, i=i: e.scalar_tensor_tensor(
                        t_st[:, hp * 256:(hp + 1) * 256], t_st[:, hp * 256:(hp + 1) * 256],
                        t_dtc[:, hp * NCH + k:hp * NCH + k + 1],
                        psF[4][:, i * 256:i * 256 + 256], ALU.mult, ALU.add),
                         inc="VE" if i == 1 else None)
            else:
                m_s1 = S.mark("tensor")
                if m_s1:
                    S.wait("vector", m_s1[0], m_s1[1])
                for i, hp in enumerate(hps):
                    crec = ap(o_scr, 128, F32)
                    S.op("vector", lambda e, hp=hp, k=k, o=crec: e.tensor_add(
                        o, t_ch[:, hp * T + k * Q:hp * T + (k + 1) * Q],
                        t_cl[:, hp * T + k * Q:hp * T + (k + 1) * Q]))
                    S.op("vector", lambda e, o=crec, i=i: e.tensor_mul(
                        o, o, psF[0][:, i * 128:(i + 1) * 128]))
                    S.op("vector", lambda e, o=crec, i=i: e.tensor_copy(out=ctk(0, i), in_=o))
                    S.op("vector", lambda e, o=crec, i=i: e.tensor_sub(ctk(1, i), o, ctk(0, i)),
                         inc="VE" if i == 1 else None)
                m_ct = S.now("VE")
                S.wait("tensor", "VE", m_ct)
                for half_j, jset in ((0, (0, 2)), (1, (1, 3))):
                    for j in jset:
                        h = heads[j]
                        hp, r0 = h // 2, (h % 2) * 64
                        i = hp - 2 * half
                        bank = psF[3] if j < 2 else psF[1]
                        c0 = (j % 2) * 256
                        nmm = 0
                        for pc in range(2):
                            for lh, rh in ((t_sth, ctk(0, i)), (t_sth, ctk(1, i)),
                                           (t_stl, ctk(0, i))):
                                nmm += 1
                                S.op("tensor", lambda e, lh=lh, rh=rh, pc=pc, hp=hp, r0=r0,
                                     c0=c0, bank=bank, fs=(nmm == 1), ls=(nmm == 6):
                                     e.matmul(
                                         bank[:, c0 + pc * 128:c0 + pc * 128 + 128],
                                         lh[r0:r0 + 64, hp * 256 + pc * 128:
                                            hp * 256 + pc * 128 + 128],
                                         rh[r0:r0 + 64, :], start=fs, stop=ls))
                    S.op("tensor", lambda e: e.matmul(
                        psX[:, 0:NCH], c_selp[:, 0:128], t_eppf, start=True, stop=True),
                         inc="TE" if half_j == 1 else None)
                m_s3 = S.now("TE")
                S.wait("vector", "TE", m_s3)
                for j, h in enumerate(heads):
                    bank = psF[3] if j < 2 else psF[1]
                    c0 = (j % 2) * 256
                    for pc in range(2):
                        ci = h * 2 + pc
                        S.op("vector", lambda e, ci=ci, pc=pc, k=k, h=h, c0=c0, bank=bank,
                             ymst=ymst: e.scalar_tensor_tensor(
                                 ymst[:, ci * Q:(ci + 1) * Q],
                                 bank[:, c0 + pc * 128:c0 + pc * 128 + 128],
                                 t_epc[:, h * NCH + k:h * NCH + k + 1],
                                 ymst[:, ci * Q:(ci + 1) * Q], ALU.mult, ALU.add,
                                 accum_out=t_parts[:, k * CI + ci:k * CI + ci + 1]),
                             inc="VE" if (j == 3 and pc == 1) else None)
                m_ev = S.now("VE")
                S.wait("scalar", "VE", m_ev)
                for j, h in enumerate(heads):
                    for pc in range(2):
                        ci = h * 2 + pc
                        S.op("scalar", lambda e, ci=ci, k=k, ymst=ymst: e.activation(
                            out=ap(o_scr + 512, 128, F32), in_=ymst[:, ci * Q:(ci + 1) * Q],
                            func=AF.Square,
                            accum_out=t_parts[:, NCH * CI + k * CI + ci:
                                              NCH * CI + k * CI + ci + 1]),
                             inc="SE" if (j == 3 and pc == 1) else None)
        gp_gate[0] = ("VE", S.now("VE"))
        S.mark("vector")
        S.wait("gpsimd", "VV")
        S.wait("gpsimd", "VE")
        if corr:
            S.mark("scalar")
            S.wait("gpsimd", "SS")
            S.wait("gpsimd", "SE")
            scan_marks[("sq", k)] = S.now("SE")
        S.dma("gpsimd", lambda e, k=k, d=ymst: e.dma_start(out=sp_ym[k], in_=d), "SP")
        scan_marks[("wb", k)] = S.now("SP")
        if not corr:
            m = S.mark("tensor")
            scan_marks[("xte", k)] = m
            scan_marks[("xve", k)] = ("VE", S.now("VE"))


    _NLIM = int(_osd.environ.get("KDBG_NCHLIM", NCH))
    for k in range(min(NCH, _NLIM)):
        scan_chunk(k, corr=False)

    S.wait("gpsimd", "VE")
    for hp in range(CB):
        S.dma("gpsimd", lambda e, hp=hp: e.dma_start(
            out=cc_in.ap()[hp, :, 0:256], in_=t_st[:, hp * 256:(hp + 1) * 256]), "SP")
        S.dma("gpsimd", lambda e, hp=hp: e.dma_start(
            out=cc_in.ap()[hp, :, 256:257], in_=t_dsh[:, hp:hp + 1],
            allow_slow_non_contiguous=True), "SP")
    S.wait("gpsimd", "SP")
    if n_shards > 1:
        S.op("gpsimd", lambda e: e.collective_compute(
            "AllGather", ALU.bypass, replica_groups=groups,
            ins=[cc_in.ap().opt()], outs=[cc_out.ap().opt()]), inc="CC")
        S.wait("gpsimd", "CC")
    else:
        S.dma("gpsimd", lambda e: e.dma_start(out=cc_out.ap()[0], in_=cc_in.ap()[:, :, :]),
              "SP")
        S.wait("gpsimd", "SP")
    A.mark("right")
    o_gth = A.alloc(n_shards * CB * 257 * 4, "right")
    o_pfx = A.alloc(2 * CB * 1024, "right")
    t_gth = ap(o_gth, n_shards * CB * 257, F32)
    for m_ in range(n_shards):
        for hp_ in range(CB):
            S.dma("gpsimd", lambda e, m_=m_, hp_=hp_: e.dma_start(
                out=t_gth[:, (m_ * CB + hp_) * 257:(m_ * CB + hp_ + 1) * 257],
                in_=cc_out.ap()[m_, hp_]), "SP")
    S.wait("vector", "SP")

    def gs(m, hp):
        return ap(o_gth + (m * CB + hp) * 257 * 4, 256, F32)

    def gd(m, hp):
        return ap(o_gth + ((m * CB + hp) * 257 + 256) * 4, 1, F32)

    def pb_(par, hp):
        return ap(o_pfx + (par * CB + hp) * 1024, 256, F32)

    for hp in range(CB):
        S.op("vector", lambda e, hp=hp: e.tensor_copy(out=pb_(0, hp), in_=gs(0, hp)))
        S.edge("vector")
        S.op("vector", lambda e, hp=hp: e.tensor_scalar(
            t_st[:, hp * 256:(hp + 1) * 256], pb_(0, hp), c_sel[:, 0:1], None, ALU.mult))
        S.edge("vector")
        for m in range(1, n_shards):
            S.op("vector", lambda e, hp=hp, m=m: e.scalar_tensor_tensor(
                pb_(m % 2, hp), pb_((m - 1) % 2, hp), gd(m, hp), gs(m, hp),
                ALU.mult, ALU.add))
            S.edge("vector")
            S.op("vector", lambda e, hp=hp, m=m: e.scalar_tensor_tensor(
                t_st[:, hp * 256:(hp + 1) * 256], pb_(m % 2, hp), c_sel[:, m:m + 1],
                t_st[:, hp * 256:(hp + 1) * 256], ALU.mult, ALU.add))
            S.edge("vector")
    S.op("vector", lambda e: e.tensor_copy(out=t_sth, in_=t_st))
    S.edge("vector")
    S.op("vector", lambda e: e.tensor_sub(t_stl, t_st, t_sth), inc="VE")
    A.release("right")

    S.barrier()
    import os as _os
    if not _os.environ.get("KDBG_NO_CORR"):
        for k in range(min(NCH, _NLIM)):
            scan_chunk(k, corr=True)
    S.barrier()
    A.release("right")

    A.mark("right")
    pview = t_parts.rearrange("p (g k c) -> p g k c", g=2, c=CI)
    for g in range(2):
        for h in range(NH):
            S.op("vector", lambda e, g=g, h=h: e.reduce_sum(
                t_tiny[:, 32 + g * 8 + h:32 + g * 8 + h + 1],
                pview[:, g, :, 2 * h:2 * h + 2], axis=AX.XY),
                 inc="VE" if (g == 1 and h == NH - 1) else None)
    S.wait("tensor", "VE")
    S.op("tensor", lambda e: e.matmul(psX[0:1, 0:16], c_o128, t_tiny[:, 32:48],
                                      start=True, stop=True), inc="TE")
    S.wait("vector", "TE")
    S.op("vector", lambda e: e.tensor_copy(out=t_tiny[0:1, 0:16], in_=psX[0:1, 0:16]),
         inc="VE")
    S.wait("gpsimd", "VE")
    S.dma("gpsimd", lambda e: e.dma_start(out=ar_in.ap()[:, :], in_=t_tiny[0:1, 0:16]), "SP")
    S.wait("gpsimd", "SP")
    if n_shards > 1:
        S.op("gpsimd", lambda e: e.collective_compute(
            "AllReduce", ALU.add, replica_groups=groups,
            ins=[ar_in.ap().opt()], outs=[ar_out.ap().opt()]), inc="CC")
        S.wait("gpsimd", "CC")
    else:
        S.dma("gpsimd", lambda e: e.dma_start(out=ar_out.ap()[:, :], in_=ar_in.ap()[:, :]),
              "SP")
        S.wait("gpsimd", "SP")
    S.dma("gpsimd", lambda e: e.dma_start(out=t_tiny[0:1, 16:32], in_=ar_out.ap()[:, :]),
          "SP")
    S.wait("tensor", "SP")
    S.op("tensor", lambda e: e.transpose(psX[0:8, 0:1], t_tiny[0:1, 16:24], c_idf[0:1, 0:1]))
    S.op("tensor", lambda e: e.transpose(psX[0:8, 1:2], t_tiny[0:1, 24:32], c_idf[0:1, 0:1]),
         inc="TE")
    S.wait("vector", "TE")
    S.op("vector", lambda e: e.tensor_scalar(t_tiny[0:8, 48:49], psX[0:8, 0:1],
                                             1.0 / CNT_GN, None, ALU.mult))
    S.op("vector", lambda e: e.tensor_scalar(t_tiny[0:8, 47:48], psX[0:8, 1:2],
                                             1.0 / CNT_GN, None, ALU.mult))
    S.op("vector", lambda e: e.tensor_mul(t_tiny[0:8, 49:50], t_tiny[0:8, 48:49],
                                          t_tiny[0:8, 48:49]))
    S.op("vector", lambda e: e.tensor_sub(t_tiny[0:8, 50:51], t_tiny[0:8, 47:48],
                                          t_tiny[0:8, 49:50]), inc="VE")
    S.wait("scalar", "VE")
    S.op("scalar", lambda e: e.activation(out=t_tiny[0:8, 54:55], in_=t_tiny[0:8, 50:51],
                                          func=AF.Sqrt, bias=c_eps8), inc="SE")
    S.wait("vector", "SE")
    S.op("vector", lambda e: e.reciprocal(t_tiny[0:8, 51:52], t_tiny[0:8, 54:55]))
    S.edge("vector")
    S.op("vector", lambda e: e.tensor_mul(t_tiny[0:8, 52:53], t_tiny[0:8, 48:49],
                                          t_tiny[0:8, 51:52]))
    S.edge("vector")
    S.op("vector", lambda e: e.tensor_scalar(t_tiny[0:8, 53:54], t_tiny[0:8, 52:53],
                                             -1.0, None, ALU.mult))
    S.edge("vector")
    S.op("vector", lambda e: e.tensor_scalar(t_tiny[0:8, 56:56 + CI], c_mci,
                                             t_tiny[0:8, 51:52], None, ALU.mult))
    S.edge("vector")
    S.op("vector", lambda e: e.tensor_scalar(t_tiny[0:8, 56 + CI:56 + 2 * CI], c_mci,
                                             t_tiny[0:8, 53:54], None, ALU.mult), inc="VE")
    S.wait("tensor", "VE")
    S.op("tensor", lambda e: e.matmul(psX[:, 0:CI], c_o8, t_tiny[0:8, 56:56 + CI],
                                      start=True, stop=True))
    S.op("tensor", lambda e: e.matmul(psX[:, CI:2 * CI], c_o8,
                                      t_tiny[0:8, 56 + CI:56 + 2 * CI],
                                      start=True, stop=True), inc="TE")
    S.wait("vector", "TE")
    S.op("vector", lambda e: e.tensor_mul(t_stc[:, 0:CI], psX[:, 0:CI], c_gnw))
    S.edge("vector")
    S.op("vector", lambda e: e.tensor_mul(t_tiny[:, 60:60 + CI], psX[:, CI:2 * CI], c_gnw))
    S.edge("vector")
    S.op("vector", lambda e: e.tensor_add(t_stc[:, CI:2 * CI], t_tiny[:, 60:60 + CI],
                                          c_gnb), inc="VE")
    S.barrier()
    A.release("right")
    A.release("left")

    A.mark("left")
    o_wrh = sb(CD * D * 2); o_wrl = sb(CD * D * 2)
    o_wvh = sb(CD * D * 2); o_wvl = sb(CD * D * 2)
    o_wkh = sb(CD * CD * 256); o_wkl = sb(CD * CD * 256)
    t_wrh = ap(o_wrh, CD * D, BF16); t_wrl = ap(o_wrl, CD * D, BF16)
    t_wvh = ap(o_wvh, CD * D, BF16); t_wvl = ap(o_wvl, CD * D, BF16)
    t_wkh = ap(o_wkh, CD * CD * 128, BF16); t_wkl = ap(o_wkl, CD * CD * 128, BF16)
    A.mark("right")
    o_wst = A.alloc(D * 4, "right")
    o_xnt = A.alloc(2 * CD * 257 * 2, "right")
    o_xnr = A.alloc(CD * 257 * 4, "right")
    o_mx2 = A.alloc(2 * FT * 4, "right")
    o_xrh = A.alloc(CD * FT * 2, "right"); o_xrl = A.alloc(CD * FT * 2, "right")
    o_xkh = A.alloc(CD * FT * 2, "right"); o_xkl = A.alloc(CD * FT * 2, "right")
    o_kq = A.alloc(FT * 4, "right")
    o_kqh = A.alloc(CD * FT * 2, "right"); o_kql = A.alloc(CD * FT * 2, "right")
    o_rt = A.alloc((FT // 128) * D * 4, "right")
    o_ffo = A.alloc(D * 4, "right")
    o_xr2 = A.alloc((FT // 128) * D * 4, "right")
    wst = ap(o_wst, D, F32)

    for (dsrc, th_, tl_, isp) in ((dwr, t_wrh, t_wrl, False), (dwv, t_wvh, t_wvl, False),
                                  (dwk, t_wkh, t_wkl, True)):
        for oc in range(CD):
            S.wait("sync", "VE")
            par = newld()
            if isp:
                S.dma("sync", lambda e, oc=oc, s=dsrc: e.dma_start(
                    out=wst[:, 0:CD * 128].rearrange("p (c q) -> p c q", q=128),
                    in_=s[oc].rearrange("c p q -> p c q")), par)
            else:
                S.dma("sync", lambda e, oc=oc, s=dsrc: e.dma_start(
                    out=wst, in_=s[oc]), par)
            S.wait("vector", par)
            S.op("vector", lambda e, oc=oc, d=th_: e.tensor_copy(
                out=d[:, oc * D:(oc + 1) * D], in_=wst))
            S.op("vector", lambda e, oc=oc, dh=th_, dl=tl_: e.tensor_sub(
                dl[:, oc * D:(oc + 1) * D], wst, dh[:, oc * D:(oc + 1) * D]), inc="VE")

    nft = T // FT
    ve_gf = []
    gfi = [0]
    for tt in range(nft):
        c0 = 2 + tt * FT
        xnth2 = ap(o_xnt, CD * 257, BF16)
        xntl2 = ap(o_xnt + CD * 257 * 2, CD * 257, BF16)
        xnr = ap(o_xnr, CD * 257, F32)
        xrh = ap(o_xrh, CD * FT, BF16); xrl = ap(o_xrl, CD * FT, BF16)
        xkh = ap(o_xkh, CD * FT, BF16); xkl = ap(o_xkl, CD * FT, BF16)
        kq = ap(o_kq, FT, F32)
        kqh = ap(o_kqh, CD * FT, BF16); kql = ap(o_kql, CD * FT, BF16)
        rt = ap(o_rt, (FT // 128) * D, F32)
        ffo = ap(o_ffo, D, F32)
        S.qbar("sync", ("tensor", "vector"))
        par = newld()
        S.dma("sync", lambda e, c0=c0, d=xnth2: e.dma_start(
            out=d.rearrange("p (c t) -> p c t", t=257),
            in_=sp_xnh.rearrange("p (c t) -> p c t", t=TH)[:, :, c0:c0 + FT + 1]), par)
        S.dma("sync", lambda e, c0=c0, d=xntl2: e.dma_start(
            out=d.rearrange("p (c t) -> p c t", t=257),
            in_=sp_xnl.rearrange("p (c t) -> p c t", t=TH)[:, :, c0:c0 + FT + 1]), par)
        for tb in range(FT // 128):
            S.dma("gpsimd", lambda e, tt=tt, tb=tb: e.dma_start(
                out=ap(o_xr2 + tb * D * 4, D, F32),
                in_=dx[tt * FT + tb * 128:tt * FT + tb * 128 + 128, :]), "SP")
        S.wait("vector", par)
        S.wait("vector", "SP")
        S.op("vector", lambda e: e.tensor_add(xnr, xnth2, xntl2))
        for cc in range(CD):
            for si, (ho, lo_, mu, mu1) in enumerate(((xrh, xrl, o_mur, o_mur1),
                                                     (xkh, xkl, o_muk, o_muk1))):
                mxf = ap(o_mx2 + si * FT * 4, FT, F32)
                S.op("vector", lambda e, cc=cc, d=mxf, mu1=mu1: e.tensor_scalar(
                    d, xnr[:, cc * 257:cc * 257 + FT],
                    ap(mu1 + cc * 4, 1, F32), None, ALU.mult))
                S.op("vector", lambda e, cc=cc, d=mxf, mu=mu: e.scalar_tensor_tensor(
                    d, xnr[:, cc * 257 + 1:cc * 257 + FT + 1],
                    ap(mu + cc * 4, 1, F32), d, ALU.mult, ALU.add))
                S.op("vector", lambda e, cc=cc, d=mxf, h=ho: e.tensor_copy(
                    out=h[:, cc * FT:(cc + 1) * FT], in_=d))
                S.op("vector", lambda e, cc=cc, d=mxf, h=ho, l=lo_: e.tensor_sub(
                    l[:, cc * FT:(cc + 1) * FT], d, h[:, cc * FT:(cc + 1) * FT]),
                     inc="VE" if (cc == CD - 1 and si == 1) else None)
        S.wait("tensor", "VE")
        for oc in range(CD):
            g = gfi[0]; gfi[0] += 1
            pb = psF[g % 4]
            if g >= 4:
                S.wait("tensor", ve_gf[g - 4][0], ve_gf[g - 4][1])
            seq = [(a, b, cc) for a, b in ((t_wkh, xkh), (t_wkh, xkl), (t_wkl, xkh))
                   for cc in range(CD)]
            for i, (wop, aop, cc) in enumerate(seq):
                S.op("tensor", lambda e, p=pb, wop=wop, aop=aop, oc=oc, cc=cc,
                     fs=(i == 0), ls=(i == len(seq) - 1): e.matmul(
                         p[:, 0:FT], wop[:, (oc * CD + cc) * 128:(oc * CD + cc) * 128 + 128],
                         aop[:, cc * FT:(cc + 1) * FT], start=fs, stop=ls),
                     inc="TE" if i == len(seq) - 1 else None)
            S.wait("vector", "TE")
            S.op("vector", lambda e, p=pb: e.tensor_scalar_max(kq, p[:, 0:FT], 0.0),
                 inc="VE")
            ve_gf.append(("VE", S.now("VE")))
            S.wait("scalar", "VE")
            S.op("scalar", lambda e: e.activation(
                out=pool[:, o_kq:o_kq + FT * 4].bitcast(F32), in_=kq, func=AF.Square))
            S.op("scalar", lambda e, oc=oc: e.copy(
                out=kqh[:, oc * FT:(oc + 1) * FT], in_=kq), inc="SE")
            S.wait("vector", "SE")
            S.op("vector", lambda e, oc=oc: e.tensor_sub(
                kql[:, oc * FT:(oc + 1) * FT], kq, kqh[:, oc * FT:(oc + 1) * FT]),
                 inc="VE")
        for tb in range(FT // 128):
            for half in range(2):
                g = gfi[0]; gfi[0] += 1
                pb = psF[g % 4]
                if g >= 4:
                    S.wait("tensor", ve_gf[g - 4][0], ve_gf[g - 4][1])
                seq = [(a, b, cc) for a, b in ((xrh, t_wrh), (xrh, t_wrl), (xrl, t_wrh))
                       for cc in range(CD)]
                for i, (aop, wop, cc) in enumerate(seq):
                    S.op("tensor", lambda e, p=pb, aop=aop, wop=wop, cc=cc, tb=tb,
                         half=half, fs=(i == 0), ls=(i == len(seq) - 1): e.matmul(
                             p[:, 0:512],
                             aop[:, cc * FT + tb * 128:cc * FT + tb * 128 + 128],
                             wop[:, cc * D + half * 512:cc * D + half * 512 + 512],
                             start=fs, stop=ls),
                         inc="TE" if i == len(seq) - 1 else None)
                S.wait("scalar", "TE")
                S.op("scalar", lambda e, p=pb, tb=tb, half=half: e.activation(
                    out=rt[:, tb * D + half * 512:tb * D + half * 512 + 512],
                    in_=p[:, 0:512], func=AF.Sigmoid), inc="SE")
                ve_gf.append(("SE", S.now("SE")))
        S.wait("tensor", "SE")
        S.wait("tensor", "VE")
        for tb in range(FT // 128):
            for half in range(2):
                g = gfi[0]; gfi[0] += 1
                pb = psF[g % 4]
                if g >= 4:
                    S.wait("tensor", ve_gf[g - 4][0], ve_gf[g - 4][1])
                seq = [(a, b, cc) for a, b in ((kqh, t_wvh), (kqh, t_wvl), (kql, t_wvh))
                       for cc in range(CD)]
                for i, (aop, wop, cc) in enumerate(seq):
                    S.op("tensor", lambda e, p=pb, aop=aop, wop=wop, cc=cc, tb=tb,
                         half=half, fs=(i == 0), ls=(i == len(seq) - 1): e.matmul(
                             p[:, 0:512],
                             aop[:, cc * FT + tb * 128:cc * FT + tb * 128 + 128],
                             wop[:, cc * D + half * 512:cc * D + half * 512 + 512],
                             start=fs, stop=ls),
                         inc="TE" if i == len(seq) - 1 else None)
                S.wait("vector", "TE")
                S.op("vector", lambda e, p=pb, tb=tb, half=half: e.tensor_mul(
                    ffo[:, half * 512:half * 512 + 512], p[:, 0:512],
                    rt[:, tb * D + half * 512:tb * D + half * 512 + 512]))
                S.op("vector", lambda e, tb=tb, half=half: e.tensor_add(
                    ffo[:, half * 512:half * 512 + 512],
                    ffo[:, half * 512:half * 512 + 512],
                    ap(o_xr2 + tb * D * 4, D, F32)[:, half * 512:half * 512 + 512]),
                     inc="VE")
                ve_gf.append(("VE", S.now("VE")))
            S.wait("gpsimd", "VE")
            S.dma("gpsimd", lambda e, tt=tt, tb=tb: e.dma_start(
                out=sp_ffn[tt * FT + tb * 128:tt * FT + tb * 128 + 128, :], in_=ffo),
                "SP")
            S.wait("vector", "SP")
    S.barrier()
    A.release("right")
    A.release("left")

    A.mark("right")
    o_wz = A.alloc(CI * CD * 256, "right")
    o_wzl = A.alloc(CI * CD * 256, "right")
    o_wo = A.alloc(CI * D * 2, "right")
    o_wol = A.alloc(CI * D * 2, "right")
    o_wst3 = A.alloc(CD * 512, "right")
    o_wst4 = A.alloc(D * 4, "right")
    o_xnt3 = A.alloc(2 * 2 * CD * 256, "right")
    o_ymt3 = A.alloc(2 * CI * Q * 4, "right")
    o_zt = A.alloc(2 * Q * 4, "right")
    o_yt = A.alloc(2 * Q * 4, "right")
    o_mtg = A.alloc(2 * 2 * CI * 256, "right")
    o_ffr = A.alloc(2 * D * 4, "right")
    t_wzh = ap(o_wz, CI * CD * 128, BF16)
    t_wzl = ap(o_wzl, CI * CD * 128, BF16)
    t_woh = ap(o_wo, CI * D, BF16)
    t_wol = ap(o_wol, CI * D, BF16)
    wst3 = ap(o_wst3, CD * 128, F32)
    wst4 = ap(o_wst4, D, F32)

    wz_ve = {}
    wo_ve = {}
    for i in range(CI):
        S.wait("scalar", "VE")
        parz = newls()
        S.dma("scalar", lambda e, oc=i: e.dma_start(
            out=wst3.rearrange("p (c q) -> p c q", q=128),
            in_=dwz[oc].rearrange("c p q -> p c q")), parz)
        S.wait("vector", parz)
        S.op("vector", lambda e, oc=i: e.tensor_copy(
            out=t_wzh[:, oc * CD * 128:(oc + 1) * CD * 128], in_=wst3))
        S.op("vector", lambda e, oc=i: e.tensor_sub(
            t_wzl[:, oc * CD * 128:(oc + 1) * CD * 128], wst3,
            t_wzh[:, oc * CD * 128:(oc + 1) * CD * 128]), inc="VE")
        wz_ve[i] = S.now("VE")
        S.wait("gpsimd", "VE")
        S.dma("gpsimd", lambda e, cc=i: e.dma_start(out=wst4, in_=dwout[cc]), "SP")
        S.wait("vector", "SP")
        S.op("vector", lambda e, cc=i: e.tensor_copy(
            out=t_woh[:, cc * D:(cc + 1) * D], in_=wst4))
        S.op("vector", lambda e, cc=i: e.tensor_sub(
            t_wol[:, cc * D:(cc + 1) * D], wst4,
            t_woh[:, cc * D:(cc + 1) * D]), inc="VE")
        wo_ve[i] = S.now("VE")

    ntl3 = NCH
    te_wz = {}
    te_out = {}
    ve_mix = {}
    ve_out = {}
    sp_out = {}
    pf_par = {}
    se_z = {}
    ve_oc = {}
    gi3 = [0]

    def sl3(t):
        sl = t % 2
        return (ap(o_xnt3 + sl * 2 * CD * 256, CD * 128, BF16),
                ap(o_xnt3 + sl * 2 * CD * 256 + CD * 256, CD * 128, BF16),
                ap(o_ymt3 + sl * CI * Q * 4, CI * Q, F32),
                ap(o_ffr + sl * D * 4, D, F32),
                ap(o_mtg + sl * 2 * CI * 256, CI * 128, BF16),
                ap(o_mtg + sl * 2 * CI * 256 + CI * 256, CI * 128, BF16))

    def pf3(t):
        if t >= ntl3:
            return
        xnh3, xnl3, ymt, ffr, _, _ = sl3(t)
        if t >= 2:
            S.wait("sync", "TE", te_wz[t - 2])
            S.wait("sync", "VE", ve_mix[t - 2])
            S.wait("scalar", "SP", sp_out[t - 2])
        par = newld()
        S.dma("sync", lambda e, t=t, d=xnh3: e.dma_start(
            out=d.rearrange("p (c q) -> p c q", q=128),
            in_=sp_xnh.rearrange("p (c th) -> p c th", th=TH)[:, :, 3 + t * Q:3 + (t + 1) * Q]), par)
        S.dma("sync", lambda e, t=t, d=xnl3: e.dma_start(
            out=d.rearrange("p (c q) -> p c q", q=128),
            in_=sp_xnl.rearrange("p (c th) -> p c th", th=TH)[:, :, 3 + t * Q:3 + (t + 1) * Q]), par)
        S.dma("sync", lambda e, t=t, d=ymt: e.dma_start(out=d, in_=sp_ym[t]), par)
        pars = newls()
        S.dma("scalar", lambda e, t=t, d=ffr: e.dma_start(
            out=d, in_=sp_ffn[t * Q:(t + 1) * Q, :]), pars)
        pf_par[t] = (par, pars)

    pf3(0)
    for t in range(ntl3 + 1):
        if t >= 1:
            tp = t - 1
            _, _, _, ffrp, mghp, mglp = sl3(tp)
            S.wait("tensor", "VE", ve_mix[tp])
            if tp == 0:
                S.wait("tensor", "VE", wo_ve[CI - 1])
            for half in range(2):
                pb = psF[4 + half]
                if tp >= 1:
                    S.wait("tensor", "VE", ve_out[tp - 1])
                seq = [(a, b, cc) for a, b in ((mghp, t_woh), (mghp, t_wol),
                                               (mglp, t_woh)) for cc in range(CI)]
                for i, (aop, wop, cc) in enumerate(seq):
                    S.op("tensor", lambda e, p=pb, aop=aop, wop=wop, cc=cc, half=half,
                         fs=(i == 0), ls=(i == len(seq) - 1): e.matmul(
                             p[:, 0:512], aop[:, cc * 128:(cc + 1) * 128],
                             wop[:, cc * D + half * 512:cc * D + half * 512 + 512],
                             start=fs, stop=ls),
                         inc="TE" if i == len(seq) - 1 else None)
            te_out[tp] = S.now("TE")
            S.wait("vector", "TE", te_out[tp])
            S.wait("vector", pf_par[tp][0])
            S.wait("vector", pf_par[tp][1])
            for half in range(2):
                S.op("vector", lambda e, p=psF[4 + half], d=ffrp, half=half:
                     e.tensor_add(d[:, half * 512:half * 512 + 512], p[:, 0:512],
                                  d[:, half * 512:half * 512 + 512]),
                     inc="VE" if half == 1 else None)
            ve_out[tp] = S.now("VE")
            S.wait("gpsimd", "VE", ve_out[tp])
            S.dma("gpsimd", lambda e, tp=tp, d=ffrp: e.dma_start(
                out=dout[tp * Q:(tp + 1) * Q, :], in_=d), "SP")
            sp_out[tp] = S.now("SP")
        if t >= ntl3:
            continue
        xnh3, xnl3, ymt, ffr, mgh, mgl = sl3(t)
        pf3(t + 1)
        S.wait("tensor", pf_par[t][0])
        first_mix = True
        for oc in range(CI):
            g = gi3[0]
            gi3[0] += 1
            pb = psF[g % 4]
            if t == 0:
                S.wait("tensor", "VE", wz_ve[oc])
            if g >= 4:
                S.wait("tensor", "SE", se_z[g - 4])
            seq = [(a, b, cc) for a, b in ((t_wzh, xnh3), (t_wzh, xnl3), (t_wzl, xnh3))
                   for cc in range(CD)]
            for i, (wop, aop, cc) in enumerate(seq):
                S.op("tensor", lambda e, p=pb, wop=wop, aop=aop, oc=oc, cc=cc,
                     fs=(i == 0), ls=(i == len(seq) - 1): e.matmul(
                         p[:, 0:Q],
                         wop[:, (oc * CD + cc) * 128:(oc * CD + cc) * 128 + 128],
                         aop[:, cc * 128:(cc + 1) * 128], start=fs, stop=ls),
                     inc="TE" if i == len(seq) - 1 else None)
            if oc == CI - 1:
                te_wz[t] = S.now("TE")
            zt = ap(o_zt + (g % 2) * Q * 4, Q, F32)
            yt = ap(o_yt + (g % 2) * Q * 4, Q, F32)
            S.wait("scalar", "TE")
            if g >= 2:
                S.wait("scalar", "VE", ve_oc[g - 2])
            S.op("scalar", lambda e, p=pb, d=zt: e.activation(
                out=d, in_=p[:, 0:Q], func=AF.Silu), inc="SE")
            se_z[g] = S.now("SE")
            S.wait("vector", "SE", se_z[g])
            if first_mix:
                S.wait("vector", pf_par[t][0])
                if t >= 2:
                    S.wait("vector", "TE", te_out[t - 2])
                first_mix = False
            S.op("vector", lambda e, oc=oc, d=yt, ymt=ymt: e.tensor_scalar(
                d, ymt[:, oc * Q:(oc + 1) * Q], ap(o_stc + oc * 4, 1, F32),
                ap(o_stc + (CI + oc) * 4, 1, F32), ALU.mult, ALU.add))
            S.op("vector", lambda e, d=yt, z=zt: e.tensor_mul(d, d, z))
            S.op("vector", lambda e, oc=oc, d=yt, m=mgh: e.tensor_copy(
                out=m[:, oc * 128:(oc + 1) * 128], in_=d))
            S.op("vector", lambda e, oc=oc, d=yt, mh=mgh, ml=mgl: e.tensor_sub(
                ml[:, oc * 128:(oc + 1) * 128], d, mh[:, oc * 128:(oc + 1) * 128]),
                 inc="VE")
            ve_oc[g] = S.now("VE")
        ve_mix[t] = ve_oc[gi3[0] - 1]
    S.wait("gpsimd", "SP")
    A.release("right")

    nc._dbg = dict(o_bh=o_bh, o_bl=o_bl,
                   o_ch=o_ch, o_cl=o_cl, o_st=o_st, o_parts=o_parts,
                   o_expdA=o_expdA, o_stc=o_stc, o_tiny=o_tiny,
                   
                   o_atn=o_atn, o_dtt=o_dtt, o_dwt=o_dwt, o_dtc=o_dtc,
                   o_epc=o_epc, o_dsh=o_dsh, o_eppf=o_eppf)

    with nc.Block() as block:
        def mk(eng_name):
            def go(e):
                for it in S.items[eng_name]:
                    if it[0] == "op":
                        ins = it[1](e)
                        if it[2] is not None:
                            ins.then_inc(S.handles[it[2]], it[3])
                    else:
                        e.wait_ge(S.handles[it[1]], it[2])
            return go
        block.sync(mk("sync"))
        block.tensor(mk("tensor"))
        block.vector(mk("vector"))
        block.scalar(mk("scalar"))
        block.gpsimd(mk("gpsimd"))
    es.close()
    return nc



def _pack(inputs, T, n_shards, n_batches):
    f32 = np.float32
    wx = np.ascontiguousarray(inputs["L0_attn_Wx"], f32)
    wz = np.ascontiguousarray(inputs["L0_attn_Wz"], f32)
    wb = np.ascontiguousarray(inputs["L0_attn_Wb"], f32)
    wc = np.ascontiguousarray(inputs["L0_attn_Wc"], f32)
    wdt = np.ascontiguousarray(inputs["L0_attn_Wdt"], f32)
    wout = np.ascontiguousarray(inputs["L0_attn_Wout"], f32)
    wr = np.ascontiguousarray(inputs["L0_ffn_W_r"], f32)
    wk = np.ascontiguousarray(inputs["L0_ffn_W_k"], f32)
    wv = np.ascontiguousarray(inputs["L0_ffn_W_v"], f32)
    import ml_dtypes
    shared = {
        "wxp": np.ascontiguousarray(wx.reshape(CI, 128, CD, 128).transpose(0, 2, 3, 1)),
        "wzp": np.ascontiguousarray(wz.reshape(CI, 128, CD, 128).transpose(0, 2, 3, 1)),
        "wbp": np.ascontiguousarray(wb.reshape(CB, 128, CI, 128).transpose(0, 2, 3, 1)),
        "wcp": np.ascontiguousarray(wc.reshape(CB, 128, CI, 128).transpose(0, 2, 3, 1)),
        "wdtp": np.ascontiguousarray(wdt.reshape(8, CI, 128).transpose(1, 2, 0)),
        "woutr": np.ascontiguousarray(wout.reshape(D, CI, 128).transpose(1, 2, 0)),
        "wrr": np.ascontiguousarray(wr.reshape(D, CD, 128).transpose(1, 2, 0)),
        "wvr": np.ascontiguousarray(wv.reshape(D, CD, 128).transpose(1, 2, 0)),
        "wkp": np.ascontiguousarray(wk.reshape(CD, 128, CD, 128).transpose(0, 2, 3, 1)),
        "convw": np.ascontiguousarray(
            np.asarray(inputs["L0_attn_conv_w"], f32)[:, 0, :]
            .reshape(CI, 128, KCV).transpose(1, 0, 2).reshape(128, CI * KCV)),
        "convb": np.ascontiguousarray(
            np.asarray(inputs["L0_attn_conv_b"], f32).reshape(CI, 128).T),
        "gnw": np.ascontiguousarray(
            np.asarray(inputs["L0_attn_gn_w"], f32).reshape(CI, 128).T),
        "gnb": np.ascontiguousarray(
            np.asarray(inputs["L0_attn_gn_b"], f32).reshape(CI, 128).T),
        "dprep": np.ascontiguousarray(
            np.repeat(np.asarray(inputs["L0_attn_d_param"], f32), PHD)
            .reshape(CI, 128).T),
        "murc": np.ascontiguousarray(
            np.asarray(inputs["L0_ffn_mu_r"], f32).reshape(CD, 128).T),
        "mukc": np.ascontiguousarray(
            np.asarray(inputs["L0_ffn_mu_k"], f32).reshape(CD, 128).T),
        "bdt": np.asarray(inputs["L0_attn_b_dt"], f32).reshape(8, 1),
        "loga": np.asarray(inputs["L0_attn_log_a"], f32).reshape(8, 1),
        "identb": np.eye(128, dtype=ml_dtypes.bfloat16),
        "identf": np.eye(128, dtype=f32),
        "maskneg": np.where(np.arange(128)[None, :] >= np.arange(128)[:, None],
                            np.float32(0.0), np.float32(NEG)),
        "ones8x128": np.ones((8, 128), f32),
        "ones128": np.ones((128, 1), f32),
        "maskci": (np.arange(CI)[None, :] // 2 == np.arange(8)[:, None]).astype(f32),
    }
    selp = np.zeros((8, CB * 128), f32)
    for pc in range(CB):
        for m in range(128):
            selp[2 * pc + (m >= 64), pc * 128 + m] = 1.0
    shared["selp"] = selp
    selh = np.zeros((8, NH * 128), f32)
    for h in range(NH):
        selh[h, h * 128:(h + 1) * 128] = 1.0
    shared["selh"] = selh

    x = np.ascontiguousarray(np.asarray(inputs["L0_attn_input_tensor"], f32))
    per_core = []
    for core in range(n_shards * n_batches):
        b, s = core // n_shards, core % n_shards
        m = dict(shared)
        m["x"] = np.ascontiguousarray(x[b, s * T:(s + 1) * T])
        if s == 0:
            m["xh3"] = np.zeros((3, D), f32)
        else:
            m["xh3"] = np.ascontiguousarray(x[b, s * T - 3:s * T])
        sel = np.zeros((128, 4), f32)
        if s > 0:
            sel[:, s - 1] = 1.0
        m["selshard"] = sel
        per_core.append(m)
    return per_core


def kernel(**inputs):
    B, L, _ = inputs["L0_attn_input_tensor"].shape
    n_batches = B
    n_shards = 8 // n_batches
    T = L // n_shards
    in_maps = _pack(inputs, T, n_shards, n_batches)
    nc = build_program(T, n_shards, n_batches)
    res = run_bass_kernel_spmd(nc, in_maps, core_ids=list(range(8)))
    out = np.empty((B, L, D), np.float32)
    for core in range(8):
        b, s = core // n_shards, core % n_shards
        out[b, s * T:(s + 1) * T] = res.results[core]["out"]
    return out
